# version 14
# baseline (speedup 1.0000x reference)
import numpy as np
import concourse.bass as bass
import concourse.mybir as mybir
from contextlib import ExitStack

F32 = mybir.dt.float32
AF = mybir.ActivationFunctionType
ALU = mybir.AluOpType
AX = mybir.AxisListType

T, DM, DI, DS, R, DFF, DQ, KC = 2048, 1024, 2048, 16, 64, 4096, 2816, 4
LC = 256
NTT = LC // 128
FDM, FDI, FFF, FQ = DM // 128, DI // 128, DFF // 128, DQ // 128
NBK_DM = DM // 16

CONSTS = {
    "th0": 0.25, "th1": 0.75, "th2": 1.25, "th3": 1.75, "th4": 2.5,
    "th5": 3.5, "th6": 5.0, "two": 2.0, "four": 4.0, "half": 0.5,
    "eps12": 1e-12, "inv1024": 1.0 / 1024.0, "six": 6.0,
}
C_IDX = {k: i for i, k in enumerate(CONSTS)}
C_T6 = len(CONSTS)

ENGS = ("tensor", "vector", "scalar", "sync", "gpsimd")


SELF_SEM = {"vector": "sV", "scalar": "sS", "gpsimd": "sP"}


class Plan:
    def __init__(self):
        self.ev = {e: [] for e in ENGS}
        self.cnt = {s: 0 for s in (["sV", "sS", "sW", "sWv", "sW0", "sW1", "sG", "sO", "sTB", "sVB", "sSB", "sX", "sP", "sW2", "sQ0", "sQ1"] + [f"sT{i}" for i in range(8)])}
        self.barn = 0

    def op(self, eng, fn, inc=None, n=1):
        if eng in SELF_SEM and (inc is None or inc == SELF_SEM[eng]):
            s = SELF_SEM[eng]
            self.cnt[s] += 1
            self.ev[eng].append(("auto", fn))
            return self.cnt[s]
        if inc is None:
            self.ev[eng].append(("op", fn))
            return None
        self.cnt[inc] += n
        self.ev[eng].append(("inc", fn, inc, n))
        return self.cnt[inc]

    def wait(self, eng, sem, val):
        if val:
            self.ev[eng].append(("wait", sem, val))

    def waitm(self, eng, marker):
        if marker is not None:
            self.wait(eng, marker[0], marker[1])


def build_program(nchunks=8, dbg=()):
    nc = bass.Bass("TRN2", target_bir_lowering=False)
    dbg = set(dbg)

    x_h = nc.dram_tensor("x", [T, DM], F32, kind="ExternalInput").ap()
    w_in = nc.dram_tensor("in_proj_w", [DM, 2 * DI], F32, kind="ExternalInput").ap()
    conv_w = nc.dram_tensor("conv_w", [DI, KC], F32, kind="ExternalInput").ap()
    conv_b = nc.dram_tensor("conv_b", [DI], F32, kind="ExternalInput").ap()
    xproj_w = nc.dram_tensor("x_proj_w", [DI, R + 2 * DS], F32, kind="ExternalInput").ap()
    dtproj_w = nc.dram_tensor("dt_proj_w", [R, DI], F32, kind="ExternalInput").ap()
    dtproj_b = nc.dram_tensor("dt_proj_b", [DI], F32, kind="ExternalInput").ap()
    dtln_w = nc.dram_tensor("dt_ln_w", [R], F32, kind="ExternalInput").ap()
    bln_w = nc.dram_tensor("b_ln_w", [DS], F32, kind="ExternalInput").ap()
    cln_w = nc.dram_tensor("c_ln_w", [DS], F32, kind="ExternalInput").ap()
    alog = nc.dram_tensor("A_log", [DI, DS], F32, kind="ExternalInput").ap()
    dskip = nc.dram_tensor("D_skip", [DI], F32, kind="ExternalInput").ap()
    wout = nc.dram_tensor("out_proj_w", [DI, DM], F32, kind="ExternalInput").ap()
    inln_w = nc.dram_tensor("input_ln_w", [DM], F32, kind="ExternalInput").ap()
    pmln_w = nc.dram_tensor("pre_moe_ln_w", [DM], F32, kind="ExternalInput").ap()
    gate_w = nc.dram_tensor("gate_w", [DM, DFF], F32, kind="ExternalInput").ap()
    up_w = nc.dram_tensor("up_w", [DM, DFF], F32, kind="ExternalInput").ap()
    down_w = nc.dram_tensor("down_w", [DFF, DM], F32, kind="ExternalInput").ap()
    fg_w = nc.dram_tensor("ffn_w_gate", [DQ, DM], F32, kind="ExternalInput").ap()
    fu_w = nc.dram_tensor("ffn_w_up", [DQ, DM], F32, kind="ExternalInput").ap()
    fd_w = nc.dram_tensor("ffn_w_down", [DM, DQ], F32, kind="ExternalInput").ap()
    out_h = nc.dram_tensor("out", [T, DM], F32, kind="ExternalOutput").ap()
    wqtg = nc.dram_tensor("wqtg", [DM, DQ], F32, kind="Internal").ap()
    wqtu = nc.dram_tensor("wqtu", [DM, DQ], F32, kind="Internal").ap()
    wqtd = nc.dram_tensor("wqtd", [DQ, DM], F32, kind="Internal").ap()

    dbg_h = {}
    def dbg_tensor(name, shape):
        if name in dbg:
            dbg_h[name] = nc.dram_tensor("d_" + name, shape, F32, kind="ExternalOutput").ap()
    dbg_tensor("xn", [T, DM]); dbg_tensor("xq", [T, DM])
    dbg_tensor("xssm", [DI, T]); dbg_tensor("sz", [DI, T])
    dbg_tensor("dt", [DI, T]); dbg_tensor("bm", [DS, T]); dbg_tensor("cm", [DS, T])
    dbg_tensor("y", [DI, T]); dbg_tensor("hid", [DM, T])
    dbg_tensor("s", [DM, T]); dbg_tensor("g", [DM, T]); dbg_tensor("hq", [T, DQ])

    ident_h = nc.inline_tensor(np.eye(128, dtype=np.float32), name="identc")
    rs = np.zeros((16, 16 * 128), dtype=np.float32)
    for n_ in range(16):
        rs[n_, n_ * 128:(n_ + 1) * 128] = 1.0
    rowsel_h = nc.inline_tensor(rs, name="rowselc")

    P = Plan()

    with ExitStack() as ctx:
        E = ctx.enter_context

        cpool = E(nc.sbuf_tensor("cpool", [128, 32], F32))
        colv = E(nc.sbuf_tensor("colv", [128, 64], F32))
        convw = E(nc.sbuf_tensor("convw", [128, 64], F32))
        aneg = E(nc.sbuf_tensor("aneg", [128, 256], F32))
        sident = E(nc.sbuf_tensor("sident", [128, 128], F32))
        rowsel = E(nc.sbuf_tensor("rowsel", [16, 16 * 128], F32))
        onesrow = E(nc.sbuf_tensor("onesrow", [1, 128], F32))
        onescol = E(nc.sbuf_tensor("onescol", [128, 1], F32))
        smallv = E(nc.sbuf_tensor("smallv", [64, 4], F32))
        vstage = E(nc.sbuf_tensor("vstage", [16, 128], F32))
        x_t = E(nc.sbuf_tensor("x_t", [128, NTT * DM], F32))
        xnT = E(nc.sbuf_tensor("xnT", [128, FDM * LC], F32))
        xnW = E(nc.sbuf_tensor("xnW", [128, FDM * LC], F32))
        xqT = E(nc.sbuf_tensor("xqT", [128, FDM * LC], F32))
        wbuf = E(nc.sbuf_tensor("wbuf", [128, 6144], F32))
        SP = E(nc.sbuf_tensor("SP", [128, 5120], F32))
        rows = E(nc.sbuf_tensor("rows", [1, 1024], F32))
        projS = E(nc.sbuf_tensor("projS", [64, LC], F32))
        projB = E(nc.sbuf_tensor("projB", [16, LC], F32))
        projC = E(nc.sbuf_tensor("projC", [16, LC], F32))
        dtrms = E(nc.sbuf_tensor("dtrms", [64, LC], F32))
        BmT = E(nc.sbuf_tensor("BmT", [16, LC], F32))
        CmT = E(nc.sbuf_tensor("CmT", [16, LC], F32))
        r2bcS = E(nc.sbuf_tensor("r2bcS", [128, LC], F32))
        r3bcS = E(nc.sbuf_tensor("r3bcS", [128, LC], F32))
        hiddenT = E(nc.sbuf_tensor("hiddenT", [128, FDM * LC], F32))
        sT_ = E(nc.sbuf_tensor("sT_", [128, FDM * LC], F32))
        szb = E(nc.sbuf_tensor("szb", [128, FDI * LC], F32))
        dtT = E(nc.sbuf_tensor("dtT", [128, FDI * LC], F32))
        hstate = E(nc.sbuf_tensor("hstate", [128, 256], F32))
        tails = E(nc.sbuf_tensor("tails", [128, 48], F32))
        xpw = E(nc.sbuf_tensor("xpw", [128, 16 * 96], F32))
        AR = E(nc.sbuf_tensor("AR", [128, 12352], F32))

        pb = [E(nc.psum_tensor(f"pb{i}", [128, 512], F32)) for i in range(8)]

        sems = {}
        for s in (["sV", "sS", "sW", "sWv", "sW0", "sW1", "sW2", "sG", "sO", "sTB", "sVB", "sSB", "sX", "sP", "sQ0", "sQ1"] + [f"sT{i}" for i in range(8)]):
            sems[s] = E(nc.semaphore(s))
        block = E(nc.Block())

        bank_free = [None] * 8
        whalf_free = [None, None, None]
        whalf_ctr = [0]

        def c(name):
            return cpool[:, C_IDX[name]:C_IDX[name] + 1]

        def arena(o, n):
            return AR[:, o:o + n]

        def sp(o, n):
            return SP[:, o:o + n]

        def bar(engs=("tensor", "vector", "scalar")):
            tvals = {f"sT{i}": P.cnt[f"sT{i}"] for i in range(8)}
            tb = P.op("tensor", lambda eng: eng.nop(), inc="sTB")
            vv = P.op("vector", lambda eng: eng.nop())
            ss = P.op("scalar", lambda eng: eng.nop())
            pp = P.op("gpsimd", lambda eng: eng.nop())
            P.wait("tensor", "sV", vv)
            P.wait("tensor", "sS", ss)
            P.wait("tensor", "sP", pp)
            for e in ("vector", "scalar"):
                for s_, val in tvals.items():
                    P.wait(e, s_, val)
                P.wait(e, "sTB", tb)
            P.wait("vector", "sS", ss)
            P.wait("vector", "sP", pp)
            P.wait("scalar", "sV", vv)
            P.wait("scalar", "sP", pp)
            P.wait("gpsimd", "sV", vv)
            P.wait("gpsimd", "sS", ss)
            for s_, val in tvals.items():
                P.wait("gpsimd", s_, val)
            P.wait("gpsimd", "sTB", tb)

        def te_bank(b):
            P.waitm("tensor", bank_free[b])
            bank_free[b] = None

        def dma(eng, out, in_, sem="sW"):
            return P.op(eng, lambda e, o=out, i=in_: e.dma_start(out=o, in_=i), inc=sem, n=16)

        def dbg_guard():
            v = P.cnt["sO"]
            for e in ("vector", "scalar", "tensor"):
                P.wait(e, "sO", v)
            dbg_post()

        def dbg_pre():
            P.wait("gpsimd", "sV", P.cnt["sV"])
            P.wait("gpsimd", "sS", P.cnt["sS"])
            P.wait("gpsimd", "sP", P.cnt["sP"])

        def dbg_post():
            vv = P.op("vector", lambda v: v.engine_nop())
            P.wait("gpsimd", "sV", vv)

        def emit_quant(xs, w_, nblk, wide, small):
            av, acc, ac2, c7s, sg = wide
            bm, scl, rcp = small
            x3 = xs.rearrange("p (b k) -> p b k", k=16)
            P.op("vector", lambda v: v.tensor_reduce(
                out=bm, in_=x3, axis=AX.X, op=ALU.max, apply_absolute_value=True))
            P.op("vector", lambda v: v.engine_nop())
            P.op("vector", lambda v: v.tensor_scalar(
                out=scl, in0=bm, scalar1=cpool[:, C_T6:C_T6 + 1], scalar2=c("eps12"),
                op0=ALU.mult, op1=ALU.max))
            P.op("vector", lambda v: v.engine_nop())
            P.op("vector", lambda v: v.reciprocal(out=rcp, in_=scl))
            P.op("vector", lambda v: v.engine_nop())
            rbc = rcp.unsqueeze(2).broadcast_to([128, nblk, 16])
            sbc = scl.unsqueeze(2).broadcast_to([128, nblk, 16])
            vq = P.op("vector", lambda v: v.tensor_tensor(
                out=av.rearrange("p (b k) -> p b k", k=16), in0=x3, in1=rbc,
                op=ALU.mult), inc="sV")
            P.wait("scalar", "sV", vq)
            P.op("scalar", lambda s: s.activation(out=sg, in_=av, func=AF.Sign))
            s2 = P.op("scalar", lambda s: s.activation(out=av, in_=av, func=AF.Abs), inc="sS")
            P.wait("vector", "sS", s2)
            P.op("vector", lambda v: v.tensor_single_scalar(
                out=acc, in_=av, scalar=c("th0"), op=ALU.is_gt))
            for th in ("th1", "th2", "th3"):
                P.op("vector", lambda v, t=th: v.scalar_tensor_tensor(
                    out=acc, in0=av, scalar=c(t), in1=acc, op0=ALU.is_gt, op1=ALU.add))
            P.op("vector", lambda v: v.tensor_single_scalar(
                out=ac2, in_=av, scalar=c("th4"), op=ALU.is_gt))
            P.op("vector", lambda v: v.scalar_tensor_tensor(
                out=ac2, in0=av, scalar=c("th5"), in1=ac2, op0=ALU.is_gt, op1=ALU.add))
            P.op("vector", lambda v: v.tensor_single_scalar(
                out=c7s, in_=av, scalar=c("th6"), op=ALU.is_gt))
            P.op("vector", lambda v: v.scalar_tensor_tensor(
                out=acc, in0=ac2, scalar=c("two"), in1=acc, op0=ALU.mult, op1=ALU.add))
            P.op("vector", lambda v: v.scalar_tensor_tensor(
                out=acc, in0=c7s, scalar=c("four"), in1=acc, op0=ALU.mult, op1=ALU.add))
            P.op("vector", lambda v: v.tensor_tensor(out=acc, in0=acc, in1=sg, op=ALU.mult))
            return P.op("vector", lambda v: v.scalar_tensor_tensor(
                out=xs.rearrange("p (b k) -> p b k", k=16),
                in0=acc.rearrange("p (b k) -> p b k", k=16),
                scalar=c("half"), in1=sbc, op0=ALU.mult, op1=ALU.mult), inc="sV")

        def emit_quant_gp(xs, w_, nblk, wide, small, pre=()):
            av, acc, ac2, c7s, sg = wide
            bm, scl, rcp = small
            x3 = xs.rearrange("p (b k) -> p b k", k=16)
            for (s_, v_) in pre:
                P.wait("gpsimd", s_, v_)
                if s_ != "sV":
                    P.wait("vector", s_, v_)
            P.op("vector", lambda v: v.tensor_reduce(
                out=bm, in_=x3, axis=AX.X, op=ALU.max, apply_absolute_value=True))
            P.op("vector", lambda v: v.engine_nop())
            P.op("vector", lambda v: v.tensor_scalar(
                out=scl, in0=bm, scalar1=cpool[:, C_T6:C_T6 + 1], scalar2=c("eps12"),
                op0=ALU.mult, op1=ALU.max))
            P.op("vector", lambda v: v.engine_nop())
            v1 = P.op("vector", lambda v: v.reciprocal(out=rcp, in_=scl))
            rbc = rcp.unsqueeze(2).broadcast_to([128, nblk, 16])
            sbc = scl.unsqueeze(2).broadcast_to([128, nblk, 16])
            P.wait("gpsimd", "sV", v1)
            gq = P.op("gpsimd", lambda g: g.tensor_tensor(
                out=av.rearrange("p (b k) -> p b k", k=16), in0=x3, in1=rbc,
                op=ALU.mult))
            P.wait("scalar", "sP", gq)
            P.op("scalar", lambda s: s.activation(out=sg, in_=av, func=AF.Sign))
            s2 = P.op("scalar", lambda s: s.activation(out=av, in_=av, func=AF.Abs), inc="sS")
            P.wait("gpsimd", "sS", s2)
            P.op("gpsimd", lambda g: g.tensor_single_scalar(
                out=acc, in_=av, scalar=c("th0"), op=ALU.is_gt))
            for th in ("th1", "th2", "th3"):
                P.op("gpsimd", lambda g, t=th: g.scalar_tensor_tensor(
                    out=acc, in0=av, scalar=c(t), in1=acc, op0=ALU.is_gt, op1=ALU.add))
            P.op("gpsimd", lambda g: g.tensor_single_scalar(
                out=ac2, in_=av, scalar=c("th4"), op=ALU.is_gt))
            P.op("gpsimd", lambda g: g.scalar_tensor_tensor(
                out=ac2, in0=av, scalar=c("th5"), in1=ac2, op0=ALU.is_gt, op1=ALU.add))
            P.op("gpsimd", lambda g: g.tensor_single_scalar(
                out=c7s, in_=av, scalar=c("th6"), op=ALU.is_gt))
            P.op("gpsimd", lambda g: g.scalar_tensor_tensor(
                out=acc, in0=ac2, scalar=c("two"), in1=acc, op0=ALU.mult, op1=ALU.add))
            P.op("gpsimd", lambda g: g.scalar_tensor_tensor(
                out=acc, in0=c7s, scalar=c("four"), in1=acc, op0=ALU.mult, op1=ALU.add))
            P.op("gpsimd", lambda g: g.tensor_tensor(out=acc, in0=acc, in1=sg, op=ALU.mult))
            gend = P.op("gpsimd", lambda g: g.scalar_tensor_tensor(
                out=xs.rearrange("p (b k) -> p b k", k=16),
                in0=acc.rearrange("p (b k) -> p b k", k=16),
                scalar=c("half"), in1=sbc, op0=ALU.mult, op1=ALU.mult))
            return ("sP", gend)

        def init_consts(vector):
            for k, v in CONSTS.items():
                vector.memset(cpool[:, C_IDX[k]:C_IDX[k] + 1], float(v))
            vector.memset(onescol[:], 1.0)
            vector.memset(onesrow[:], 1.0)
            vector.memset(hstate[:], 0.0)
            vector.memset(tails[:], 0.0)
        P.op("vector", init_consts)
        P.op("vector", lambda v: v.engine_nop())
        P.op("vector", lambda v: v.reciprocal(out=cpool[:, C_T6:C_T6 + 1], in_=c("six")))
        P.op("vector", lambda v: v.engine_nop(), inc="sV")

        dma("sync", sident[:], ident_h.ap()[:])
        dma("sync", rowsel[:], rowsel_h.ap()[:])
        for dt_i in range(FDI):
            dma("sync", convw[:, dt_i * 4:(dt_i + 1) * 4],
                conv_w[dt_i * 128:(dt_i + 1) * 128, :])
        for dt_i in range(FDI):
            dma("sync", arena(dt_i * 16, 16), alog[dt_i * 128:(dt_i + 1) * 128, :])
        w_res = dma("sync", xpw[:].rearrange("p (k j) -> p k j", j=96),
                    xproj_w[:].rearrange("(k p) j -> p k j", p=128))
        P.wait("tensor", "sW", w_res)
        P.wait("scalar", "sW", w_res)
        for dt_i in range(FDI):
            P.op("scalar", lambda s, i=dt_i: s.activation(
                out=aneg[:, i * 16:(i + 1) * 16], in_=arena(i * 16, 16), func=AF.Exp))
        for dt_i in range(FDI):
            P.op("scalar", lambda s, i=dt_i: s.mul(
                out=aneg[:, i * 16:(i + 1) * 16], in_=aneg[:, i * 16:(i + 1) * 16],
                mul=np.float32(-1.0)))

        for src, n, off in ((inln_w, DM, 0), (pmln_w, DM, 8), (conv_b, DI, 16),
                            (dtproj_b, DI, 32), (dskip, DI, 48)):
            k = n // 128
            w = dma("sync", vstage[0:k, :], src.rearrange("(k p) -> k p", p=128),
                    sem="sWv")
            P.wait("tensor", "sWv", w)
            t = P.op("tensor", lambda e, kk=k: e.transpose(
                pb[6][0:128, 0:kk], vstage[0:kk, 0:128], sident[0:kk, 0:kk]), inc="sT6")
            P.wait("scalar", "sT6", t)
            sc = P.op("scalar", lambda s, kk=k, o=off: s.copy(
                out=colv[:, o:o + kk], in_=pb[6][0:128, 0:kk]), inc="sS")
            P.wait("tensor", "sS", sc)
            P.wait("sync", "sS", sc)
        for src, n, col in ((dtln_w, R, 0), (bln_w, DS, 1), (cln_w, DS, 2)):
            w = dma("sync", vstage[0:1, 0:n], src.unsqueeze(0), sem="sWv")
            P.wait("tensor", "sWv", w)
            t = P.op("tensor", lambda e, nn=n: e.transpose(
                pb[6][0:nn, 0:1], vstage[0:1, 0:nn], sident[0:1, 0:1]), inc="sT6")
            P.wait("scalar", "sT6", t)
            sc = P.op("scalar", lambda s, nn=n, cc=col: s.copy(
                out=smallv[0:nn, cc:cc + 1], in_=pb[6][0:nn, 0:1]), inc="sS")
            P.wait("tensor", "sS", sc)
            P.wait("sync", "sS", sc)

        QW_X, QW_AV, QW_ACC, QW_AC2, QW_C7, QW_SG = 0, 1024, 2048, 3072, 4096, 4096
        W_STAGE = 6144
        CP_STAGE = 768
        P.wait("vector", "sS", P.cnt["sS"])
        jobs = []
        for (w_src, w_dst, nrows, kdim) in (
            (fg_w, wqtg, DQ, DM), (fu_w, wqtu, DQ, DM), (fd_w, wqtd, DM, DQ),
        ):
            for rt in range(nrows // 128):
                jobs.append((w_src, w_dst, rt, kdim))
        stage_mark = [None, None]
        st_mark = None
        for ji, (w_src, w_dst, rt, kdim) in enumerate(jobs):
            stg = ji % 2
            soff = W_STAGE + stg * 2816
            P.waitm("sync", stage_mark[stg])
            wld = dma("sync", arena(soff, kdim), w_src[rt * 128:(rt + 1) * 128, :],
                      sem=f"sQ{stg}")
            P.wait("vector", f"sQ{stg}", wld)
            slw = 1024 if kdim % 1024 == 0 else 704
            vq = None
            for s0 in range(0, kdim, slw):
                nb = slw // 16
                vq = emit_quant(
                    arena(soff + s0, slw), slw, nb,
                    (sp(QW_AV, slw), sp(QW_ACC, slw), sp(QW_AC2, slw),
                     sp(QW_X, slw), sp(QW_SG, slw)),
                    (arena(0, nb), arena(256, nb), arena(512, nb)))
            P.wait("tensor", "sV", vq)
            P.waitm("scalar", st_mark)
            last_t = None
            for fi in range(kdim // 128):
                bk = 6 + (fi % 2)
                te_bank(bk)
                t = P.op("tensor", lambda e, f=fi, bb=bk, so=soff: e.transpose(
                    pb[bb][0:128, 0:128], arena(so + f * 128, 128),
                    sident[:]), inc=f"sT{bk}")
                P.wait("scalar", f"sT{bk}", t)
                sc = P.op("scalar", lambda s, f=fi, bb=bk: s.copy(
                    out=arena(CP_STAGE + f * 128, 128), in_=pb[bb][0:128, 0:128]),
                    inc="sS")
                bank_free[bk] = ("sS", sc)
                last_t = t
            stage_mark[stg] = (f"sT{6 + (kdim // 128 - 1) % 2}", last_t)
            P.wait("gpsimd", "sS", P.cnt["sS"])
            gd = P.op("gpsimd", lambda g, r=rt, d=w_dst, kd=kdim: g.dma_start(
                out=d[0:kd, r * 128:(r + 1) * 128]
                .rearrange("(f p) j -> p f j", p=128),
                in_=arena(CP_STAGE, kd).rearrange("p (f j) -> p f j", j=128)),
                inc="sG", n=16)
            st_mark = ("sG", gd)
        P.wait("vector", "sG", P.cnt["sG"])
        P.wait("scalar", "sG", P.cnt["sG"])
        bar()
        P.wait("vector", "sG", P.cnt["sG"])

        def stream_mm(weight_h, mcol, k_tiles, rhs_fn, bank, ncols=LC):
            nseg = (k_tiles + 15) // 16
            tm = None
            for seg in range(nseg):
                k0 = seg * ((k_tiles + nseg - 1) // nseg)
                k1 = min(k_tiles, k0 + (k_tiles + nseg - 1) // nseg)
                half = whalf_ctr[0] % 3
                whalf_ctr[0] += 1
                P.waitm("sync", whalf_free[half]); whalf_free[half] = None
                wgx = dma("sync", wbuf[:, half * 2048:half * 2048 + (k1 - k0) * 128]
                          .rearrange("p (k j) -> p k j", j=128),
                          weight_h[k0 * 128:k1 * 128, mcol * 128:(mcol + 1) * 128]
                          .rearrange("(k p) j -> p k j", p=128),
                          sem=f"sW{half}")
                P.wait("tensor", f"sW{half}", wgx)
                for k in range(k0, k1):
                    tm = P.op("tensor", lambda e, kk=k, hh=half, bb=bank, kz=k0,
                              nc_=ncols: e.matmul(
                        pb[bb][0:128, 0:nc_],
                        wbuf[:, hh * 2048 + (kk - kz) * 128:
                             hh * 2048 + (kk - kz + 1) * 128],
                        rhs_fn(kk), start=(kk == 0), stop=(kk == k_tiles - 1)),
                        inc=(f"sT{bank}" if k == k1 - 1 else None))
                whalf_free[half] = (f"sT{bank}", tm)
            return (f"sT{bank}", tm)

        def load_pair(weight_h, m0, k_tiles=8):
            slot = whalf_ctr[0] % 3
            whalf_ctr[0] += 1
            P.waitm("sync", whalf_free[slot]); whalf_free[slot] = None
            off = slot * 2048
            wg = dma("sync", wbuf[:, off:off + k_tiles * 256]
                     .rearrange("p (k j) -> p k j", j=256),
                     weight_h[0:k_tiles * 128, m0 * 128:(m0 + 2) * 128]
                     .rearrange("(k p) j -> p k j", p=128),
                     sem=f"sW{slot}")
            return (slot, off, f"sW{slot}", wg)

        def pair_mm(ld, sub, bank, rhs_fn, k_tiles=8, ncols=LC):
            slot, off, sem_, wg = ld
            P.wait("tensor", sem_, wg)
            tm = None
            for k in range(k_tiles):
                tm = P.op("tensor", lambda e, kk=k, bb=bank, o=off, ss=sub,
                          nc_=ncols: e.matmul(
                    pb[bb][0:128, 0:nc_],
                    wbuf[:, o + kk * 256 + ss * 128: o + kk * 256 + ss * 128 + 128],
                    rhs_fn(kk), start=(kk == 0), stop=(kk == k_tiles - 1)),
                    inc=(f"sT{bank}" if k == k_tiles - 1 else None))
            if sub == 1:
                whalf_free[slot] = (f"sT{bank}", tm)
            return (f"sT{bank}", tm)

        A_XC, A_XN, A_XQ = 0, 2048, 4096
        A_XPAD, A_XSSM, A_Y = 0, 4160, 8256
        A_PROD, A_HNT = 0, 8192
        A_PRODQ, A_HQT, A_HQTT, A_PR = 0, 5632, 0, 8192
        A_GT = 5632

        def xpad(d):
            return arena(A_XPAD + d * 259, 259)

        def xssm(d):
            return arena(A_XSSM + d * 256, 256)

        def dtx(d):
            return arena(A_XPAD + d * 256, 256)

        def ya(d):
            return arena(A_Y + d * 256, 256)

        def szt(d):
            return szb[:, d * LC:(d + 1) * LC]

        def dtTt(d):
            return dtT[:, d * LC:(d + 1) * LC]

        RW_A, RW_B = 0, 256
        RW_C, RW_D = 512, 768
        RW_E, RW_F = 512, 768

        last_combine = None

        for ch in range(nchunks):
            tok0 = ch * LC

            P.waitm("gpsimd", last_combine)
            P.wait("scalar", "sO", P.cnt["sO"])
            wx = None
            for tt in range(NTT):
                wx = dma("gpsimd", x_t[:, tt * DM:(tt + 1) * DM],
                         x_h[tok0 + tt * 128: tok0 + (tt + 1) * 128, :], sem="sX")
            P.wait("vector", "sX", wx)
            stat = r2bcS
            for tt in range(NTT):
                P.op("vector", lambda v, t_=tt: v.reduce_sum(
                    out=stat[:, t_:t_ + 1], in_=x_t[:, t_ * DM:(t_ + 1) * DM], axis=AX.X))
            P.op("vector", lambda v: v.engine_nop())
            for tt in range(NTT):
                P.op("vector", lambda v, t_=tt: v.tensor_single_scalar(
                    out=stat[:, 2 + t_:3 + t_], in_=stat[:, t_:t_ + 1],
                    scalar=c("inv1024"), op=ALU.mult))
            P.op("vector", lambda v: v.engine_nop())
            vml = None
            for tt in range(NTT):
                vml = P.op("vector", lambda v, t_=tt: v.tensor_scalar(
                    out=arena(A_XC + t_ * DM, DM), in0=x_t[:, t_ * DM:(t_ + 1) * DM],
                    scalar1=stat[:, 2 + t_:3 + t_], scalar2=None, op0=ALU.subtract), inc="sV")
            P.wait("scalar", "sV", vml)
            ssl = None
            for tt in range(NTT):
                ssl = P.op("scalar", lambda s, t_=tt: s.activation(
                    out=arena(A_XN + t_ * DM, DM), in_=arena(A_XC + t_ * DM, DM),
                    func=AF.Square, accum_out=stat[:, 4 + t_:5 + t_]), inc="sS")
            ssl = P.op("scalar", lambda s: s.activation(
                out=stat[:, 6:6 + NTT], in_=stat[:, 4:4 + NTT], func=AF.Sqrt,
                scale=np.float32(1.0 / DM), bias=np.float32(1e-5)), inc="sS")
            P.wait("vector", "sS", ssl)
            vr = P.op("vector", lambda v: v.reciprocal(
                out=stat[:, 8:8 + NTT], in_=stat[:, 6:6 + NTT]), inc="sV")
            P.wait("scalar", "sV", vr)
            scn = None
            for tt in range(NTT):
                scn = P.op("scalar", lambda s, t_=tt: s.activation(
                    out=arena(A_XN + t_ * DM, DM), in_=arena(A_XC + t_ * DM, DM),
                    func=AF.Copy, scale=stat[:, 8 + t_:9 + t_]), inc="sS")
            if "xn" in dbg:
                dbg_pre()
                P.wait("gpsimd", "sS", scn)
                for tt in range(NTT):
                    dma("gpsimd", dbg_h["xn"][tok0 + tt * 128: tok0 + (tt + 1) * 128, :],
                        arena(A_XN + tt * DM, DM), sem="sO")
            P.wait("tensor", "sS", scn)
            for (src_off, kind) in ((A_XN, "xn"),):
                for tt in range(NTT):
                    for fi in range(FDM):
                        b = 6 + ((tt * FDM + fi) % 2)
                        te_bank(b)
                        t = P.op("tensor", lambda e, o=src_off, t_=tt, f=fi, bb=b: e.transpose(
                            pb[bb][0:128, 0:128], arena(o + t_ * DM + f * 128, 128),
                            sident[:]), inc=f"sT{b}")
                        P.wait("scalar", f"sT{b}", t)
                        if kind == "xn":
                            P.op("scalar", lambda s, f=fi, t_=tt, bb=b: s.copy(
                                out=xnT[:, f * LC + t_ * 128: f * LC + (t_ + 1) * 128],
                                in_=pb[bb][0:128, 0:128]))
                            sc = P.op("scalar", lambda s, f=fi, t_=tt, bb=b: s.activation(
                                out=xnW[:, f * LC + t_ * 128: f * LC + (t_ + 1) * 128],
                                in_=pb[bb][0:128, 0:128], func=AF.Copy,
                                scale=colv[:, f:f + 1]), inc="sS")
                        else:
                            sc = P.op("scalar", lambda s, f=fi, t_=tt, bb=b: s.copy(
                                out=xqT[:, f * LC + t_ * 128: f * LC + (t_ + 1) * 128],
                                in_=pb[bb][0:128, 0:128]), inc="sS")
                        bank_free[b] = ("sS", sc)
            P.wait("vector", "sS", scn)
            vdq = None
            for tt in range(NTT):
                P.op("vector", lambda v, t_=tt: v.tensor_copy(
                    out=arena(A_XQ + t_ * DM, DM), in_=arena(A_XN + t_ * DM, DM)))
                P.op("vector", lambda v: v.engine_nop())
                vdq = emit_quant(
                    arena(A_XQ + tt * DM, DM), DM, NBK_DM,
                    (sp(QW_AV, DM), sp(QW_ACC, DM), sp(QW_AC2, DM),
                     sp(QW_X, DM), sp(QW_SG, DM)),
                    (arena(12352 - 256, NBK_DM), arena(12352 - 512, NBK_DM),
                     arena(12352 - 768, NBK_DM)))
            if "xq" in dbg:
                dbg_pre()
                P.wait("gpsimd", "sV", vdq)
                for tt in range(NTT):
                    dma("gpsimd", dbg_h["xq"][tok0 + tt * 128: tok0 + (tt + 1) * 128, :],
                        arena(A_XQ + tt * DM, DM), sem="sO")
            P.wait("tensor", "sV", vdq)
            for (src_off, kind) in ((A_XQ, "xq"),):
                for tt in range(NTT):
                    for fi in range(FDM):
                        b = 6 + ((tt * FDM + fi) % 2)
                        te_bank(b)
                        t = P.op("tensor", lambda e, o=src_off, t_=tt, f=fi, bb=b: e.transpose(
                            pb[bb][0:128, 0:128], arena(o + t_ * DM + f * 128, 128),
                            sident[:]), inc=f"sT{b}")
                        P.wait("scalar", f"sT{b}", t)
                        if kind == "xn":
                            P.op("scalar", lambda s, f=fi, t_=tt, bb=b: s.copy(
                                out=xnT[:, f * LC + t_ * 128: f * LC + (t_ + 1) * 128],
                                in_=pb[bb][0:128, 0:128]))
                            sc = P.op("scalar", lambda s, f=fi, t_=tt, bb=b: s.activation(
                                out=xnW[:, f * LC + t_ * 128: f * LC + (t_ + 1) * 128],
                                in_=pb[bb][0:128, 0:128], func=AF.Copy,
                                scale=colv[:, f:f + 1]), inc="sS")
                        else:
                            sc = P.op("scalar", lambda s, f=fi, t_=tt, bb=b: s.copy(
                                out=xqT[:, f * LC + t_ * 128: f * LC + (t_ + 1) * 128],
                                in_=pb[bb][0:128, 0:128]), inc="sS")
                        bank_free[b] = ("sS", sc)
            bar()
            dbg_guard()

            te_bank(5)
            slot_mark = {}
            for fi in range(FDM):
                sl = fi % 2
                P.waitm("scalar", slot_mark.get(sl))
                ssq = P.op("scalar", lambda s, f=fi, o=sl: s.activation(
                    out=sp(o * 256, LC), in_=xnT[:, f * LC:(f + 1) * LC],
                    func=AF.Square), inc="sS")
                P.wait("tensor", "sS", ssq)
                tm = P.op("tensor", lambda e, o=sl, f=fi: e.matmul(
                    pb[5][0:1, 0:LC], onescol[:, 0:1], sp(o * 256, LC),
                    start=(f == 0), stop=(f == FDM - 1)), inc="sT5")
                slot_mark[sl] = ("sT5", tm)
            P.wait("scalar", "sT5", P.cnt["sT5"])
            srq = P.op("scalar", lambda s: s.activation(
                out=rows[0:1, RW_A:RW_A + LC], in_=pb[5][0:1, 0:LC], func=AF.Sqrt,
                scale=np.float32(1.0 / DM), bias=np.float32(1e-6)), inc="sS")
            P.wait("vector", "sS", srq)
            vr2 = P.op("vector", lambda v: v.reciprocal(
                out=rows[0:1, RW_B:RW_B + LC], in_=rows[0:1, RW_A:RW_A + LC]), inc="sV")
            P.wait("tensor", "sV", vr2)
            tbc = P.op("tensor", lambda e: e.matmul(
                pb[5][0:128, 0:LC], onesrow[0:1, 0:128], rows[0:1, RW_B:RW_B + LC],
                start=True, stop=True), inc="sT5")
            P.wait("scalar", "sT5", tbc)
            sbc2 = P.op("scalar", lambda s: s.copy(out=r2bcS[:], in_=pb[5][0:128, 0:LC]), inc="sS")
            P.wait("tensor", "sS", sbc2)
            P.wait("vector", "sS", sbc2)

            zmark = {}
            wg = None
            ip_half = {}
            for m in range(2 * FDI):
                if m % 2 == 0:
                    ip_half[m // 2] = whalf_ctr[0] % 3
                    whalf_ctr[0] += 1
                half = ip_half[m // 2]
                if m % 2 == 0:
                    P.waitm("sync", whalf_free[half]); whalf_free[half] = None
                    g = m // 2
                    wg = dma("sync", wbuf[:, half * 2048:half * 2048 + 2048]
                             .rearrange("p (k j) -> p k j", j=256),
                             w_in[:, g * 256:(g + 1) * 256]
                             .rearrange("(k p) j -> p k j", p=128),
                             sem=f"sW{half}")
                b = m % 4
                te_bank(b)
                P.wait("tensor", f"sW{half}", wg)
                tm = None
                for k in range(FDM):
                    tm = P.op("tensor", lambda e, mm=m, kk=k, hh=half, bb=b: e.matmul(
                        pb[bb][0:128, 0:LC],
                        wbuf[:, hh * 2048 + kk * 256 + (mm % 2) * 128:
                             hh * 2048 + kk * 256 + (mm % 2) * 128 + 128],
                        xnW[:, kk * LC:(kk + 1) * LC],
                        start=(kk == 0), stop=(kk == FDM - 1)),
                        inc=(f"sT{b}" if k == FDM - 1 else None))
                if m % 2 == 1:
                    whalf_free[half] = (f"sT{b}", tm)
                P.wait("vector", f"sT{b}", tm)
                if m < FDI:
                    vv = P.op("vector", lambda v, mm=m, bb=b: v.tensor_tensor(
                        out=xpad(mm)[:, 3:3 + LC], in0=pb[bb][0:128, 0:LC],
                        in1=r2bcS[:], op=ALU.mult), inc="sV")
                    bank_free[b] = ("sV", vv)
                else:
                    d = m - FDI
                    sl = 512 + (d % 2) * 256
                    P.waitm("vector", zmark.get(d % 2))
                    vv = P.op("vector", lambda v, bb=b, o=sl: v.tensor_tensor(
                        out=sp(o, LC), in0=pb[bb][0:128, 0:LC], in1=r2bcS[:],
                        op=ALU.mult), inc="sV")
                    bank_free[b] = ("sV", vv)
                    P.wait("scalar", "sV", vv)
                    scz = P.op("scalar", lambda s, dd=d, o=sl: s.activation(
                        out=szt(dd), in_=sp(o, LC), func=AF.Silu), inc="sS")
                    zmark[d % 2] = ("sS", scz)
            bar()

            for d in range(FDI):
                P.op("vector", lambda v, dd=d: v.tensor_copy(
                    out=xpad(dd)[:, 0:3], in_=tails[:, dd * 3:(dd + 1) * 3]))
            vtl = P.op("vector", lambda v: v.engine_nop(), inc="sV")
            P.wait("scalar", "sV", vtl)
            conv_marks = [None] * FDI
            for d in range(FDI):
                sl = (d % 2) * 256
                sm = P.op("scalar", lambda s, dd=d, o=sl: s.activation(
                    out=sp(o, LC), in_=xpad(dd)[:, 0:LC], func=AF.Copy,
                    scale=convw[:, dd * 4:dd * 4 + 1]), inc="sS")
                P.wait("vector", "sS", sm)
                for j in range(1, 4):
                    P.op("vector", lambda v, dd=d, jj=j, o=sl: v.scalar_tensor_tensor(
                        out=sp(o, LC), in0=xpad(dd)[:, jj:jj + LC],
                        scalar=convw[:, dd * 4 + jj:dd * 4 + jj + 1], in1=sp(o, LC),
                        op0=ALU.mult, op1=ALU.add))
                P.op("vector", lambda v, dd=d: v.tensor_copy(
                    out=tails[:, dd * 3:(dd + 1) * 3], in_=xpad(dd)[:, LC:LC + 3]))
                vcv = P.op("vector", lambda v: v.engine_nop(), inc="sV")
                P.wait("scalar", "sV", vcv)
                conv_marks[d] = P.op("scalar", lambda s, dd=d, o=sl: s.activation(
                    out=xssm(dd), in_=sp(o, LC), func=AF.Silu,
                    bias=colv[:, 16 + dd:17 + dd]), inc="sS")
            if dbg:
                bar()
            if "xssm" in dbg:
                dbg_pre()
                for d in range(FDI):
                    dma("gpsimd", dbg_h["xssm"][d * 128:(d + 1) * 128, tok0:tok0 + LC],
                        xssm(d), sem="sO")
            if "sz" in dbg:
                dbg_pre()
                for d in range(FDI):
                    dma("gpsimd", dbg_h["sz"][d * 128:(d + 1) * 128, tok0:tok0 + LC],
                        szt(d), sem="sO")
            dbg_guard()

            te_bank(4)
            te_bank(5)
            te_bank(6)
            tpj = None
            for k in range(FDI):
                P.wait("tensor", "sS", conv_marks[k])
                P.op("tensor", lambda e, kk=k: e.matmul(
                    pb[4][0:64, 0:LC], xpw[:, kk * 96:kk * 96 + 64], xssm(kk),
                    start=(kk == 0), stop=(kk == FDI - 1)))
                P.op("tensor", lambda e, kk=k: e.matmul(
                    pb[5][0:16, 0:LC], xpw[:, kk * 96 + 64:kk * 96 + 80], xssm(kk),
                    start=(kk == 0), stop=(kk == FDI - 1)))
                tpj = P.op("tensor", lambda e, kk=k: e.matmul(
                    pb[6][0:16, 0:LC], xpw[:, kk * 96 + 80:kk * 96 + 96], xssm(kk),
                    start=(kk == 0), stop=(kk == FDI - 1)),
                    inc=("sT4" if k == FDI - 1 else None))
            P.wait("vector", "sT4", tpj)
            P.op("vector", lambda v: v.tensor_copy(out=projS[:], in_=pb[4][0:64, 0:LC]))
            P.op("vector", lambda v: v.tensor_copy(out=projB[:], in_=pb[5][0:16, 0:LC]))
            vpj = P.op("vector", lambda v: v.tensor_copy(
                out=projC[:], in_=pb[6][0:16, 0:LC]), inc="sV")
            P.wait("scalar", "sV", vpj)
            s64 = P.op("scalar", lambda s: s.activation(
                out=sp(0, LC)[0:64, :], in_=projS[0:64, 0:LC], func=AF.Square), inc="sS")
            P.wait("tensor", "sS", s64)
            t64 = P.op("tensor", lambda e: e.matmul(
                pb[5][0:1, 0:LC], onescol[0:64, 0:1], sp(0, LC)[0:64, :],
                start=True, stop=True), inc="sT5")
            P.wait("scalar", "sT5", t64)
            s64b = P.op("scalar", lambda s: s.activation(
                out=rows[0:1, RW_A:RW_A + LC], in_=pb[5][0:1, 0:LC], func=AF.Sqrt,
                scale=np.float32(1.0 / R), bias=np.float32(1e-6)), inc="sS")
            P.wait("vector", "sS", s64b)
            v64 = P.op("vector", lambda v: v.reciprocal(
                out=rows[0:1, RW_B:RW_B + LC], in_=rows[0:1, RW_A:RW_A + LC]), inc="sV")
            P.wait("tensor", "sV", v64)
            t64b = P.op("tensor", lambda e: e.matmul(
                pb[5][0:64, 0:LC], onesrow[0:1, 0:64], rows[0:1, RW_B:RW_B + LC],
                start=True, stop=True), inc="sT5")
            P.wait("vector", "sT5", t64b)
            vdt = P.op("vector", lambda v: v.tensor_tensor(
                out=dtrms[:], in0=projS[0:64, 0:LC], in1=pb[5][0:64, 0:LC],
                op=ALU.mult), inc="sV")
            for (psrc, colid, dst, rw1, rw2) in ((projB, 1, BmT, RW_C, RW_D),
                                                 (projC, 2, CmT, RW_E, RW_F)):
                sB = P.op("scalar", lambda s, pp=psrc: s.activation(
                    out=sp(256, LC)[0:16, :], in_=pp[0:16, 0:LC], func=AF.Square), inc="sS")
                P.wait("tensor", "sS", sB)
                te_bank(6)
                tB = P.op("tensor", lambda e: e.matmul(
                    pb[6][0:1, 0:LC], onescol[0:16, 0:1], sp(256, LC)[0:16, :],
                    start=True, stop=True), inc="sT6")
                P.wait("scalar", "sT6", tB)
                sB2 = P.op("scalar", lambda s, w=rw1: s.activation(
                    out=rows[0:1, w:w + LC], in_=pb[6][0:1, 0:LC], func=AF.Sqrt,
                    scale=np.float32(1.0 / DS), bias=np.float32(1e-6)), inc="sS")
                P.wait("vector", "sS", sB2)
                vB = P.op("vector", lambda v, w1=rw1, w2=rw2: v.reciprocal(
                    out=rows[0:1, w2:w2 + LC], in_=rows[0:1, w1:w1 + LC]), inc="sV")
                P.wait("tensor", "sV", vB)
                tB2 = P.op("tensor", lambda e, w2=rw2: e.matmul(
                    pb[6][0:16, 0:LC], onesrow[0:1, 0:16], rows[0:1, w2:w2 + LC],
                    start=True, stop=True), inc="sT6")
                P.wait("vector", "sT6", tB2)
                vB2 = P.op("vector", lambda v, pp=psrc, cc=colid, dd=dst: v.scalar_tensor_tensor(
                    out=dd[:], in0=pp[0:16, 0:LC], scalar=smallv[0:16, cc:cc + 1],
                    in1=pb[6][0:16, 0:LC], op0=ALU.mult, op1=ALU.mult), inc="sV")
                bank_free[6] = ("sV", vB2)
            bar()
            if "bm" in dbg:
                dbg_pre()
                dma("gpsimd", dbg_h["bm"][0:16, tok0:tok0 + LC], BmT[:], sem="sO")
            if "cm" in dbg:
                dbg_pre()
                dma("gpsimd", dbg_h["cm"][0:16, tok0:tok0 + LC], CmT[:], sem="sO")
            dbg_guard()

            dslot = whalf_ctr[0] % 3
            whalf_ctr[0] += 1
            P.waitm("sync", whalf_free[dslot]); whalf_free[dslot] = None
            wdt = dma("sync", wbuf[0:64, dslot * 2048:(dslot + 1) * 2048]
                      .rearrange("p (k j) -> p k j", j=128),
                      dtproj_w[:].rearrange("p (k j) -> p k j", j=128),
                      sem=f"sW{dslot}")
            P.wait("scalar", f"sW{dslot}", wdt)
            sfold = P.op("scalar", lambda s, o=dslot * 2048: s.activation(
                out=wbuf[0:64, o:o + 2048], in_=wbuf[0:64, o:o + 2048],
                func=AF.Copy, scale=smallv[0:64, 0:1]), inc="sS")
            P.wait("tensor", "sS", sfold)
            vdt2_marks = {}
            for m in range(FDI):
                b = m % 4
                te_bank(b)
                tdm = P.op("tensor", lambda e, mm=m, bb=b, o=dslot * 2048: e.matmul(
                    pb[bb][0:128, 0:LC], wbuf[0:64, o + mm * 128:o + (mm + 1) * 128],
                    dtrms[0:64, 0:LC], start=True, stop=True), inc=f"sT{b}")
                P.wait("scalar", f"sT{b}", tdm)
                sl = (m % 2) * 1024
                P.waitm("scalar", vdt2_marks.get(m % 2))
                P.op("scalar", lambda s, bb=b, mm=m, o=sl: s.activation(
                    out=sp(o, LC), in_=pb[bb][0:128, 0:LC], func=AF.Relu,
                    bias=colv[:, 32 + mm:33 + mm]))
                sab = P.op("scalar", lambda s, bb=b, mm=m, o=sl: s.activation(
                    out=sp(o + 256, LC), in_=pb[bb][0:128, 0:LC], func=AF.Abs,
                    bias=colv[:, 32 + mm:33 + mm]), inc="sS")
                bank_free[b] = ("sS", sab)
                P.op("scalar", lambda s, o=sl: s.activation(
                    out=sp(o + 512, LC), in_=sp(o + 256, LC), func=AF.Exp,
                    scale=np.float32(-1.0)))
                sln = P.op("scalar", lambda s, o=sl: s.activation(
                    out=sp(o + 768, LC), in_=sp(o + 512, LC), func=AF.Ln,
                    bias=np.float32(1.0)), inc="sS")
                P.wait("vector", "sS", sln)
                vdt2 = P.op("vector", lambda v, mm=m, o=sl: v.tensor_tensor(
                    out=dtTt(mm), in0=sp(o, LC), in1=sp(o + 768, LC), op=ALU.add), inc="sV")
                vdt2_marks[m % 2] = ("sV", vdt2)
                if m == FDI - 1:
                    whalf_free[dslot] = (f"sT{b}", tdm)
            bar()
            if "dt" in dbg:
                dbg_pre()
                for d in range(FDI):
                    dma("gpsimd", dbg_h["dt"][d * 128:(d + 1) * 128, tok0:tok0 + LC],
                        dtTt(d), sem="sO")
            dbg_guard()
            for d in range(FDI):
                P.op("scalar", lambda s, dd=d: s.activation(
                    out=ya(dd), in_=xssm(dd), func=AF.Copy, scale=colv[:, 48 + dd:49 + dd]))
            for d in range(FDI):
                P.op("vector", lambda v, dd=d: v.tensor_tensor(
                    out=dtx(dd), in0=dtTt(dd), in1=xssm(dd), op=ALU.mult))
            bar()

            def pq0(i):
                return hiddenT[:, i * 128:(i + 1) * 128] if i < 16 else \
                    sT_[:, (i - 16) * 128:(i - 15) * 128]

            def pq1(i):
                return arena(4160 + i * 128, 128)

            def hq1t(i):
                return sT_[:, i * 128:(i + 1) * 128] if i < 16 else \
                    dtT[:, (i - 16) * 128:(i - 15) * 128]

            def hqt(i):
                return xnW[:, i * 128:(i + 1) * 128] if i < 16 else \
                    arena(6976 + (i - 16) * 128, 128)

            def hq_home(off, n_):
                if off < 2048:
                    return xnW[:, off:off + n_]
                return arena(6976 + (off - 2048), n_)

            HQ_SLICES = ((0, 1024), (1024, 1024), (2048, 768))

            F_DA, F_U, F_H, F_P, F_CBS = 0, 1028, 2056, 3084, 4108
            for buf in range(2):
                P.op("vector", lambda v, b_=buf: v.memset(
                    SP[:, F_DA + b_ * 514:F_DA + b_ * 514 + 514:257], 0.0))
            scan_marks = [None, None]
            ymark = [None, None]
            pmark = [None, None]
            cbs_mark = [None, None]
            hs_mark = [None, None]
            hs_pend = None
            P.wait("gpsimd", "sS", P.cnt["sS"])

            gsm_q = {}
            fp4_steps = []
            fp4_state = {}

            def fp4_make_steps(mp):
                def s_load():
                    fp4_state[(mp, "ldg")] = load_pair(wqtg, mp)
                    fp4_state[(mp, "ldu")] = load_pair(wqtu, mp)

                def s_gate(sub):
                    m = mp + sub
                    gs = r2bcS if m % 2 == 0 else r3bcS
                    b = m % 2
                    te_bank(b)
                    tm = pair_mm(fp4_state[(mp, "ldg")], sub, b,
                                 lambda k: xqT[:, k * LC:(k + 1) * LC])
                    P.waitm("scalar", tm)
                    P.waitm("scalar", gsm_q.get(m % 2))
                    sg_ = P.op("scalar", lambda s, bb=b, g_=gs: s.activation(
                        out=g_[:], in_=pb[bb][0:128, 0:LC], func=AF.Silu), inc="sS")
                    bank_free[b] = ("sS", sg_)
                    fp4_state[(m, "sg")] = sg_

                def s_up(sub):
                    m = mp + sub
                    b2 = 2 + (m % 2)
                    te_bank(b2)
                    fp4_state[(m, "tm2")] = pair_mm(
                        fp4_state[(mp, "ldu")], sub, b2,
                        lambda k: xqT[:, k * LC:(k + 1) * LC])

                def s_prod(sub):
                    m = mp + sub
                    gs = r2bcS if m % 2 == 0 else r3bcS
                    b2 = 2 + (m % 2)
                    P.waitm("vector", fp4_state[(m, "tm2")])
                    P.wait("vector", "sS", fp4_state[(m, "sg")])
                    P.op("vector", lambda v, bb=b2, g_=gs, mm=m: v.tensor_tensor(
                        out=pq0(mm), in0=g_[:, 0:128],
                        in1=pb[bb][0:128, 0:128], op=ALU.mult))
                    vp1 = P.op("vector", lambda v, bb=b2, g_=gs, mm=m: v.tensor_tensor(
                        out=pq1(mm), in0=g_[:, 128:256],
                        in1=pb[bb][0:128, 128:256], op=ALU.mult))
                    bank_free[b2] = ("sV", vp1)
                    gsm_q[m % 2] = ("sV", vp1)
                    fp4_state["vp_last"] = vp1

                fp4_steps.append(s_load)
                fp4_steps.append(lambda: s_gate(0))
                fp4_steps.append(lambda: (s_up(0), s_prod(0)))
                fp4_steps.append(lambda: s_gate(1))
                fp4_steps.append(lambda: (s_up(1), s_prod(1)))

            for mp_ in range(0, FQ, 2):
                fp4_make_steps(mp_)
            fp4_sched = {}
            tot_slots = 14 * 8
            for si in range(len(fp4_steps)):
                fp4_sched.setdefault(si * tot_slots // len(fp4_steps), []).append(si)

            tr0_last = None
            tr1_last = None
            for n in range(DS):
                bp = 4 + (n % 2)
                te_bank(bp)
                tbm = P.op("tensor", lambda e, nn=n, bb=bp: e.matmul(
                    pb[bb][0:128, 0:LC], rowsel[:, nn * 128:(nn + 1) * 128],
                    BmT[0:16, 0:LC], start=True, stop=True), inc=f"sT{bp}")
                te_bank(6)
                tc = P.op("tensor", lambda e, nn=n: e.matmul(
                    pb[6][0:128, 0:LC], rowsel[:, nn * 128:(nn + 1) * 128],
                    CmT[0:16, 0:LC], start=True, stop=True), inc="sT6")
                P.wait("vector", f"sT{bp}", tbm)
                P.wait("scalar", "sT6", tc)
                P.waitm("scalar", cbs_mark[n % 2])
                scb = P.op("scalar", lambda s, nn=n: s.copy(
                    out=sp(F_CBS + (nn % 2) * 256, LC), in_=pb[6][0:128, 0:LC]))
                bank_free[6] = ("sS", scb)
                ulast = None
                for j in range(FDI // 2):
                    pj = n * 8 + j
                    buf = pj % 2
                    dab = sp(F_DA + buf * 514, 514)
                    ub = sp(F_U + buf * 514, 514)
                    hb = sp(F_H + buf * 514, 514)
                    pbuf = sp(F_P + buf * 512, 512)
                    P.waitm("scalar", scan_marks[buf])
                    P.op("scalar", lambda s, a=ub, nn=n, jj=j: s.copy(
                        out=a[:, 0:514:257],
                        in_=hstate[:, nn * FDI + 2 * jj:nn * FDI + 2 * jj + 2]))
                    for hh in range(2):
                        d = 2 * j + hh
                        sda = P.op("scalar", lambda s, dd=d, nn=n, a=dab, o=1 + 257 * hh:
                                   s.activation(
                                       out=a[:, o:o + LC], in_=dtTt(dd), func=AF.Exp,
                                       scale=aneg[:, dd * 16 + nn:dd * 16 + nn + 1]), inc="sS")
                    if hs_pend is not None:
                        hs_pend()
                        hs_pend = None
                    ulast = P.op("vector", lambda v, a=ub, jj=j, bb=bp: v.tensor_tensor(
                        out=a.rearrange("p (two l) -> p two l", l=257)[:, :, 1:257],
                        in0=arena(A_XPAD + 2 * jj * 256, 512).rearrange(
                            "p (two l) -> p two l", l=256),
                        in1=pb[bb][0:128, 0:LC].unsqueeze(1).broadcast_to([128, 2, LC]),
                        op=ALU.mult))
                    P.wait("vector", "sS", sda)
                    P.waitm("vector", pmark[buf])
                    P.waitm("vector", hs_mark[buf])
                    sc_ = P.op("vector", lambda v, a=hb, d0=dab, d1=ub: v.tensor_tensor_scan(
                        out=a, data0=d0, data1=d1, initial=0.0,
                        op0=ALU.mult, op1=ALU.add))
                    scan_marks[buf] = ("sV", sc_)

                    def _mk_hsave(a=hb, nn=n, jj=j, sv=sc_, bf=buf):
                        def emit():
                            P.wait("scalar", "sV", sv)
                            hs_ = P.op("scalar", lambda s: s.copy(
                                out=hstate[:, nn * FDI + 2 * jj:
                                           nn * FDI + 2 * jj + 2],
                                in_=a[:, 256:514:257]))
                            hs_mark[bf] = ("sS", hs_)
                        return emit
                    hs_pend = _mk_hsave()
                    P.wait("gpsimd", "sV", sc_)
                    P.wait("gpsimd", "sS", scb)
                    P.waitm("gpsimd", ymark[buf])
                    pg = P.op("gpsimd", lambda g, a=pbuf, hh_=hb, nn=n: g.tensor_tensor(
                        out=a.rearrange("p (two l) -> p two l", l=256),
                        in0=hh_.rearrange("p (two l) -> p two l", l=257)[:, :, 1:257],
                        in1=sp(F_CBS + (nn % 2) * 256, LC).unsqueeze(1).broadcast_to(
                            [128, 2, LC]),
                        op=ALU.mult))
                    pmark[buf] = ("sP", pg)
                    if j % 2 == 0:
                        yv = P.op("gpsimd", lambda g, a=pbuf, jj=j: g.tensor_tensor(
                            out=arena(A_Y + 2 * jj * 256, 512),
                            in0=arena(A_Y + 2 * jj * 256, 512), in1=a, op=ALU.add))
                        ymark[buf] = ("sP", yv)
                    else:
                        P.wait("vector", "sP", pg)
                        yv = P.op("vector", lambda v, a=pbuf, jj=j: v.tensor_tensor(
                            out=arena(A_Y + 2 * jj * 256, 512),
                            in0=arena(A_Y + 2 * jj * 256, 512), in1=a, op=ALU.add))
                        ymark[buf] = ("sV", yv)
                    for si in fp4_sched.get(n * 8 + j, ()):
                        fp4_steps[si]()
                bank_free[bp] = ("sV", ulast)
                cbs_mark[n % 2] = pmark[(n * 8 + 7) % 2]
                if n == 15:
                    if hs_pend is not None:
                        hs_pend()
                        hs_pend = None
                    for i in range(FQ):
                        te_bank(7)
                        t = P.op("tensor", lambda e, ii=i: e.transpose(
                            pb[7][0:128, 0:128], pq1(ii), sident[:]), inc="sT7")
                        P.wait("scalar", "sT7", t)
                        sc = P.op("scalar", lambda s, ii=i: s.copy(
                            out=hq1t(ii), in_=pb[7][0:128, 0:128]), inc="sS")
                        bank_free[7] = ("sS", sc)
                        tr1_last = ("sS", sc)
                if n == 14:
                    P.wait("tensor", "sV", fp4_state["vp_last"])
                    for i in range(FQ):
                        te_bank(7)
                        t = P.op("tensor", lambda e, ii=i: e.transpose(
                            pb[7][0:128, 0:128], pq0(ii), sident[:]), inc="sT7")
                        P.wait("scalar", "sT7", t)
                        sc = P.op("scalar", lambda s, ii=i: s.copy(
                            out=hqt(ii), in_=pb[7][0:128, 0:128]), inc="sS")
                        bank_free[7] = ("sS", sc)
                        tr0_last = ("sS", sc)

            if hs_pend is not None:
                hs_pend()
                hs_pend = None

            def hqT0(i):
                return arena(768 + i * 128, 128)
            if "y" in dbg:
                dbg_pre()
                for d in range(FDI):
                    dma("gpsimd", dbg_h["y"][d * 128:(d + 1) * 128, tok0:tok0 + LC],
                        ya(d), sem="sO")
                dbg_guard()
            P.wait("vector", "sP", P.cnt["sP"])
            vg = None
            for d in range(FDI):
                vg = P.op("vector", lambda v, dd=d: v.tensor_tensor(
                    out=ya(dd), in0=ya(dd), in1=szt(dd), op=ALU.mult), inc="sV")
            P.wait("tensor", "sV", vg)
            def hq1_home(off, w_):
                return sT_[:, off:off + w_] if off < 2048 else \
                    dtT[:, off - 2048:off - 2048 + w_]
            gq1 = None
            P.wait("vector", "sS", tr1_last[1])
            for (off, w_) in HQ_SLICES:
                nb = w_ // 16
                gq1 = ("sV", emit_quant(
                    hq1_home(off, w_), w_, nb,
                    (sp(QW_AV, w_), sp(QW_ACC, w_), sp(QW_AC2, w_),
                     sp(QW_X, w_), sp(QW_SG, w_)),
                    (arena(0, nb), arena(256, nb), arena(512, nb))))
            for mo in range(FDM):
                half = whalf_ctr[0] % 3
                whalf_ctr[0] += 1
                P.waitm("sync", whalf_free[half]); whalf_free[half] = None
                wg = dma("sync", wbuf[:, half * 2048:half * 2048 + 2048]
                         .rearrange("p (k j) -> p k j", j=128),
                         wout[:, mo * 128:(mo + 1) * 128]
                         .rearrange("(k p) j -> p k j", p=128),
                         sem=f"sW{half}")
                b = mo % 7
                te_bank(b)
                P.wait("tensor", f"sW{half}", wg)
                tm = None
                for k in range(FDI):
                    tm = P.op("tensor", lambda e, kk=k, hh=half, bb=b: e.matmul(
                        pb[bb][0:128, 0:LC],
                        wbuf[:, hh * 2048 + kk * 128: hh * 2048 + (kk + 1) * 128],
                        ya(kk), start=(kk == 0), stop=(kk == FDI - 1)),
                        inc=(f"sT{b}" if k == FDI - 1 else None))
                whalf_free[half] = (f"sT{b}", tm)
                P.wait("vector", f"sT{b}", tm)
                vh = P.op("vector", lambda v, m=mo, bb=b: v.tensor_tensor(
                    out=hiddenT[:, m * LC:(m + 1) * LC], in0=pb[bb][0:128, 0:LC],
                    in1=xnT[:, m * LC:(m + 1) * LC], op=ALU.add), inc="sV")
                bank_free[b] = ("sV", vh)
            vq0 = None
            for (off, w_) in HQ_SLICES:
                nb = w_ // 16
                vq0 = emit_quant(
                    hq_home(off, w_), w_, nb,
                    (sp(QW_AV, w_), sp(QW_ACC, w_), sp(QW_AC2, w_),
                     sp(QW_X, w_), sp(QW_SG, w_)),
                    (arena(0, nb), arena(256, nb), arena(512, nb)))
            P.waitm("tensor", gq1)
            for i in range(FQ):
                te_bank(7)
                t = P.op("tensor", lambda e, ii=i: e.transpose(
                    pb[7][0:128, 0:128], hq1t(ii), sident[:]), inc="sT7")
                P.wait("scalar", "sT7", t)
                sc = P.op("scalar", lambda s, ii=i: s.copy(
                    out=pq1(ii), in_=pb[7][0:128, 0:128]), inc="sS")
                bank_free[7] = ("sS", sc)
            P.wait("tensor", "sV", vq0)
            for i in range(FQ):
                te_bank(7)
                t = P.op("tensor", lambda e, ii=i: e.transpose(
                    pb[7][0:128, 0:128], hqt(ii), sident[:]), inc="sT7")
                P.wait("scalar", "sT7", t)
                sc = P.op("scalar", lambda s, ii=i: s.copy(
                    out=hqT0(ii), in_=pb[7][0:128, 0:128]), inc="sS")
                bank_free[7] = ("sS", sc)
            P.wait("tensor", "sS", P.cnt["sS"])
            for mo in range(FDM):
                b = 4 + (mo % 3)
                te_bank(b)
                tmv = None
                for seg in range(2):
                    k0, k1 = seg * 11, seg * 11 + 11
                    slot = whalf_ctr[0] % 3
                    whalf_ctr[0] += 1
                    P.waitm("sync", whalf_free[slot]); whalf_free[slot] = None
                    so = slot * 2048
                    wgx = dma("sync", wbuf[:, so:so + 11 * 128]
                              .rearrange("p (k j) -> p k j", j=128),
                              wqtd[k0 * 128:k1 * 128, mo * 128:(mo + 1) * 128]
                              .rearrange("(k p) j -> p k j", p=128),
                              sem=f"sW{slot}")
                    P.wait("tensor", f"sW{slot}", wgx)
                    for k in range(k0, k1):
                        P.op("tensor", lambda e, kk=k, o=so, kz=k0, bb=b: e.matmul(
                            pb[bb][0:128, 0:128],
                            wbuf[:, o + (kk - kz) * 128: o + (kk - kz + 1) * 128],
                            hqT0(kk), start=(kk == 0), stop=False))
                        tmv = P.op("tensor", lambda e, kk=k, o=so, kz=k0, bb=b: e.matmul(
                            pb[bb][0:128, 128:256],
                            wbuf[:, o + (kk - kz) * 128: o + (kk - kz + 1) * 128],
                            pq1(kk), start=False, stop=(kk == FQ - 1)),
                            inc=(f"sT{b}" if k == k1 - 1 else None))
                    whalf_free[slot] = (f"sT{b}", tmv)
                P.waitm("scalar", (f"sT{b}", tmv))
                sg_ = P.op("scalar", lambda s, m=mo, bb=b: s.activation(
                    out=dtT[:, 1024 + m * 256:1024 + (m + 1) * 256],
                    in_=pb[bb][0:128, 0:256], func=AF.Silu), inc="sS")
                bank_free[b] = ("sS", sg_)
            bar()
            if "g" in dbg:
                dbg_pre()
                for mo in range(FDM):
                    for hh in range(2):
                        dma("gpsimd",
                            dbg_h["g"][mo * 128:(mo + 1) * 128,
                                       tok0 + hh * 128:tok0 + hh * 128 + 128],
                            dtT[:, 1024 + mo * 256 + hh * 128:
                                1024 + mo * 256 + hh * 128 + 128],
                            sem="sO")
            dbg_guard()
            if "hid" in dbg:
                dbg_pre()
                for mo in range(FDM):
                    dma("gpsimd", dbg_h["hid"][mo * 128:(mo + 1) * 128, tok0:tok0 + LC],
                        hiddenT[:, mo * LC:(mo + 1) * LC], sem="sO")
            dbg_guard()

            te_bank(5)
            slot_mark = {}
            for fi in range(FDM):
                sl = fi % 2
                P.waitm("scalar", slot_mark.get(sl))
                ssq = P.op("scalar", lambda s, f=fi, o=sl: s.activation(
                    out=sp(o * 256, LC), in_=hiddenT[:, f * LC:(f + 1) * LC],
                    func=AF.Square), inc="sS")
                P.wait("tensor", "sS", ssq)
                tm = P.op("tensor", lambda e, o=sl, f=fi: e.matmul(
                    pb[5][0:1, 0:LC], onescol[:, 0:1], sp(o * 256, LC),
                    start=(f == 0), stop=(f == FDM - 1)), inc="sT5")
                slot_mark[sl] = ("sT5", tm)
            P.wait("scalar", "sT5", P.cnt["sT5"])
            sm3 = P.op("scalar", lambda s: s.activation(
                out=rows[0:1, RW_A:RW_A + LC], in_=pb[5][0:1, 0:LC], func=AF.Sqrt,
                scale=np.float32(1.0 / DM), bias=np.float32(1e-6)), inc="sS")
            P.wait("vector", "sS", sm3)
            vr3 = P.op("vector", lambda v: v.reciprocal(
                out=rows[0:1, RW_B:RW_B + LC], in_=rows[0:1, RW_A:RW_A + LC]), inc="sV")
            P.wait("tensor", "sV", vr3)
            tbc3 = P.op("tensor", lambda e: e.matmul(
                pb[5][0:128, 0:LC], onesrow[0:1, 0:128], rows[0:1, RW_B:RW_B + LC],
                start=True, stop=True), inc="sT5")
            P.wait("scalar", "sT5", tbc3)
            sbc3 = P.op("scalar", lambda s: s.copy(out=r3bcS[:], in_=pb[5][0:128, 0:LC]), inc="sS")
            P.wait("vector", "sS", sbc3)
            vhn = None
            for m in range(FDM):
                vhn = P.op("vector", lambda v, mm=m: v.scalar_tensor_tensor(
                    out=arena(A_HNT + mm * LC, LC), in0=hiddenT[:, mm * LC:(mm + 1) * LC],
                    scalar=colv[:, 8 + mm:9 + mm], in1=r3bcS[:],
                    op0=ALU.mult, op1=ALU.mult), inc="sV")
            P.wait("tensor", "sV", vhn)


            gsm = {}
            for mp in range(0, FFF, 2):
                ldg = load_pair(gate_w, mp)
                ldu = load_pair(up_w, mp)
                for sub in range(2):
                    m = mp + sub
                    b = m % 2
                    te_bank(b)
                    tm = pair_mm(ldg, sub, b, lambda k: arena(A_HNT + k * LC, LC))
                    P.waitm("scalar", tm)
                    sl = 512 + (m % 2) * 256
                    P.waitm("scalar", gsm.get(m % 2))
                    sg_ = P.op("scalar", lambda s, bb=b, o=sl: s.activation(
                        out=sp(o, LC), in_=pb[bb][0:128, 0:LC], func=AF.Silu), inc="sS")
                    bank_free[b] = ("sS", sg_)
                    b2 = 2 + (m % 2)
                    te_bank(b2)
                    tm2 = pair_mm(ldu, sub, b2, lambda k: arena(A_HNT + k * LC, LC))
                    P.waitm("vector", tm2)
                    P.wait("vector", "sS", sg_)
                    vp = P.op("vector", lambda v, bb=b2, o=sl, mm=m: v.tensor_tensor(
                        out=arena(A_PROD + mm * LC, LC), in0=sp(o, LC),
                        in1=pb[bb][0:128, 0:LC], op=ALU.mult), inc="sV")
                    bank_free[b2] = ("sV", vp)
                    gsm[m % 2] = ("sV", vp)
            P.wait("tensor", "sV", P.cnt["sV"])
            for mo in range(FDM):
                b = mo % 4
                te_bank(b)
                tm = stream_mm(down_w, mo, FFF, lambda k: arena(A_PROD + k * LC, LC), b)
                P.waitm("vector", tm)
                vd_ = P.op("vector", lambda v, m=mo, bb=b: v.tensor_tensor(
                    out=sT_[:, m * LC:(m + 1) * LC], in0=pb[bb][0:128, 0:LC],
                    in1=hiddenT[:, m * LC:(m + 1) * LC], op=ALU.add), inc="sV")
                bank_free[b] = ("sV", vd_)
            bar()
            if "s" in dbg:
                dbg_pre()
                for mo in range(FDM):
                    dma("gpsimd", dbg_h["s"][mo * 128:(mo + 1) * 128, tok0:tok0 + LC],
                        sT_[:, mo * LC:(mo + 1) * LC], sem="sO")
            dbg_guard()

            vpr = None
            for mo in range(FDM):
                vpr = P.op("vector", lambda v, m=mo: v.tensor_tensor(
                    out=arena(A_PR + m * LC, LC), in0=sT_[:, m * LC:(m + 1) * LC],
                    in1=dtT[:, 1024 + m * 256:1024 + (m + 1) * 256],
                    op=ALU.mult), inc="sV")
            P.wait("tensor", "sV", vpr)
            vo = None
            for mo in range(FDM):
                for tt in range(NTT):
                    b = 6 + ((mo * NTT + tt) % 2)
                    te_bank(b)
                    t = P.op("tensor", lambda e, m=mo, t_=tt, bb=b: e.transpose(
                        pb[bb][0:128, 0:128], arena(A_PR + m * LC + t_ * 128, 128),
                        sident[:]), inc=f"sT{b}")
                    P.wait("vector", f"sT{b}", t)
                    vo = P.op("vector", lambda v, m=mo, t_=tt, bb=b: v.tensor_tensor(
                        out=xnW[:, t_ * DM + m * 128: t_ * DM + (m + 1) * 128],
                        in0=pb[bb][0:128, 0:128],
                        in1=x_t[:, t_ * DM + m * 128: t_ * DM + (m + 1) * 128],
                        op=ALU.add), inc="sV")
                    bank_free[b] = ("sV", vo)
            P.wait("gpsimd", "sV", vo)
            for tt in range(NTT):
                dma("gpsimd", out_h[tok0 + tt * 128: tok0 + (tt + 1) * 128, :],
                    xnW[:, tt * DM:(tt + 1) * DM], sem="sO")
            last_combine = ("sV", vo)
            bar()

        P.wait("gpsimd", "sO", P.cnt["sO"])
        P.wait("gpsimd", "sG", P.cnt["sG"])
        P.op("gpsimd", lambda g: g.nop())

        def runner(eng_name):
            self_sem = SELF_SEM.get(eng_name)

            def run(eng):
                idx = 0
                for e in P.ev[eng_name]:
                    if e[0] == "auto":
                        if idx > 0:
                            eng.wait_ge(sems[self_sem], idx)
                        r = e[1](eng)
                        if r is None:
                            r = eng.nop()
                        r.then_inc(sems[self_sem], 1)
                        idx += 1
                    elif e[0] == "op":
                        e[1](eng)
                    elif e[0] == "inc":
                        e[1](eng).then_inc(sems[e[2]], e[3])
                    else:
                        eng.wait_ge(sems[e[1]], e[2])
            return run

        block.tensor(runner("tensor"))
        block.vector(runner("vector"))
        block.scalar(runner("scalar"))
        block.sync(runner("sync"))
        block.gpsimd(runner("gpsimd"))

    return nc


def make_inputs(inp, b):
    m = {"x": np.ascontiguousarray(np.asarray(inp["L0_attn_hidden_states"])[b])}
    names = {
        "in_proj_w": "L0_attn_in_proj_w", "conv_w": "L0_attn_conv_w",
        "conv_b": "L0_attn_conv_b", "x_proj_w": "L0_attn_x_proj_w",
        "dt_proj_w": "L0_attn_dt_proj_w", "dt_proj_b": "L0_attn_dt_proj_b",
        "dt_ln_w": "L0_attn_dt_ln_w", "b_ln_w": "L0_attn_b_ln_w",
        "c_ln_w": "L0_attn_c_ln_w", "A_log": "L0_attn_A_log",
        "D_skip": "L0_attn_D_skip", "out_proj_w": "L0_attn_out_proj_w",
        "input_ln_w": "L0_attn_input_ln_w", "pre_moe_ln_w": "L0_attn_pre_moe_ln_w",
        "gate_w": "L0_attn_gate_w", "up_w": "L0_attn_up_w",
        "down_w": "L0_attn_down_w", "ffn_w_gate": "L0_ffn_w_gate",
        "ffn_w_up": "L0_ffn_w_up", "ffn_w_down": "L0_ffn_w_down",
    }
    for k, src in names.items():
        m[k] = np.ascontiguousarray(np.asarray(inp[src]))
    return m

def make_inputs(inp, b):
    m = {"x": np.ascontiguousarray(np.asarray(inp["L0_attn_hidden_states"])[b])}
    names = {
        "in_proj_w": "L0_attn_in_proj_w", "conv_w": "L0_attn_conv_w",
        "conv_b": "L0_attn_conv_b", "x_proj_w": "L0_attn_x_proj_w",
        "dt_proj_w": "L0_attn_dt_proj_w", "dt_proj_b": "L0_attn_dt_proj_b",
        "dt_ln_w": "L0_attn_dt_ln_w", "b_ln_w": "L0_attn_b_ln_w",
        "c_ln_w": "L0_attn_c_ln_w", "A_log": "L0_attn_A_log",
        "D_skip": "L0_attn_D_skip", "out_proj_w": "L0_attn_out_proj_w",
        "input_ln_w": "L0_attn_input_ln_w", "pre_moe_ln_w": "L0_attn_pre_moe_ln_w",
        "gate_w": "L0_attn_gate_w", "up_w": "L0_attn_up_w",
        "down_w": "L0_attn_down_w", "ffn_w_gate": "L0_ffn_w_gate",
        "ffn_w_up": "L0_ffn_w_up", "ffn_w_down": "L0_ffn_w_down",
    }
    for k, src in names.items():
        m[k] = np.ascontiguousarray(np.asarray(inp[src]))
    return m

def make_inputs(inp, b):
    m = {"x": np.ascontiguousarray(np.asarray(inp["L0_attn_hidden_states"])[b])}
    names = {
        "in_proj_w": "L0_attn_in_proj_w", "conv_w": "L0_attn_conv_w",
        "conv_b": "L0_attn_conv_b", "x_proj_w": "L0_attn_x_proj_w",
        "dt_proj_w": "L0_attn_dt_proj_w", "dt_proj_b": "L0_attn_dt_proj_b",
        "dt_ln_w": "L0_attn_dt_ln_w", "b_ln_w": "L0_attn_b_ln_w",
        "c_ln_w": "L0_attn_c_ln_w", "A_log": "L0_attn_A_log",
        "D_skip": "L0_attn_D_skip", "out_proj_w": "L0_attn_out_proj_w",
        "input_ln_w": "L0_attn_input_ln_w", "pre_moe_ln_w": "L0_attn_pre_moe_ln_w",
        "gate_w": "L0_attn_gate_w", "up_w": "L0_attn_up_w",
        "down_w": "L0_attn_down_w", "ffn_w_gate": "L0_ffn_w_gate",
        "ffn_w_up": "L0_ffn_w_up", "ffn_w_down": "L0_ffn_w_down",
    }
    for k, src in names.items():
        m[k] = np.ascontiguousarray(np.asarray(inp[src]))
    return m

def make_inputs(inp, b):
    m = {"x": np.ascontiguousarray(np.asarray(inp["L0_attn_hidden_states"])[b])}
    names = {
        "in_proj_w": "L0_attn_in_proj_w", "conv_w": "L0_attn_conv_w",
        "conv_b": "L0_attn_conv_b", "x_proj_w": "L0_attn_x_proj_w",
        "dt_proj_w": "L0_attn_dt_proj_w", "dt_proj_b": "L0_attn_dt_proj_b",
        "dt_ln_w": "L0_attn_dt_ln_w", "b_ln_w": "L0_attn_b_ln_w",
        "c_ln_w": "L0_attn_c_ln_w", "A_log": "L0_attn_A_log",
        "D_skip": "L0_attn_D_skip", "out_proj_w": "L0_attn_out_proj_w",
        "input_ln_w": "L0_attn_input_ln_w", "pre_moe_ln_w": "L0_attn_pre_moe_ln_w",
        "gate_w": "L0_attn_gate_w", "up_w": "L0_attn_up_w",
        "down_w": "L0_attn_down_w", "ffn_w_gate": "L0_ffn_w_gate",
        "ffn_w_up": "L0_ffn_w_up", "ffn_w_down": "L0_ffn_w_down",
    }
    for k, src in names.items():
        m[k] = np.ascontiguousarray(np.asarray(inp[src]))
    return m


_NC_CACHE = {}


def kernel(**inputs):
    from concourse.bass_utils import run_bass_kernel_spmd
    if "nc" not in _NC_CACHE:
        _NC_CACHE["nc"] = build_program(nchunks=8, dbg=())
    nc = _NC_CACHE["nc"]
    in_maps = [make_inputs(inputs, b) for b in range(8)]
    res = run_bass_kernel_spmd(nc, in_maps, core_ids=list(range(8)))
    return np.stack([np.asarray(r["out"]) for r in res.results], axis=0)


# version 15
# speedup vs baseline: 1.0622x; 1.0622x over previous
import numpy as np
import concourse.bass as bass
import concourse.mybir as mybir
from contextlib import ExitStack

F32 = mybir.dt.float32
AF = mybir.ActivationFunctionType
ALU = mybir.AluOpType
AX = mybir.AxisListType

T, DM, DI, DS, R, DFF, DQ, KC = 2048, 1024, 2048, 16, 64, 4096, 2816, 4
LC = 256
NTT = LC // 128
FDM, FDI, FFF, FQ = DM // 128, DI // 128, DFF // 128, DQ // 128
NBK_DM = DM // 16

CONSTS = {
    "th0": 0.25, "th1": 0.75, "th2": 1.25, "th3": 1.75, "th4": 2.5,
    "th5": 3.5, "th6": 5.0, "two": 2.0, "four": 4.0, "half": 0.5,
    "eps12": 1e-12, "inv1024": 1.0 / 1024.0, "six": 6.0,
}
C_IDX = {k: i for i, k in enumerate(CONSTS)}
C_T6 = len(CONSTS)

ENGS = ("tensor", "vector", "scalar", "sync", "gpsimd")


SELF_SEM = {"vector": "sV", "scalar": "sS", "gpsimd": "sP"}


class Plan:
    def __init__(self):
        self.ev = {e: [] for e in ENGS}
        self.cnt = {s: 0 for s in (["sV", "sS", "sW", "sWv", "sW0", "sW1", "sG", "sO", "sTB", "sVB", "sSB", "sX", "sP", "sW2", "sQ0", "sQ1"] + [f"sT{i}" for i in range(8)])}
        self.barn = 0

    def op(self, eng, fn, inc=None, n=1):
        if eng in SELF_SEM and (inc is None or inc == SELF_SEM[eng]):
            s = SELF_SEM[eng]
            self.cnt[s] += 1
            self.ev[eng].append(("auto", fn))
            return self.cnt[s]
        if inc is None:
            self.ev[eng].append(("op", fn))
            return None
        self.cnt[inc] += n
        self.ev[eng].append(("inc", fn, inc, n))
        return self.cnt[inc]

    def wait(self, eng, sem, val):
        if val:
            self.ev[eng].append(("wait", sem, val))

    def waitm(self, eng, marker):
        if marker is not None:
            self.wait(eng, marker[0], marker[1])


def build_program(nchunks=8, dbg=()):
    nc = bass.Bass("TRN2", target_bir_lowering=False)
    dbg = set(dbg)

    x_h = nc.dram_tensor("x", [T, DM], F32, kind="ExternalInput").ap()
    w_in = nc.dram_tensor("in_proj_w", [DM, 2 * DI], F32, kind="ExternalInput").ap()
    conv_w = nc.dram_tensor("conv_w", [DI, KC], F32, kind="ExternalInput").ap()
    conv_b = nc.dram_tensor("conv_b", [DI], F32, kind="ExternalInput").ap()
    xproj_w = nc.dram_tensor("x_proj_w", [DI, R + 2 * DS], F32, kind="ExternalInput").ap()
    dtproj_w = nc.dram_tensor("dt_proj_w", [R, DI], F32, kind="ExternalInput").ap()
    dtproj_b = nc.dram_tensor("dt_proj_b", [DI], F32, kind="ExternalInput").ap()
    dtln_w = nc.dram_tensor("dt_ln_w", [R], F32, kind="ExternalInput").ap()
    bln_w = nc.dram_tensor("b_ln_w", [DS], F32, kind="ExternalInput").ap()
    cln_w = nc.dram_tensor("c_ln_w", [DS], F32, kind="ExternalInput").ap()
    alog = nc.dram_tensor("A_log", [DI, DS], F32, kind="ExternalInput").ap()
    dskip = nc.dram_tensor("D_skip", [DI], F32, kind="ExternalInput").ap()
    wout = nc.dram_tensor("out_proj_w", [DI, DM], F32, kind="ExternalInput").ap()
    inln_w = nc.dram_tensor("input_ln_w", [DM], F32, kind="ExternalInput").ap()
    pmln_w = nc.dram_tensor("pre_moe_ln_w", [DM], F32, kind="ExternalInput").ap()
    gate_w = nc.dram_tensor("gate_w", [DM, DFF], F32, kind="ExternalInput").ap()
    up_w = nc.dram_tensor("up_w", [DM, DFF], F32, kind="ExternalInput").ap()
    down_w = nc.dram_tensor("down_w", [DFF, DM], F32, kind="ExternalInput").ap()
    fg_w = nc.dram_tensor("ffn_w_gate", [DQ, DM], F32, kind="ExternalInput").ap()
    fu_w = nc.dram_tensor("ffn_w_up", [DQ, DM], F32, kind="ExternalInput").ap()
    fd_w = nc.dram_tensor("ffn_w_down", [DM, DQ], F32, kind="ExternalInput").ap()
    out_h = nc.dram_tensor("out", [T, DM], F32, kind="ExternalOutput").ap()
    wqtg = nc.dram_tensor("wqtg", [DM, DQ], F32, kind="Internal").ap()
    wqtu = nc.dram_tensor("wqtu", [DM, DQ], F32, kind="Internal").ap()
    wqtd = nc.dram_tensor("wqtd", [DQ, DM], F32, kind="Internal").ap()

    dbg_h = {}
    def dbg_tensor(name, shape):
        if name in dbg:
            dbg_h[name] = nc.dram_tensor("d_" + name, shape, F32, kind="ExternalOutput").ap()
    dbg_tensor("xn", [T, DM]); dbg_tensor("xq", [T, DM])
    dbg_tensor("xssm", [DI, T]); dbg_tensor("sz", [DI, T])
    dbg_tensor("dt", [DI, T]); dbg_tensor("bm", [DS, T]); dbg_tensor("cm", [DS, T])
    dbg_tensor("y", [DI, T]); dbg_tensor("hid", [DM, T])
    dbg_tensor("s", [DM, T]); dbg_tensor("g", [DM, T]); dbg_tensor("hq", [T, DQ])

    ident_h = nc.inline_tensor(np.eye(128, dtype=np.float32), name="identc")
    rs = np.zeros((16, 16 * 128), dtype=np.float32)
    for n_ in range(16):
        rs[n_, n_ * 128:(n_ + 1) * 128] = 1.0
    rowsel_h = nc.inline_tensor(rs, name="rowselc")

    P = Plan()

    with ExitStack() as ctx:
        E = ctx.enter_context

        cpool = E(nc.sbuf_tensor("cpool", [128, 32], F32))
        colv = E(nc.sbuf_tensor("colv", [128, 64], F32))
        convw = E(nc.sbuf_tensor("convw", [128, 64], F32))
        aneg = E(nc.sbuf_tensor("aneg", [128, 256], F32))
        sident = E(nc.sbuf_tensor("sident", [128, 128], F32))
        rowsel = E(nc.sbuf_tensor("rowsel", [16, 16 * 128], F32))
        onesrow = E(nc.sbuf_tensor("onesrow", [1, 128], F32))
        onescol = E(nc.sbuf_tensor("onescol", [128, 1], F32))
        smallv = E(nc.sbuf_tensor("smallv", [64, 4], F32))
        vstage = E(nc.sbuf_tensor("vstage", [16, 128], F32))
        x_t = E(nc.sbuf_tensor("x_t", [128, NTT * DM], F32))
        xnT = E(nc.sbuf_tensor("xnT", [128, FDM * LC], F32))
        xnW = E(nc.sbuf_tensor("xnW", [128, FDM * LC], F32))
        xqT = E(nc.sbuf_tensor("xqT", [128, FDM * LC], F32))
        wbuf = E(nc.sbuf_tensor("wbuf", [128, 6144], F32))
        SP = E(nc.sbuf_tensor("SP", [128, 5120], F32))
        rows = E(nc.sbuf_tensor("rows", [1, 1024], F32))
        projS = E(nc.sbuf_tensor("projS", [64, LC], F32))
        projB = E(nc.sbuf_tensor("projB", [16, LC], F32))
        projC = E(nc.sbuf_tensor("projC", [16, LC], F32))
        dtrms = E(nc.sbuf_tensor("dtrms", [64, LC], F32))
        BmT = E(nc.sbuf_tensor("BmT", [16, LC], F32))
        CmT = E(nc.sbuf_tensor("CmT", [16, LC], F32))
        r2bcS = E(nc.sbuf_tensor("r2bcS", [128, LC], F32))
        r3bcS = E(nc.sbuf_tensor("r3bcS", [128, LC], F32))
        hiddenT = E(nc.sbuf_tensor("hiddenT", [128, FDM * LC], F32))
        sT_ = E(nc.sbuf_tensor("sT_", [128, FDM * LC], F32))
        szb = E(nc.sbuf_tensor("szb", [128, FDI * LC], F32))
        dtT = E(nc.sbuf_tensor("dtT", [128, FDI * LC], F32))
        hstate = E(nc.sbuf_tensor("hstate", [128, 256], F32))
        tails = E(nc.sbuf_tensor("tails", [128, 48], F32))
        xpw = E(nc.sbuf_tensor("xpw", [128, 16 * 96], F32))
        AR = E(nc.sbuf_tensor("AR", [128, 12352], F32))

        pb = [E(nc.psum_tensor(f"pb{i}", [128, 512], F32)) for i in range(8)]

        sems = {}
        for s in (["sV", "sS", "sW", "sWv", "sW0", "sW1", "sW2", "sG", "sO", "sTB", "sVB", "sSB", "sX", "sP", "sQ0", "sQ1"] + [f"sT{i}" for i in range(8)]):
            sems[s] = E(nc.semaphore(s))
        block = E(nc.Block())

        bank_free = [None] * 8
        whalf_free = [None, None, None]
        whalf_ctr = [0]

        def c(name):
            return cpool[:, C_IDX[name]:C_IDX[name] + 1]

        def arena(o, n):
            return AR[:, o:o + n]

        def sp(o, n):
            return SP[:, o:o + n]

        def bar(engs=("tensor", "vector", "scalar")):
            tvals = {f"sT{i}": P.cnt[f"sT{i}"] for i in range(8)}
            tb = P.op("tensor", lambda eng: eng.nop(), inc="sTB")
            vv = P.op("vector", lambda eng: eng.nop())
            ss = P.op("scalar", lambda eng: eng.nop())
            pp = P.op("gpsimd", lambda eng: eng.nop())
            P.wait("tensor", "sV", vv)
            P.wait("tensor", "sS", ss)
            P.wait("tensor", "sP", pp)
            for e in ("vector", "scalar"):
                for s_, val in tvals.items():
                    P.wait(e, s_, val)
                P.wait(e, "sTB", tb)
            P.wait("vector", "sS", ss)
            P.wait("vector", "sP", pp)
            P.wait("scalar", "sV", vv)
            P.wait("scalar", "sP", pp)
            P.wait("gpsimd", "sV", vv)
            P.wait("gpsimd", "sS", ss)
            for s_, val in tvals.items():
                P.wait("gpsimd", s_, val)
            P.wait("gpsimd", "sTB", tb)

        def te_bank(b):
            P.waitm("tensor", bank_free[b])
            bank_free[b] = None

        def dma(eng, out, in_, sem="sW"):
            return P.op(eng, lambda e, o=out, i=in_: e.dma_start(out=o, in_=i), inc=sem, n=16)

        def dbg_guard():
            v = P.cnt["sO"]
            for e in ("vector", "scalar", "tensor"):
                P.wait(e, "sO", v)
            dbg_post()

        def dbg_pre():
            P.wait("gpsimd", "sV", P.cnt["sV"])
            P.wait("gpsimd", "sS", P.cnt["sS"])
            P.wait("gpsimd", "sP", P.cnt["sP"])

        def dbg_post():
            vv = P.op("vector", lambda v: v.engine_nop())
            P.wait("gpsimd", "sV", vv)

        def emit_quant(xs, w_, nblk, wide, small):
            av, acc, ac2, c7s, sg = wide
            bm, scl, rcp = small
            x3 = xs.rearrange("p (b k) -> p b k", k=16)
            P.op("vector", lambda v: v.tensor_reduce(
                out=bm, in_=x3, axis=AX.X, op=ALU.max, apply_absolute_value=True))
            P.op("vector", lambda v: v.engine_nop())
            P.op("vector", lambda v: v.tensor_scalar(
                out=scl, in0=bm, scalar1=cpool[:, C_T6:C_T6 + 1], scalar2=c("eps12"),
                op0=ALU.mult, op1=ALU.max))
            P.op("vector", lambda v: v.engine_nop())
            P.op("vector", lambda v: v.reciprocal(out=rcp, in_=scl))
            P.op("vector", lambda v: v.engine_nop())
            rbc = rcp.unsqueeze(2).broadcast_to([128, nblk, 16])
            sbc = scl.unsqueeze(2).broadcast_to([128, nblk, 16])
            vq = P.op("vector", lambda v: v.tensor_tensor(
                out=av.rearrange("p (b k) -> p b k", k=16), in0=x3, in1=rbc,
                op=ALU.mult), inc="sV")
            P.wait("scalar", "sV", vq)
            P.op("scalar", lambda s: s.activation(out=sg, in_=av, func=AF.Sign))
            s2 = P.op("scalar", lambda s: s.activation(out=av, in_=av, func=AF.Abs), inc="sS")
            P.wait("vector", "sS", s2)
            P.op("vector", lambda v: v.tensor_single_scalar(
                out=acc, in_=av, scalar=c("th0"), op=ALU.is_gt))
            for th in ("th1", "th2", "th3"):
                P.op("vector", lambda v, t=th: v.scalar_tensor_tensor(
                    out=acc, in0=av, scalar=c(t), in1=acc, op0=ALU.is_gt, op1=ALU.add))
            P.op("vector", lambda v: v.tensor_single_scalar(
                out=ac2, in_=av, scalar=c("th4"), op=ALU.is_gt))
            P.op("vector", lambda v: v.scalar_tensor_tensor(
                out=ac2, in0=av, scalar=c("th5"), in1=ac2, op0=ALU.is_gt, op1=ALU.add))
            P.op("vector", lambda v: v.tensor_single_scalar(
                out=c7s, in_=av, scalar=c("th6"), op=ALU.is_gt))
            P.op("vector", lambda v: v.scalar_tensor_tensor(
                out=acc, in0=ac2, scalar=c("two"), in1=acc, op0=ALU.mult, op1=ALU.add))
            P.op("vector", lambda v: v.scalar_tensor_tensor(
                out=acc, in0=c7s, scalar=c("four"), in1=acc, op0=ALU.mult, op1=ALU.add))
            P.op("vector", lambda v: v.tensor_tensor(out=acc, in0=acc, in1=sg, op=ALU.mult))
            return P.op("vector", lambda v: v.scalar_tensor_tensor(
                out=xs.rearrange("p (b k) -> p b k", k=16),
                in0=acc.rearrange("p (b k) -> p b k", k=16),
                scalar=c("half"), in1=sbc, op0=ALU.mult, op1=ALU.mult), inc="sV")

        def emit_quant_gp(xs, w_, nblk, wide, small, pre=()):
            av, acc, ac2, c7s, sg = wide
            bm, scl, rcp = small
            x3 = xs.rearrange("p (b k) -> p b k", k=16)
            for (s_, v_) in pre:
                P.wait("gpsimd", s_, v_)
                if s_ != "sV":
                    P.wait("vector", s_, v_)
            P.op("vector", lambda v: v.tensor_reduce(
                out=bm, in_=x3, axis=AX.X, op=ALU.max, apply_absolute_value=True))
            P.op("vector", lambda v: v.engine_nop())
            P.op("vector", lambda v: v.tensor_scalar(
                out=scl, in0=bm, scalar1=cpool[:, C_T6:C_T6 + 1], scalar2=c("eps12"),
                op0=ALU.mult, op1=ALU.max))
            P.op("vector", lambda v: v.engine_nop())
            v1 = P.op("vector", lambda v: v.reciprocal(out=rcp, in_=scl))
            rbc = rcp.unsqueeze(2).broadcast_to([128, nblk, 16])
            sbc = scl.unsqueeze(2).broadcast_to([128, nblk, 16])
            P.wait("gpsimd", "sV", v1)
            gq = P.op("gpsimd", lambda g: g.tensor_tensor(
                out=av.rearrange("p (b k) -> p b k", k=16), in0=x3, in1=rbc,
                op=ALU.mult))
            P.wait("scalar", "sP", gq)
            P.op("scalar", lambda s: s.activation(out=sg, in_=av, func=AF.Sign))
            s2 = P.op("scalar", lambda s: s.activation(out=av, in_=av, func=AF.Abs), inc="sS")
            P.wait("gpsimd", "sS", s2)
            P.op("gpsimd", lambda g: g.tensor_single_scalar(
                out=acc, in_=av, scalar=c("th0"), op=ALU.is_gt))
            for th in ("th1", "th2", "th3"):
                P.op("gpsimd", lambda g, t=th: g.scalar_tensor_tensor(
                    out=acc, in0=av, scalar=c(t), in1=acc, op0=ALU.is_gt, op1=ALU.add))
            P.op("gpsimd", lambda g: g.tensor_single_scalar(
                out=ac2, in_=av, scalar=c("th4"), op=ALU.is_gt))
            P.op("gpsimd", lambda g: g.scalar_tensor_tensor(
                out=ac2, in0=av, scalar=c("th5"), in1=ac2, op0=ALU.is_gt, op1=ALU.add))
            P.op("gpsimd", lambda g: g.tensor_single_scalar(
                out=c7s, in_=av, scalar=c("th6"), op=ALU.is_gt))
            P.op("gpsimd", lambda g: g.scalar_tensor_tensor(
                out=acc, in0=ac2, scalar=c("two"), in1=acc, op0=ALU.mult, op1=ALU.add))
            P.op("gpsimd", lambda g: g.scalar_tensor_tensor(
                out=acc, in0=c7s, scalar=c("four"), in1=acc, op0=ALU.mult, op1=ALU.add))
            P.op("gpsimd", lambda g: g.tensor_tensor(out=acc, in0=acc, in1=sg, op=ALU.mult))
            gend = P.op("gpsimd", lambda g: g.scalar_tensor_tensor(
                out=xs.rearrange("p (b k) -> p b k", k=16),
                in0=acc.rearrange("p (b k) -> p b k", k=16),
                scalar=c("half"), in1=sbc, op0=ALU.mult, op1=ALU.mult))
            return ("sP", gend)

        def init_consts(vector):
            for k, v in CONSTS.items():
                vector.memset(cpool[:, C_IDX[k]:C_IDX[k] + 1], float(v))
            vector.memset(onescol[:], 1.0)
            vector.memset(onesrow[:], 1.0)
            vector.memset(hstate[:], 0.0)
            vector.memset(tails[:], 0.0)
        P.op("vector", init_consts)
        P.op("vector", lambda v: v.engine_nop())
        P.op("vector", lambda v: v.reciprocal(out=cpool[:, C_T6:C_T6 + 1], in_=c("six")))
        P.op("vector", lambda v: v.engine_nop(), inc="sV")

        dma("sync", sident[:], ident_h.ap()[:])
        dma("sync", rowsel[:], rowsel_h.ap()[:])
        for dt_i in range(FDI):
            dma("sync", convw[:, dt_i * 4:(dt_i + 1) * 4],
                conv_w[dt_i * 128:(dt_i + 1) * 128, :])
        for dt_i in range(FDI):
            dma("sync", arena(dt_i * 16, 16), alog[dt_i * 128:(dt_i + 1) * 128, :])
        w_res = dma("sync", xpw[:].rearrange("p (k j) -> p k j", j=96),
                    xproj_w[:].rearrange("(k p) j -> p k j", p=128))
        P.wait("tensor", "sW", w_res)
        P.wait("scalar", "sW", w_res)
        for dt_i in range(FDI):
            P.op("scalar", lambda s, i=dt_i: s.activation(
                out=aneg[:, i * 16:(i + 1) * 16], in_=arena(i * 16, 16), func=AF.Exp))
        for dt_i in range(FDI):
            P.op("scalar", lambda s, i=dt_i: s.mul(
                out=aneg[:, i * 16:(i + 1) * 16], in_=aneg[:, i * 16:(i + 1) * 16],
                mul=np.float32(-1.0)))

        for src, n, off in ((inln_w, DM, 0), (pmln_w, DM, 8), (conv_b, DI, 16),
                            (dtproj_b, DI, 32), (dskip, DI, 48)):
            k = n // 128
            w = dma("sync", vstage[0:k, :], src.rearrange("(k p) -> k p", p=128),
                    sem="sWv")
            P.wait("tensor", "sWv", w)
            t = P.op("tensor", lambda e, kk=k: e.transpose(
                pb[6][0:128, 0:kk], vstage[0:kk, 0:128], sident[0:kk, 0:kk]), inc="sT6")
            P.wait("scalar", "sT6", t)
            sc = P.op("scalar", lambda s, kk=k, o=off: s.copy(
                out=colv[:, o:o + kk], in_=pb[6][0:128, 0:kk]), inc="sS")
            P.wait("tensor", "sS", sc)
            P.wait("sync", "sS", sc)
        for src, n, col in ((dtln_w, R, 0), (bln_w, DS, 1), (cln_w, DS, 2)):
            w = dma("sync", vstage[0:1, 0:n], src.unsqueeze(0), sem="sWv")
            P.wait("tensor", "sWv", w)
            t = P.op("tensor", lambda e, nn=n: e.transpose(
                pb[6][0:nn, 0:1], vstage[0:1, 0:nn], sident[0:1, 0:1]), inc="sT6")
            P.wait("scalar", "sT6", t)
            sc = P.op("scalar", lambda s, nn=n, cc=col: s.copy(
                out=smallv[0:nn, cc:cc + 1], in_=pb[6][0:nn, 0:1]), inc="sS")
            P.wait("tensor", "sS", sc)
            P.wait("sync", "sS", sc)

        QW_X, QW_AV, QW_ACC, QW_AC2, QW_C7, QW_SG = 0, 1024, 2048, 3072, 4096, 4096
        W_STAGE = 6144
        CP_STAGE = 768
        P.wait("vector", "sS", P.cnt["sS"])
        jobs = []
        for (w_src, w_dst, nrows, kdim) in (
            (fg_w, wqtg, DQ, DM), (fu_w, wqtu, DQ, DM), (fd_w, wqtd, DM, DQ),
        ):
            for rt in range(nrows // 128):
                jobs.append((w_src, w_dst, rt, kdim))
        stage_mark = [None, None]
        st_mark = None
        for ji, (w_src, w_dst, rt, kdim) in enumerate(jobs):
            stg = ji % 2
            soff = W_STAGE + stg * 2816
            P.waitm("sync", stage_mark[stg])
            wld = dma("sync", arena(soff, kdim), w_src[rt * 128:(rt + 1) * 128, :],
                      sem=f"sQ{stg}")
            P.wait("vector", f"sQ{stg}", wld)
            slw = 1024 if kdim % 1024 == 0 else 704
            vq = None
            for s0 in range(0, kdim, slw):
                nb = slw // 16
                vq = emit_quant(
                    arena(soff + s0, slw), slw, nb,
                    (sp(QW_AV, slw), sp(QW_ACC, slw), sp(QW_AC2, slw),
                     sp(QW_X, slw), sp(QW_SG, slw)),
                    (arena(0, nb), arena(256, nb), arena(512, nb)))
            P.wait("tensor", "sV", vq)
            P.waitm("scalar", st_mark)
            last_t = None
            for fi in range(kdim // 128):
                bk = 6 + (fi % 2)
                te_bank(bk)
                t = P.op("tensor", lambda e, f=fi, bb=bk, so=soff: e.transpose(
                    pb[bb][0:128, 0:128], arena(so + f * 128, 128),
                    sident[:]), inc=f"sT{bk}")
                P.wait("scalar", f"sT{bk}", t)
                sc = P.op("scalar", lambda s, f=fi, bb=bk: s.copy(
                    out=arena(CP_STAGE + f * 128, 128), in_=pb[bb][0:128, 0:128]),
                    inc="sS")
                bank_free[bk] = ("sS", sc)
                last_t = t
            stage_mark[stg] = (f"sT{6 + (kdim // 128 - 1) % 2}", last_t)
            P.wait("gpsimd", "sS", P.cnt["sS"])
            gd = P.op("gpsimd", lambda g, r=rt, d=w_dst, kd=kdim: g.dma_start(
                out=d[0:kd, r * 128:(r + 1) * 128]
                .rearrange("(f p) j -> p f j", p=128),
                in_=arena(CP_STAGE, kd).rearrange("p (f j) -> p f j", j=128)),
                inc="sG", n=16)
            st_mark = ("sG", gd)
        P.wait("vector", "sG", P.cnt["sG"])
        P.wait("scalar", "sG", P.cnt["sG"])
        bar()
        P.wait("vector", "sG", P.cnt["sG"])

        def stream_mm(weight_h, mcol, k_tiles, rhs_fn, bank, ncols=LC):
            nseg = (k_tiles + 15) // 16
            tm = None
            for seg in range(nseg):
                k0 = seg * ((k_tiles + nseg - 1) // nseg)
                k1 = min(k_tiles, k0 + (k_tiles + nseg - 1) // nseg)
                half = whalf_ctr[0] % 3
                whalf_ctr[0] += 1
                P.waitm("sync", whalf_free[half]); whalf_free[half] = None
                wgx = dma("sync", wbuf[:, half * 2048:half * 2048 + (k1 - k0) * 128]
                          .rearrange("p (k j) -> p k j", j=128),
                          weight_h[k0 * 128:k1 * 128, mcol * 128:(mcol + 1) * 128]
                          .rearrange("(k p) j -> p k j", p=128),
                          sem=f"sW{half}")
                P.wait("tensor", f"sW{half}", wgx)
                for k in range(k0, k1):
                    tm = P.op("tensor", lambda e, kk=k, hh=half, bb=bank, kz=k0,
                              nc_=ncols: e.matmul(
                        pb[bb][0:128, 0:nc_],
                        wbuf[:, hh * 2048 + (kk - kz) * 128:
                             hh * 2048 + (kk - kz + 1) * 128],
                        rhs_fn(kk), start=(kk == 0), stop=(kk == k_tiles - 1)),
                        inc=(f"sT{bank}" if k == k1 - 1 else None))
                whalf_free[half] = (f"sT{bank}", tm)
            return (f"sT{bank}", tm)

        def load_pair(weight_h, m0, k_tiles=8):
            slot = whalf_ctr[0] % 3
            whalf_ctr[0] += 1
            P.waitm("sync", whalf_free[slot]); whalf_free[slot] = None
            off = slot * 2048
            wg = dma("sync", wbuf[:, off:off + k_tiles * 256]
                     .rearrange("p (k j) -> p k j", j=256),
                     weight_h[0:k_tiles * 128, m0 * 128:(m0 + 2) * 128]
                     .rearrange("(k p) j -> p k j", p=128),
                     sem=f"sW{slot}")
            return (slot, off, f"sW{slot}", wg)

        def pair_mm(ld, sub, bank, rhs_fn, k_tiles=8, ncols=LC):
            slot, off, sem_, wg = ld
            P.wait("tensor", sem_, wg)
            tm = None
            for k in range(k_tiles):
                tm = P.op("tensor", lambda e, kk=k, bb=bank, o=off, ss=sub,
                          nc_=ncols: e.matmul(
                    pb[bb][0:128, 0:nc_],
                    wbuf[:, o + kk * 256 + ss * 128: o + kk * 256 + ss * 128 + 128],
                    rhs_fn(kk), start=(kk == 0), stop=(kk == k_tiles - 1)),
                    inc=(f"sT{bank}" if k == k_tiles - 1 else None))
            if sub == 1:
                whalf_free[slot] = (f"sT{bank}", tm)
            return (f"sT{bank}", tm)

        A_XC, A_XN, A_XQ = 0, 2048, 4096
        A_XPAD, A_XSSM, A_Y = 0, 4160, 8256
        A_PROD, A_HNT = 0, 8192
        A_PRODQ, A_HQT, A_HQTT, A_PR = 0, 5632, 0, 8192
        A_GT = 5632

        def xpad(d):
            return arena(A_XPAD + d * 259, 259)

        def xssm(d):
            return arena(A_XSSM + d * 256, 256)

        def dtx(d):
            return arena(A_XPAD + d * 256, 256)

        def ya(d):
            return arena(A_Y + d * 256, 256)

        def szt(d):
            return szb[:, d * LC:(d + 1) * LC]

        def dtTt(d):
            return dtT[:, d * LC:(d + 1) * LC]

        RW_A, RW_B = 0, 256
        RW_C, RW_D = 512, 768
        RW_E, RW_F = 512, 768

        last_combine = None

        for ch in range(nchunks):
            tok0 = ch * LC

            P.waitm("gpsimd", last_combine)
            P.wait("scalar", "sO", P.cnt["sO"])
            wx = None
            for tt in range(NTT):
                wx = dma("gpsimd", x_t[:, tt * DM:(tt + 1) * DM],
                         x_h[tok0 + tt * 128: tok0 + (tt + 1) * 128, :], sem="sX")
            P.wait("vector", "sX", wx)
            stat = r2bcS
            for tt in range(NTT):
                P.op("vector", lambda v, t_=tt: v.reduce_sum(
                    out=stat[:, t_:t_ + 1], in_=x_t[:, t_ * DM:(t_ + 1) * DM], axis=AX.X))
            P.op("vector", lambda v: v.engine_nop())
            for tt in range(NTT):
                P.op("vector", lambda v, t_=tt: v.tensor_single_scalar(
                    out=stat[:, 2 + t_:3 + t_], in_=stat[:, t_:t_ + 1],
                    scalar=c("inv1024"), op=ALU.mult))
            P.op("vector", lambda v: v.engine_nop())
            vml = None
            for tt in range(NTT):
                vml = P.op("vector", lambda v, t_=tt: v.tensor_scalar(
                    out=arena(A_XC + t_ * DM, DM), in0=x_t[:, t_ * DM:(t_ + 1) * DM],
                    scalar1=stat[:, 2 + t_:3 + t_], scalar2=None, op0=ALU.subtract), inc="sV")
            P.wait("scalar", "sV", vml)
            ssl = None
            for tt in range(NTT):
                ssl = P.op("scalar", lambda s, t_=tt: s.activation(
                    out=arena(A_XN + t_ * DM, DM), in_=arena(A_XC + t_ * DM, DM),
                    func=AF.Square, accum_out=stat[:, 4 + t_:5 + t_]), inc="sS")
            ssl = P.op("scalar", lambda s: s.activation(
                out=stat[:, 6:6 + NTT], in_=stat[:, 4:4 + NTT], func=AF.Sqrt,
                scale=np.float32(1.0 / DM), bias=np.float32(1e-5)), inc="sS")
            P.wait("vector", "sS", ssl)
            vr = P.op("vector", lambda v: v.reciprocal(
                out=stat[:, 8:8 + NTT], in_=stat[:, 6:6 + NTT]), inc="sV")
            P.wait("scalar", "sV", vr)
            scn = None
            for tt in range(NTT):
                scn = P.op("scalar", lambda s, t_=tt: s.activation(
                    out=arena(A_XN + t_ * DM, DM), in_=arena(A_XC + t_ * DM, DM),
                    func=AF.Copy, scale=stat[:, 8 + t_:9 + t_]), inc="sS")
            if "xn" in dbg:
                dbg_pre()
                P.wait("gpsimd", "sS", scn)
                for tt in range(NTT):
                    dma("gpsimd", dbg_h["xn"][tok0 + tt * 128: tok0 + (tt + 1) * 128, :],
                        arena(A_XN + tt * DM, DM), sem="sO")
            P.wait("tensor", "sS", scn)
            for (src_off, kind) in ((A_XN, "xn"),):
                for tt in range(NTT):
                    for fi in range(FDM):
                        b = 6 + ((tt * FDM + fi) % 2)
                        te_bank(b)
                        t = P.op("tensor", lambda e, o=src_off, t_=tt, f=fi, bb=b: e.transpose(
                            pb[bb][0:128, 0:128], arena(o + t_ * DM + f * 128, 128),
                            sident[:]), inc=f"sT{b}")
                        P.wait("scalar", f"sT{b}", t)
                        if kind == "xn":
                            P.op("scalar", lambda s, f=fi, t_=tt, bb=b: s.copy(
                                out=xnT[:, f * LC + t_ * 128: f * LC + (t_ + 1) * 128],
                                in_=pb[bb][0:128, 0:128]))
                            sc = P.op("scalar", lambda s, f=fi, t_=tt, bb=b: s.activation(
                                out=xnW[:, f * LC + t_ * 128: f * LC + (t_ + 1) * 128],
                                in_=pb[bb][0:128, 0:128], func=AF.Copy,
                                scale=colv[:, f:f + 1]), inc="sS")
                        else:
                            sc = P.op("scalar", lambda s, f=fi, t_=tt, bb=b: s.copy(
                                out=xqT[:, f * LC + t_ * 128: f * LC + (t_ + 1) * 128],
                                in_=pb[bb][0:128, 0:128]), inc="sS")
                        bank_free[b] = ("sS", sc)
            P.wait("vector", "sS", scn)
            vdq = None
            for tt in range(NTT):
                P.op("vector", lambda v, t_=tt: v.tensor_copy(
                    out=arena(A_XQ + t_ * DM, DM), in_=arena(A_XN + t_ * DM, DM)))
                P.op("vector", lambda v: v.engine_nop())
                vdq = emit_quant(
                    arena(A_XQ + tt * DM, DM), DM, NBK_DM,
                    (sp(QW_AV, DM), sp(QW_ACC, DM), sp(QW_AC2, DM),
                     sp(QW_X, DM), sp(QW_SG, DM)),
                    (arena(12352 - 256, NBK_DM), arena(12352 - 512, NBK_DM),
                     arena(12352 - 768, NBK_DM)))
            if "xq" in dbg:
                dbg_pre()
                P.wait("gpsimd", "sV", vdq)
                for tt in range(NTT):
                    dma("gpsimd", dbg_h["xq"][tok0 + tt * 128: tok0 + (tt + 1) * 128, :],
                        arena(A_XQ + tt * DM, DM), sem="sO")
            P.wait("tensor", "sV", vdq)
            for (src_off, kind) in ((A_XQ, "xq"),):
                for tt in range(NTT):
                    for fi in range(FDM):
                        b = 6 + ((tt * FDM + fi) % 2)
                        te_bank(b)
                        t = P.op("tensor", lambda e, o=src_off, t_=tt, f=fi, bb=b: e.transpose(
                            pb[bb][0:128, 0:128], arena(o + t_ * DM + f * 128, 128),
                            sident[:]), inc=f"sT{b}")
                        P.wait("scalar", f"sT{b}", t)
                        if kind == "xn":
                            P.op("scalar", lambda s, f=fi, t_=tt, bb=b: s.copy(
                                out=xnT[:, f * LC + t_ * 128: f * LC + (t_ + 1) * 128],
                                in_=pb[bb][0:128, 0:128]))
                            sc = P.op("scalar", lambda s, f=fi, t_=tt, bb=b: s.activation(
                                out=xnW[:, f * LC + t_ * 128: f * LC + (t_ + 1) * 128],
                                in_=pb[bb][0:128, 0:128], func=AF.Copy,
                                scale=colv[:, f:f + 1]), inc="sS")
                        else:
                            sc = P.op("scalar", lambda s, f=fi, t_=tt, bb=b: s.copy(
                                out=xqT[:, f * LC + t_ * 128: f * LC + (t_ + 1) * 128],
                                in_=pb[bb][0:128, 0:128]), inc="sS")
                        bank_free[b] = ("sS", sc)
            bar()
            dbg_guard()

            te_bank(5)
            slot_mark = {}
            for fi in range(FDM):
                sl = fi % 2
                P.waitm("scalar", slot_mark.get(sl))
                ssq = P.op("scalar", lambda s, f=fi, o=sl: s.activation(
                    out=sp(o * 256, LC), in_=xnT[:, f * LC:(f + 1) * LC],
                    func=AF.Square), inc="sS")
                P.wait("tensor", "sS", ssq)
                tm = P.op("tensor", lambda e, o=sl, f=fi: e.matmul(
                    pb[5][0:1, 0:LC], onescol[:, 0:1], sp(o * 256, LC),
                    start=(f == 0), stop=(f == FDM - 1)), inc="sT5")
                slot_mark[sl] = ("sT5", tm)
            P.wait("scalar", "sT5", P.cnt["sT5"])
            srq = P.op("scalar", lambda s: s.activation(
                out=rows[0:1, RW_A:RW_A + LC], in_=pb[5][0:1, 0:LC], func=AF.Sqrt,
                scale=np.float32(1.0 / DM), bias=np.float32(1e-6)), inc="sS")
            P.wait("vector", "sS", srq)
            vr2 = P.op("vector", lambda v: v.reciprocal(
                out=rows[0:1, RW_B:RW_B + LC], in_=rows[0:1, RW_A:RW_A + LC]), inc="sV")
            P.wait("tensor", "sV", vr2)
            tbc = P.op("tensor", lambda e: e.matmul(
                pb[5][0:128, 0:LC], onesrow[0:1, 0:128], rows[0:1, RW_B:RW_B + LC],
                start=True, stop=True), inc="sT5")
            P.wait("scalar", "sT5", tbc)
            sbc2 = P.op("scalar", lambda s: s.copy(out=r2bcS[:], in_=pb[5][0:128, 0:LC]), inc="sS")
            P.wait("tensor", "sS", sbc2)
            P.wait("vector", "sS", sbc2)

            zmark = {}
            wg = None
            ip_half = {}
            for m in range(2 * FDI):
                if m % 2 == 0:
                    ip_half[m // 2] = whalf_ctr[0] % 3
                    whalf_ctr[0] += 1
                half = ip_half[m // 2]
                if m % 2 == 0:
                    P.waitm("sync", whalf_free[half]); whalf_free[half] = None
                    g = m // 2
                    wg = dma("sync", wbuf[:, half * 2048:half * 2048 + 2048]
                             .rearrange("p (k j) -> p k j", j=256),
                             w_in[:, g * 256:(g + 1) * 256]
                             .rearrange("(k p) j -> p k j", p=128),
                             sem=f"sW{half}")
                b = m % 4
                te_bank(b)
                P.wait("tensor", f"sW{half}", wg)
                tm = None
                for k in range(FDM):
                    tm = P.op("tensor", lambda e, mm=m, kk=k, hh=half, bb=b: e.matmul(
                        pb[bb][0:128, 0:LC],
                        wbuf[:, hh * 2048 + kk * 256 + (mm % 2) * 128:
                             hh * 2048 + kk * 256 + (mm % 2) * 128 + 128],
                        xnW[:, kk * LC:(kk + 1) * LC],
                        start=(kk == 0), stop=(kk == FDM - 1)),
                        inc=(f"sT{b}" if k == FDM - 1 else None))
                if m % 2 == 1:
                    whalf_free[half] = (f"sT{b}", tm)
                P.wait("vector", f"sT{b}", tm)
                if m < FDI:
                    vv = P.op("vector", lambda v, mm=m, bb=b: v.tensor_tensor(
                        out=xpad(mm)[:, 3:3 + LC], in0=pb[bb][0:128, 0:LC],
                        in1=r2bcS[:], op=ALU.mult), inc="sV")
                    bank_free[b] = ("sV", vv)
                else:
                    d = m - FDI
                    sl = 512 + (d % 2) * 256
                    P.waitm("vector", zmark.get(d % 2))
                    vv = P.op("vector", lambda v, bb=b, o=sl: v.tensor_tensor(
                        out=sp(o, LC), in0=pb[bb][0:128, 0:LC], in1=r2bcS[:],
                        op=ALU.mult), inc="sV")
                    bank_free[b] = ("sV", vv)
                    P.wait("scalar", "sV", vv)
                    scz = P.op("scalar", lambda s, dd=d, o=sl: s.activation(
                        out=szt(dd), in_=sp(o, LC), func=AF.Silu), inc="sS")
                    zmark[d % 2] = ("sS", scz)
            bar()

            for d in range(FDI):
                P.op("vector", lambda v, dd=d: v.tensor_copy(
                    out=xpad(dd)[:, 0:3], in_=tails[:, dd * 3:(dd + 1) * 3]))
            vtl = P.op("vector", lambda v: v.engine_nop(), inc="sV")
            P.wait("scalar", "sV", vtl)
            conv_marks = [None] * FDI
            for d in range(FDI):
                sl = (d % 2) * 256
                sm = P.op("scalar", lambda s, dd=d, o=sl: s.activation(
                    out=sp(o, LC), in_=xpad(dd)[:, 0:LC], func=AF.Copy,
                    scale=convw[:, dd * 4:dd * 4 + 1]), inc="sS")
                P.wait("vector", "sS", sm)
                for j in range(1, 4):
                    P.op("vector", lambda v, dd=d, jj=j, o=sl: v.scalar_tensor_tensor(
                        out=sp(o, LC), in0=xpad(dd)[:, jj:jj + LC],
                        scalar=convw[:, dd * 4 + jj:dd * 4 + jj + 1], in1=sp(o, LC),
                        op0=ALU.mult, op1=ALU.add))
                P.op("vector", lambda v, dd=d: v.tensor_copy(
                    out=tails[:, dd * 3:(dd + 1) * 3], in_=xpad(dd)[:, LC:LC + 3]))
                vcv = P.op("vector", lambda v: v.engine_nop(), inc="sV")
                P.wait("scalar", "sV", vcv)
                conv_marks[d] = P.op("scalar", lambda s, dd=d, o=sl: s.activation(
                    out=xssm(dd), in_=sp(o, LC), func=AF.Silu,
                    bias=colv[:, 16 + dd:17 + dd]), inc="sS")
            if dbg:
                bar()
            if "xssm" in dbg:
                dbg_pre()
                for d in range(FDI):
                    dma("gpsimd", dbg_h["xssm"][d * 128:(d + 1) * 128, tok0:tok0 + LC],
                        xssm(d), sem="sO")
            if "sz" in dbg:
                dbg_pre()
                for d in range(FDI):
                    dma("gpsimd", dbg_h["sz"][d * 128:(d + 1) * 128, tok0:tok0 + LC],
                        szt(d), sem="sO")
            dbg_guard()

            te_bank(4)
            te_bank(5)
            te_bank(6)
            tpj = None
            for k in range(FDI):
                P.wait("tensor", "sS", conv_marks[k])
                P.op("tensor", lambda e, kk=k: e.matmul(
                    pb[4][0:64, 0:LC], xpw[:, kk * 96:kk * 96 + 64], xssm(kk),
                    start=(kk == 0), stop=(kk == FDI - 1)))
                P.op("tensor", lambda e, kk=k: e.matmul(
                    pb[5][0:16, 0:LC], xpw[:, kk * 96 + 64:kk * 96 + 80], xssm(kk),
                    start=(kk == 0), stop=(kk == FDI - 1)))
                tpj = P.op("tensor", lambda e, kk=k: e.matmul(
                    pb[6][0:16, 0:LC], xpw[:, kk * 96 + 80:kk * 96 + 96], xssm(kk),
                    start=(kk == 0), stop=(kk == FDI - 1)),
                    inc=("sT4" if k == FDI - 1 else None))
            P.wait("vector", "sT4", tpj)
            P.op("vector", lambda v: v.tensor_copy(out=projS[:], in_=pb[4][0:64, 0:LC]))
            P.op("vector", lambda v: v.tensor_copy(out=projB[:], in_=pb[5][0:16, 0:LC]))
            vpj = P.op("vector", lambda v: v.tensor_copy(
                out=projC[:], in_=pb[6][0:16, 0:LC]), inc="sV")
            P.wait("scalar", "sV", vpj)
            s64 = P.op("scalar", lambda s: s.activation(
                out=sp(0, LC)[0:64, :], in_=projS[0:64, 0:LC], func=AF.Square), inc="sS")
            P.wait("tensor", "sS", s64)
            t64 = P.op("tensor", lambda e: e.matmul(
                pb[5][0:1, 0:LC], onescol[0:64, 0:1], sp(0, LC)[0:64, :],
                start=True, stop=True), inc="sT5")
            P.wait("scalar", "sT5", t64)
            s64b = P.op("scalar", lambda s: s.activation(
                out=rows[0:1, RW_A:RW_A + LC], in_=pb[5][0:1, 0:LC], func=AF.Sqrt,
                scale=np.float32(1.0 / R), bias=np.float32(1e-6)), inc="sS")
            P.wait("vector", "sS", s64b)
            v64 = P.op("vector", lambda v: v.reciprocal(
                out=rows[0:1, RW_B:RW_B + LC], in_=rows[0:1, RW_A:RW_A + LC]), inc="sV")
            P.wait("tensor", "sV", v64)
            t64b = P.op("tensor", lambda e: e.matmul(
                pb[5][0:64, 0:LC], onesrow[0:1, 0:64], rows[0:1, RW_B:RW_B + LC],
                start=True, stop=True), inc="sT5")
            P.wait("vector", "sT5", t64b)
            vdt = P.op("vector", lambda v: v.tensor_tensor(
                out=dtrms[:], in0=projS[0:64, 0:LC], in1=pb[5][0:64, 0:LC],
                op=ALU.mult), inc="sV")
            for (psrc, colid, dst, rw1, rw2) in ((projB, 1, BmT, RW_C, RW_D),
                                                 (projC, 2, CmT, RW_E, RW_F)):
                sB = P.op("scalar", lambda s, pp=psrc: s.activation(
                    out=sp(256, LC)[0:16, :], in_=pp[0:16, 0:LC], func=AF.Square), inc="sS")
                P.wait("tensor", "sS", sB)
                te_bank(6)
                tB = P.op("tensor", lambda e: e.matmul(
                    pb[6][0:1, 0:LC], onescol[0:16, 0:1], sp(256, LC)[0:16, :],
                    start=True, stop=True), inc="sT6")
                P.wait("scalar", "sT6", tB)
                sB2 = P.op("scalar", lambda s, w=rw1: s.activation(
                    out=rows[0:1, w:w + LC], in_=pb[6][0:1, 0:LC], func=AF.Sqrt,
                    scale=np.float32(1.0 / DS), bias=np.float32(1e-6)), inc="sS")
                P.wait("vector", "sS", sB2)
                vB = P.op("vector", lambda v, w1=rw1, w2=rw2: v.reciprocal(
                    out=rows[0:1, w2:w2 + LC], in_=rows[0:1, w1:w1 + LC]), inc="sV")
                P.wait("tensor", "sV", vB)
                tB2 = P.op("tensor", lambda e, w2=rw2: e.matmul(
                    pb[6][0:16, 0:LC], onesrow[0:1, 0:16], rows[0:1, w2:w2 + LC],
                    start=True, stop=True), inc="sT6")
                P.wait("vector", "sT6", tB2)
                vB2 = P.op("vector", lambda v, pp=psrc, cc=colid, dd=dst: v.scalar_tensor_tensor(
                    out=dd[:], in0=pp[0:16, 0:LC], scalar=smallv[0:16, cc:cc + 1],
                    in1=pb[6][0:16, 0:LC], op0=ALU.mult, op1=ALU.mult), inc="sV")
                bank_free[6] = ("sV", vB2)
            bar()
            if "bm" in dbg:
                dbg_pre()
                dma("gpsimd", dbg_h["bm"][0:16, tok0:tok0 + LC], BmT[:], sem="sO")
            if "cm" in dbg:
                dbg_pre()
                dma("gpsimd", dbg_h["cm"][0:16, tok0:tok0 + LC], CmT[:], sem="sO")
            dbg_guard()

            dslot = whalf_ctr[0] % 3
            whalf_ctr[0] += 1
            P.waitm("sync", whalf_free[dslot]); whalf_free[dslot] = None
            wdt = dma("sync", wbuf[0:64, dslot * 2048:(dslot + 1) * 2048]
                      .rearrange("p (k j) -> p k j", j=128),
                      dtproj_w[:].rearrange("p (k j) -> p k j", j=128),
                      sem=f"sW{dslot}")
            P.wait("scalar", f"sW{dslot}", wdt)
            sfold = P.op("scalar", lambda s, o=dslot * 2048: s.activation(
                out=wbuf[0:64, o:o + 2048], in_=wbuf[0:64, o:o + 2048],
                func=AF.Copy, scale=smallv[0:64, 0:1]), inc="sS")
            P.wait("tensor", "sS", sfold)
            vdt2_marks = {}
            for m in range(FDI):
                b = m % 4
                te_bank(b)
                tdm = P.op("tensor", lambda e, mm=m, bb=b, o=dslot * 2048: e.matmul(
                    pb[bb][0:128, 0:LC], wbuf[0:64, o + mm * 128:o + (mm + 1) * 128],
                    dtrms[0:64, 0:LC], start=True, stop=True), inc=f"sT{b}")
                P.wait("scalar", f"sT{b}", tdm)
                sl = (m % 2) * 1024
                P.waitm("scalar", vdt2_marks.get(m % 2))
                P.op("scalar", lambda s, bb=b, mm=m, o=sl: s.activation(
                    out=sp(o, LC), in_=pb[bb][0:128, 0:LC], func=AF.Relu,
                    bias=colv[:, 32 + mm:33 + mm]))
                sab = P.op("scalar", lambda s, bb=b, mm=m, o=sl: s.activation(
                    out=sp(o + 256, LC), in_=pb[bb][0:128, 0:LC], func=AF.Abs,
                    bias=colv[:, 32 + mm:33 + mm]), inc="sS")
                bank_free[b] = ("sS", sab)
                P.op("scalar", lambda s, o=sl: s.activation(
                    out=sp(o + 512, LC), in_=sp(o + 256, LC), func=AF.Exp,
                    scale=np.float32(-1.0)))
                sln = P.op("scalar", lambda s, o=sl: s.activation(
                    out=sp(o + 768, LC), in_=sp(o + 512, LC), func=AF.Ln,
                    bias=np.float32(1.0)), inc="sS")
                P.wait("vector", "sS", sln)
                vdt2 = P.op("vector", lambda v, mm=m, o=sl: v.tensor_tensor(
                    out=dtTt(mm), in0=sp(o, LC), in1=sp(o + 768, LC), op=ALU.add), inc="sV")
                vdt2_marks[m % 2] = ("sV", vdt2)
                if m == FDI - 1:
                    whalf_free[dslot] = (f"sT{b}", tdm)
            bar()
            if "dt" in dbg:
                dbg_pre()
                for d in range(FDI):
                    dma("gpsimd", dbg_h["dt"][d * 128:(d + 1) * 128, tok0:tok0 + LC],
                        dtTt(d), sem="sO")
            dbg_guard()
            for d in range(FDI):
                P.op("scalar", lambda s, dd=d: s.activation(
                    out=ya(dd), in_=xssm(dd), func=AF.Copy, scale=colv[:, 48 + dd:49 + dd]))
            for d in range(FDI):
                P.op("vector", lambda v, dd=d: v.tensor_tensor(
                    out=dtx(dd), in0=dtTt(dd), in1=xssm(dd), op=ALU.mult))
            bar()

            def pq0(i):
                return hiddenT[:, i * 128:(i + 1) * 128] if i < 16 else \
                    sT_[:, (i - 16) * 128:(i - 15) * 128]

            def pq1(i):
                return arena(4160 + i * 128, 128)

            def hq1t(i):
                return sT_[:, i * 128:(i + 1) * 128] if i < 16 else \
                    dtT[:, (i - 16) * 128:(i - 15) * 128]

            def hqt(i):
                return xnW[:, i * 128:(i + 1) * 128] if i < 16 else \
                    arena(6976 + (i - 16) * 128, 128)

            def hq_home(off, n_):
                if off < 2048:
                    return xnW[:, off:off + n_]
                return arena(6976 + (off - 2048), n_)

            HQ_SLICES = ((0, 1024), (1024, 1024), (2048, 768))

            F_DA, F_U, F_H, F_P, F_CBS = 0, 1028, 2056, 3084, 4108
            for buf in range(2):
                P.op("vector", lambda v, b_=buf: v.memset(
                    SP[:, F_DA + b_ * 514:F_DA + b_ * 514 + 514:257], 0.0))
            scan_marks = [None, None]
            ymark = [None, None]
            pmark = [None, None]
            cbs_mark = [None, None]
            hs_mark = [None, None]
            hs_pend = None
            P.wait("gpsimd", "sS", P.cnt["sS"])

            gsm_q = {}
            fp4_steps = []
            fp4_state = {}

            def fp4_make_steps(mp):
                def s_load():
                    fp4_state[(mp, "ldg")] = load_pair(wqtg, mp)
                    fp4_state[(mp, "ldu")] = load_pair(wqtu, mp)

                def s_gate(sub):
                    m = mp + sub
                    gs = r2bcS if m % 2 == 0 else r3bcS
                    b = m % 2
                    te_bank(b)
                    tm = pair_mm(fp4_state[(mp, "ldg")], sub, b,
                                 lambda k: xqT[:, k * LC:(k + 1) * LC])
                    P.waitm("scalar", tm)
                    P.waitm("scalar", gsm_q.get(m % 2))
                    sg_ = P.op("scalar", lambda s, bb=b, g_=gs: s.activation(
                        out=g_[:], in_=pb[bb][0:128, 0:LC], func=AF.Silu), inc="sS")
                    bank_free[b] = ("sS", sg_)
                    fp4_state[(m, "sg")] = sg_

                def s_up(sub):
                    m = mp + sub
                    b2 = 2 + (m % 2)
                    te_bank(b2)
                    fp4_state[(m, "tm2")] = pair_mm(
                        fp4_state[(mp, "ldu")], sub, b2,
                        lambda k: xqT[:, k * LC:(k + 1) * LC])

                def s_prod(sub):
                    m = mp + sub
                    gs = r2bcS if m % 2 == 0 else r3bcS
                    b2 = 2 + (m % 2)
                    P.waitm("vector", fp4_state[(m, "tm2")])
                    P.wait("vector", "sS", fp4_state[(m, "sg")])
                    P.op("vector", lambda v, bb=b2, g_=gs, mm=m: v.tensor_tensor(
                        out=pq0(mm), in0=g_[:, 0:128],
                        in1=pb[bb][0:128, 0:128], op=ALU.mult))
                    vp1 = P.op("vector", lambda v, bb=b2, g_=gs, mm=m: v.tensor_tensor(
                        out=pq1(mm), in0=g_[:, 128:256],
                        in1=pb[bb][0:128, 128:256], op=ALU.mult))
                    bank_free[b2] = ("sV", vp1)
                    gsm_q[m % 2] = ("sV", vp1)
                    fp4_state["vp_last"] = vp1

                fp4_steps.append(s_load)
                fp4_steps.append(lambda: s_gate(0))
                fp4_steps.append(lambda: (s_up(0), s_prod(0)))
                fp4_steps.append(lambda: s_gate(1))
                fp4_steps.append(lambda: (s_up(1), s_prod(1)))

            for mp_ in range(0, FQ, 2):
                fp4_make_steps(mp_)
            fp4_sched = {}
            tot_slots = 14 * 8
            for si in range(len(fp4_steps)):
                fp4_sched.setdefault(si * tot_slots // len(fp4_steps), []).append(si)

            tr0_last = None
            tr1_last = None
            for n in range(DS):
                bp = 4 + (n % 2)
                te_bank(bp)
                tbm = P.op("tensor", lambda e, nn=n, bb=bp: e.matmul(
                    pb[bb][0:128, 0:LC], rowsel[:, nn * 128:(nn + 1) * 128],
                    BmT[0:16, 0:LC], start=True, stop=True), inc=f"sT{bp}")
                te_bank(6)
                tc = P.op("tensor", lambda e, nn=n: e.matmul(
                    pb[6][0:128, 0:LC], rowsel[:, nn * 128:(nn + 1) * 128],
                    CmT[0:16, 0:LC], start=True, stop=True), inc="sT6")
                P.wait("vector", f"sT{bp}", tbm)
                P.wait("scalar", "sT6", tc)
                P.waitm("scalar", cbs_mark[n % 2])
                scb = P.op("scalar", lambda s, nn=n: s.copy(
                    out=sp(F_CBS + (nn % 2) * 256, LC), in_=pb[6][0:128, 0:LC]))
                bank_free[6] = ("sS", scb)
                ulast = None
                for j in range(FDI // 2):
                    pj = n * 8 + j
                    buf = pj % 2
                    dab = sp(F_DA + buf * 514, 514)
                    ub = sp(F_U + buf * 514, 514)
                    hb = sp(F_H + buf * 514, 514)
                    pbuf = sp(F_P + buf * 512, 512)
                    P.waitm("scalar", scan_marks[buf])
                    P.op("scalar", lambda s, a=ub, nn=n, jj=j: s.copy(
                        out=a[:, 0:514:257],
                        in_=hstate[:, nn * FDI + 2 * jj:nn * FDI + 2 * jj + 2]))
                    for hh in range(2):
                        d = 2 * j + hh
                        sda = P.op("scalar", lambda s, dd=d, nn=n, a=dab, o=1 + 257 * hh:
                                   s.activation(
                                       out=a[:, o:o + LC], in_=dtTt(dd), func=AF.Exp,
                                       scale=aneg[:, dd * 16 + nn:dd * 16 + nn + 1]), inc="sS")
                    if hs_pend is not None:
                        hs_pend()
                        hs_pend = None
                    ulast = P.op("vector", lambda v, a=ub, jj=j, bb=bp: v.tensor_tensor(
                        out=a.rearrange("p (two l) -> p two l", l=257)[:, :, 1:257],
                        in0=arena(A_XPAD + 2 * jj * 256, 512).rearrange(
                            "p (two l) -> p two l", l=256),
                        in1=pb[bb][0:128, 0:LC].unsqueeze(1).broadcast_to([128, 2, LC]),
                        op=ALU.mult))
                    P.wait("vector", "sS", sda)
                    P.waitm("vector", pmark[buf])
                    P.waitm("vector", hs_mark[buf])
                    sc_ = P.op("vector", lambda v, a=hb, d0=dab, d1=ub: v.tensor_tensor_scan(
                        out=a, data0=d0, data1=d1, initial=0.0,
                        op0=ALU.mult, op1=ALU.add))
                    scan_marks[buf] = ("sV", sc_)

                    def _mk_hsave(a=hb, nn=n, jj=j, sv=sc_, bf=buf):
                        def emit():
                            P.wait("scalar", "sV", sv)
                            hs_ = P.op("scalar", lambda s: s.copy(
                                out=hstate[:, nn * FDI + 2 * jj:
                                           nn * FDI + 2 * jj + 2],
                                in_=a[:, 256:514:257]))
                            hs_mark[bf] = ("sS", hs_)
                        return emit
                    hs_pend = _mk_hsave()
                    P.wait("gpsimd", "sV", sc_)
                    P.wait("gpsimd", "sS", scb)
                    P.waitm("gpsimd", ymark[buf])
                    pg = P.op("gpsimd", lambda g, a=pbuf, hh_=hb, nn=n: g.tensor_tensor(
                        out=a.rearrange("p (two l) -> p two l", l=256),
                        in0=hh_.rearrange("p (two l) -> p two l", l=257)[:, :, 1:257],
                        in1=sp(F_CBS + (nn % 2) * 256, LC).unsqueeze(1).broadcast_to(
                            [128, 2, LC]),
                        op=ALU.mult))
                    pmark[buf] = ("sP", pg)
                    if j % 2 == 0:
                        yv = P.op("gpsimd", lambda g, a=pbuf, jj=j: g.tensor_tensor(
                            out=arena(A_Y + 2 * jj * 256, 512),
                            in0=arena(A_Y + 2 * jj * 256, 512), in1=a, op=ALU.add))
                        ymark[buf] = ("sP", yv)
                    else:
                        P.wait("vector", "sP", pg)
                        yv = P.op("vector", lambda v, a=pbuf, jj=j: v.tensor_tensor(
                            out=arena(A_Y + 2 * jj * 256, 512),
                            in0=arena(A_Y + 2 * jj * 256, 512), in1=a, op=ALU.add))
                        ymark[buf] = ("sV", yv)
                    for si in fp4_sched.get(n * 8 + j, ()):
                        fp4_steps[si]()
                bank_free[bp] = ("sV", ulast)
                cbs_mark[n % 2] = pmark[(n * 8 + 7) % 2]
                if n == 15:
                    if hs_pend is not None:
                        hs_pend()
                        hs_pend = None
                    for i in range(FQ):
                        bt = 7 if i % 2 == 0 else 5
                        te_bank(bt)
                        t = P.op("tensor", lambda e, ii=i, bb=bt: e.transpose(
                            pb[bb][0:128, 0:128], pq1(ii), sident[:]), inc=f"sT{bt}")
                        P.wait("scalar", f"sT{bt}", t)
                        sc = P.op("scalar", lambda s, ii=i, bb=bt: s.copy(
                            out=hq1t(ii), in_=pb[bb][0:128, 0:128]), inc="sS")
                        bank_free[bt] = ("sS", sc)
                        tr1_last = ("sS", sc)
                if n == 14:
                    P.wait("tensor", "sV", fp4_state["vp_last"])
                    for i in range(FQ):
                        bt = 7 if i % 2 == 0 else 4
                        te_bank(bt)
                        t = P.op("tensor", lambda e, ii=i, bb=bt: e.transpose(
                            pb[bb][0:128, 0:128], pq0(ii), sident[:]), inc=f"sT{bt}")
                        P.wait("scalar", f"sT{bt}", t)
                        sc = P.op("scalar", lambda s, ii=i, bb=bt: s.copy(
                            out=hqt(ii), in_=pb[bb][0:128, 0:128]), inc="sS")
                        bank_free[bt] = ("sS", sc)
                        tr0_last = ("sS", sc)

            if hs_pend is not None:
                hs_pend()
                hs_pend = None

            def hqT0(i):
                return arena(768 + i * 128, 128)
            if "y" in dbg:
                dbg_pre()
                for d in range(FDI):
                    dma("gpsimd", dbg_h["y"][d * 128:(d + 1) * 128, tok0:tok0 + LC],
                        ya(d), sem="sO")
                dbg_guard()
            P.wait("vector", "sP", P.cnt["sP"])
            vg = None
            for d in range(FDI):
                vg = P.op("vector", lambda v, dd=d: v.tensor_tensor(
                    out=ya(dd), in0=ya(dd), in1=szt(dd), op=ALU.mult), inc="sV")
            P.wait("tensor", "sV", vg)
            def hq1_home(off, w_):
                return sT_[:, off:off + w_] if off < 2048 else \
                    dtT[:, off - 2048:off - 2048 + w_]
            gq1 = None
            P.wait("vector", "sS", tr1_last[1])
            for (off, w_) in HQ_SLICES:
                nb = w_ // 16
                gq1 = ("sV", emit_quant(
                    hq1_home(off, w_), w_, nb,
                    (sp(QW_AV, w_), sp(QW_ACC, w_), sp(QW_AC2, w_),
                     sp(QW_X, w_), sp(QW_SG, w_)),
                    (arena(0, nb), arena(256, nb), arena(512, nb))))
            for mo in range(FDM):
                half = whalf_ctr[0] % 3
                whalf_ctr[0] += 1
                P.waitm("sync", whalf_free[half]); whalf_free[half] = None
                wg = dma("sync", wbuf[:, half * 2048:half * 2048 + 2048]
                         .rearrange("p (k j) -> p k j", j=128),
                         wout[:, mo * 128:(mo + 1) * 128]
                         .rearrange("(k p) j -> p k j", p=128),
                         sem=f"sW{half}")
                b = mo % 7
                te_bank(b)
                P.wait("tensor", f"sW{half}", wg)
                tm = None
                for k in range(FDI):
                    tm = P.op("tensor", lambda e, kk=k, hh=half, bb=b: e.matmul(
                        pb[bb][0:128, 0:LC],
                        wbuf[:, hh * 2048 + kk * 128: hh * 2048 + (kk + 1) * 128],
                        ya(kk), start=(kk == 0), stop=(kk == FDI - 1)),
                        inc=(f"sT{b}" if k == FDI - 1 else None))
                whalf_free[half] = (f"sT{b}", tm)
                P.wait("vector", f"sT{b}", tm)
                vh = P.op("vector", lambda v, m=mo, bb=b: v.tensor_tensor(
                    out=hiddenT[:, m * LC:(m + 1) * LC], in0=pb[bb][0:128, 0:LC],
                    in1=xnT[:, m * LC:(m + 1) * LC], op=ALU.add), inc="sV")
                bank_free[b] = ("sV", vh)
            vq0 = None
            for (off, w_) in HQ_SLICES:
                nb = w_ // 16
                vq0 = emit_quant(
                    hq_home(off, w_), w_, nb,
                    (sp(QW_AV, w_), sp(QW_ACC, w_), sp(QW_AC2, w_),
                     sp(QW_X, w_), sp(QW_SG, w_)),
                    (arena(0, nb), arena(256, nb), arena(512, nb)))
            P.waitm("tensor", gq1)
            for i in range(FQ):
                bt = 7 if i % 2 == 0 else 5
                te_bank(bt)
                t = P.op("tensor", lambda e, ii=i, bb=bt: e.transpose(
                    pb[bb][0:128, 0:128], hq1t(ii), sident[:]), inc=f"sT{bt}")
                P.wait("scalar", f"sT{bt}", t)
                sc = P.op("scalar", lambda s, ii=i, bb=bt: s.copy(
                    out=pq1(ii), in_=pb[bb][0:128, 0:128]), inc="sS")
                bank_free[bt] = ("sS", sc)
            P.wait("tensor", "sV", vq0)
            for i in range(FQ):
                bt = 7 if i % 2 == 0 else 5
                te_bank(bt)
                t = P.op("tensor", lambda e, ii=i, bb=bt: e.transpose(
                    pb[bb][0:128, 0:128], hqt(ii), sident[:]), inc=f"sT{bt}")
                P.wait("scalar", f"sT{bt}", t)
                sc = P.op("scalar", lambda s, ii=i, bb=bt: s.copy(
                    out=hqT0(ii), in_=pb[bb][0:128, 0:128]), inc="sS")
                bank_free[bt] = ("sS", sc)
            P.wait("tensor", "sS", P.cnt["sS"])
            for mo in range(FDM):
                b = 4 + (mo % 3)
                te_bank(b)
                tmv = None
                for seg in range(2):
                    k0, k1 = seg * 11, seg * 11 + 11
                    slot = whalf_ctr[0] % 3
                    whalf_ctr[0] += 1
                    P.waitm("sync", whalf_free[slot]); whalf_free[slot] = None
                    so = slot * 2048
                    wgx = dma("sync", wbuf[:, so:so + 11 * 128]
                              .rearrange("p (k j) -> p k j", j=128),
                              wqtd[k0 * 128:k1 * 128, mo * 128:(mo + 1) * 128]
                              .rearrange("(k p) j -> p k j", p=128),
                              sem=f"sW{slot}")
                    P.wait("tensor", f"sW{slot}", wgx)
                    for k in range(k0, k1):
                        P.op("tensor", lambda e, kk=k, o=so, kz=k0, bb=b: e.matmul(
                            pb[bb][0:128, 0:128],
                            wbuf[:, o + (kk - kz) * 128: o + (kk - kz + 1) * 128],
                            hqT0(kk), start=(kk == 0), stop=False))
                        tmv = P.op("tensor", lambda e, kk=k, o=so, kz=k0, bb=b: e.matmul(
                            pb[bb][0:128, 128:256],
                            wbuf[:, o + (kk - kz) * 128: o + (kk - kz + 1) * 128],
                            pq1(kk), start=False, stop=(kk == FQ - 1)),
                            inc=(f"sT{b}" if k == k1 - 1 else None))
                    whalf_free[slot] = (f"sT{b}", tmv)
                P.waitm("scalar", (f"sT{b}", tmv))
                sg_ = P.op("scalar", lambda s, m=mo, bb=b: s.activation(
                    out=dtT[:, 1024 + m * 256:1024 + (m + 1) * 256],
                    in_=pb[bb][0:128, 0:256], func=AF.Silu), inc="sS")
                bank_free[b] = ("sS", sg_)
            bar()
            if "g" in dbg:
                dbg_pre()
                for mo in range(FDM):
                    for hh in range(2):
                        dma("gpsimd",
                            dbg_h["g"][mo * 128:(mo + 1) * 128,
                                       tok0 + hh * 128:tok0 + hh * 128 + 128],
                            dtT[:, 1024 + mo * 256 + hh * 128:
                                1024 + mo * 256 + hh * 128 + 128],
                            sem="sO")
            dbg_guard()
            if "hid" in dbg:
                dbg_pre()
                for mo in range(FDM):
                    dma("gpsimd", dbg_h["hid"][mo * 128:(mo + 1) * 128, tok0:tok0 + LC],
                        hiddenT[:, mo * LC:(mo + 1) * LC], sem="sO")
            dbg_guard()

            te_bank(5)
            slot_mark = {}
            for fi in range(FDM):
                sl = fi % 2
                P.waitm("scalar", slot_mark.get(sl))
                ssq = P.op("scalar", lambda s, f=fi, o=sl: s.activation(
                    out=sp(o * 256, LC), in_=hiddenT[:, f * LC:(f + 1) * LC],
                    func=AF.Square), inc="sS")
                P.wait("tensor", "sS", ssq)
                tm = P.op("tensor", lambda e, o=sl, f=fi: e.matmul(
                    pb[5][0:1, 0:LC], onescol[:, 0:1], sp(o * 256, LC),
                    start=(f == 0), stop=(f == FDM - 1)), inc="sT5")
                slot_mark[sl] = ("sT5", tm)
            P.wait("scalar", "sT5", P.cnt["sT5"])
            sm3 = P.op("scalar", lambda s: s.activation(
                out=rows[0:1, RW_A:RW_A + LC], in_=pb[5][0:1, 0:LC], func=AF.Sqrt,
                scale=np.float32(1.0 / DM), bias=np.float32(1e-6)), inc="sS")
            P.wait("vector", "sS", sm3)
            vr3 = P.op("vector", lambda v: v.reciprocal(
                out=rows[0:1, RW_B:RW_B + LC], in_=rows[0:1, RW_A:RW_A + LC]), inc="sV")
            P.wait("tensor", "sV", vr3)
            tbc3 = P.op("tensor", lambda e: e.matmul(
                pb[5][0:128, 0:LC], onesrow[0:1, 0:128], rows[0:1, RW_B:RW_B + LC],
                start=True, stop=True), inc="sT5")
            P.wait("scalar", "sT5", tbc3)
            sbc3 = P.op("scalar", lambda s: s.copy(out=r3bcS[:], in_=pb[5][0:128, 0:LC]), inc="sS")
            P.wait("vector", "sS", sbc3)
            vhn = None
            for m in range(FDM):
                vhn = P.op("vector", lambda v, mm=m: v.scalar_tensor_tensor(
                    out=arena(A_HNT + mm * LC, LC), in0=hiddenT[:, mm * LC:(mm + 1) * LC],
                    scalar=colv[:, 8 + mm:9 + mm], in1=r3bcS[:],
                    op0=ALU.mult, op1=ALU.mult), inc="sV")
            P.wait("tensor", "sV", vhn)


            gsm = {}
            for mp in range(0, FFF, 2):
                ldg = load_pair(gate_w, mp)
                ldu = load_pair(up_w, mp)
                for sub in range(2):
                    m = mp + sub
                    b = m % 2
                    te_bank(b)
                    tm = pair_mm(ldg, sub, b, lambda k: arena(A_HNT + k * LC, LC))
                    P.waitm("scalar", tm)
                    sl = 512 + (m % 2) * 256
                    P.waitm("scalar", gsm.get(m % 2))
                    sg_ = P.op("scalar", lambda s, bb=b, o=sl: s.activation(
                        out=sp(o, LC), in_=pb[bb][0:128, 0:LC], func=AF.Silu), inc="sS")
                    bank_free[b] = ("sS", sg_)
                    b2 = 2 + (m % 2)
                    te_bank(b2)
                    tm2 = pair_mm(ldu, sub, b2, lambda k: arena(A_HNT + k * LC, LC))
                    P.waitm("vector", tm2)
                    P.wait("vector", "sS", sg_)
                    vp = P.op("vector", lambda v, bb=b2, o=sl, mm=m: v.tensor_tensor(
                        out=arena(A_PROD + mm * LC, LC), in0=sp(o, LC),
                        in1=pb[bb][0:128, 0:LC], op=ALU.mult), inc="sV")
                    bank_free[b2] = ("sV", vp)
                    gsm[m % 2] = ("sV", vp)
            P.wait("tensor", "sV", P.cnt["sV"])
            for mo in range(FDM):
                b = mo % 4
                te_bank(b)
                tm = stream_mm(down_w, mo, FFF, lambda k: arena(A_PROD + k * LC, LC), b)
                P.waitm("vector", tm)
                vd_ = P.op("vector", lambda v, m=mo, bb=b: v.tensor_tensor(
                    out=sT_[:, m * LC:(m + 1) * LC], in0=pb[bb][0:128, 0:LC],
                    in1=hiddenT[:, m * LC:(m + 1) * LC], op=ALU.add), inc="sV")
                bank_free[b] = ("sV", vd_)
            bar()
            if "s" in dbg:
                dbg_pre()
                for mo in range(FDM):
                    dma("gpsimd", dbg_h["s"][mo * 128:(mo + 1) * 128, tok0:tok0 + LC],
                        sT_[:, mo * LC:(mo + 1) * LC], sem="sO")
            dbg_guard()

            vpr = None
            for mo in range(FDM):
                vpr = P.op("vector", lambda v, m=mo: v.tensor_tensor(
                    out=arena(A_PR + m * LC, LC), in0=sT_[:, m * LC:(m + 1) * LC],
                    in1=dtT[:, 1024 + m * 256:1024 + (m + 1) * 256],
                    op=ALU.mult), inc="sV")
            P.wait("tensor", "sV", vpr)
            vo = None
            for mo in range(FDM):
                for tt in range(NTT):
                    b = 6 + ((mo * NTT + tt) % 2)
                    te_bank(b)
                    t = P.op("tensor", lambda e, m=mo, t_=tt, bb=b: e.transpose(
                        pb[bb][0:128, 0:128], arena(A_PR + m * LC + t_ * 128, 128),
                        sident[:]), inc=f"sT{b}")
                    P.wait("vector", f"sT{b}", t)
                    vo = P.op("vector", lambda v, m=mo, t_=tt, bb=b: v.tensor_tensor(
                        out=xnW[:, t_ * DM + m * 128: t_ * DM + (m + 1) * 128],
                        in0=pb[bb][0:128, 0:128],
                        in1=x_t[:, t_ * DM + m * 128: t_ * DM + (m + 1) * 128],
                        op=ALU.add), inc="sV")
                    bank_free[b] = ("sV", vo)
            P.wait("gpsimd", "sV", vo)
            for tt in range(NTT):
                dma("gpsimd", out_h[tok0 + tt * 128: tok0 + (tt + 1) * 128, :],
                    xnW[:, tt * DM:(tt + 1) * DM], sem="sO")
            last_combine = ("sV", vo)
            bar()

        P.wait("gpsimd", "sO", P.cnt["sO"])
        P.wait("gpsimd", "sG", P.cnt["sG"])
        P.op("gpsimd", lambda g: g.nop())

        def runner(eng_name):
            self_sem = SELF_SEM.get(eng_name)

            def run(eng):
                idx = 0
                for e in P.ev[eng_name]:
                    if e[0] == "auto":
                        if idx > 0:
                            eng.wait_ge(sems[self_sem], idx)
                        r = e[1](eng)
                        if r is None:
                            r = eng.nop()
                        r.then_inc(sems[self_sem], 1)
                        idx += 1
                    elif e[0] == "op":
                        e[1](eng)
                    elif e[0] == "inc":
                        e[1](eng).then_inc(sems[e[2]], e[3])
                    else:
                        eng.wait_ge(sems[e[1]], e[2])
            return run

        block.tensor(runner("tensor"))
        block.vector(runner("vector"))
        block.scalar(runner("scalar"))
        block.sync(runner("sync"))
        block.gpsimd(runner("gpsimd"))

    return nc


def make_inputs(inp, b):
    m = {"x": np.ascontiguousarray(np.asarray(inp["L0_attn_hidden_states"])[b])}
    names = {
        "in_proj_w": "L0_attn_in_proj_w", "conv_w": "L0_attn_conv_w",
        "conv_b": "L0_attn_conv_b", "x_proj_w": "L0_attn_x_proj_w",
        "dt_proj_w": "L0_attn_dt_proj_w", "dt_proj_b": "L0_attn_dt_proj_b",
        "dt_ln_w": "L0_attn_dt_ln_w", "b_ln_w": "L0_attn_b_ln_w",
        "c_ln_w": "L0_attn_c_ln_w", "A_log": "L0_attn_A_log",
        "D_skip": "L0_attn_D_skip", "out_proj_w": "L0_attn_out_proj_w",
        "input_ln_w": "L0_attn_input_ln_w", "pre_moe_ln_w": "L0_attn_pre_moe_ln_w",
        "gate_w": "L0_attn_gate_w", "up_w": "L0_attn_up_w",
        "down_w": "L0_attn_down_w", "ffn_w_gate": "L0_ffn_w_gate",
        "ffn_w_up": "L0_ffn_w_up", "ffn_w_down": "L0_ffn_w_down",
    }
    for k, src in names.items():
        m[k] = np.ascontiguousarray(np.asarray(inp[src]))
    return m

def make_inputs(inp, b):
    m = {"x": np.ascontiguousarray(np.asarray(inp["L0_attn_hidden_states"])[b])}
    names = {
        "in_proj_w": "L0_attn_in_proj_w", "conv_w": "L0_attn_conv_w",
        "conv_b": "L0_attn_conv_b", "x_proj_w": "L0_attn_x_proj_w",
        "dt_proj_w": "L0_attn_dt_proj_w", "dt_proj_b": "L0_attn_dt_proj_b",
        "dt_ln_w": "L0_attn_dt_ln_w", "b_ln_w": "L0_attn_b_ln_w",
        "c_ln_w": "L0_attn_c_ln_w", "A_log": "L0_attn_A_log",
        "D_skip": "L0_attn_D_skip", "out_proj_w": "L0_attn_out_proj_w",
        "input_ln_w": "L0_attn_input_ln_w", "pre_moe_ln_w": "L0_attn_pre_moe_ln_w",
        "gate_w": "L0_attn_gate_w", "up_w": "L0_attn_up_w",
        "down_w": "L0_attn_down_w", "ffn_w_gate": "L0_ffn_w_gate",
        "ffn_w_up": "L0_ffn_w_up", "ffn_w_down": "L0_ffn_w_down",
    }
    for k, src in names.items():
        m[k] = np.ascontiguousarray(np.asarray(inp[src]))
    return m

def make_inputs(inp, b):
    m = {"x": np.ascontiguousarray(np.asarray(inp["L0_attn_hidden_states"])[b])}
    names = {
        "in_proj_w": "L0_attn_in_proj_w", "conv_w": "L0_attn_conv_w",
        "conv_b": "L0_attn_conv_b", "x_proj_w": "L0_attn_x_proj_w",
        "dt_proj_w": "L0_attn_dt_proj_w", "dt_proj_b": "L0_attn_dt_proj_b",
        "dt_ln_w": "L0_attn_dt_ln_w", "b_ln_w": "L0_attn_b_ln_w",
        "c_ln_w": "L0_attn_c_ln_w", "A_log": "L0_attn_A_log",
        "D_skip": "L0_attn_D_skip", "out_proj_w": "L0_attn_out_proj_w",
        "input_ln_w": "L0_attn_input_ln_w", "pre_moe_ln_w": "L0_attn_pre_moe_ln_w",
        "gate_w": "L0_attn_gate_w", "up_w": "L0_attn_up_w",
        "down_w": "L0_attn_down_w", "ffn_w_gate": "L0_ffn_w_gate",
        "ffn_w_up": "L0_ffn_w_up", "ffn_w_down": "L0_ffn_w_down",
    }
    for k, src in names.items():
        m[k] = np.ascontiguousarray(np.asarray(inp[src]))
    return m

def make_inputs(inp, b):
    m = {"x": np.ascontiguousarray(np.asarray(inp["L0_attn_hidden_states"])[b])}
    names = {
        "in_proj_w": "L0_attn_in_proj_w", "conv_w": "L0_attn_conv_w",
        "conv_b": "L0_attn_conv_b", "x_proj_w": "L0_attn_x_proj_w",
        "dt_proj_w": "L0_attn_dt_proj_w", "dt_proj_b": "L0_attn_dt_proj_b",
        "dt_ln_w": "L0_attn_dt_ln_w", "b_ln_w": "L0_attn_b_ln_w",
        "c_ln_w": "L0_attn_c_ln_w", "A_log": "L0_attn_A_log",
        "D_skip": "L0_attn_D_skip", "out_proj_w": "L0_attn_out_proj_w",
        "input_ln_w": "L0_attn_input_ln_w", "pre_moe_ln_w": "L0_attn_pre_moe_ln_w",
        "gate_w": "L0_attn_gate_w", "up_w": "L0_attn_up_w",
        "down_w": "L0_attn_down_w", "ffn_w_gate": "L0_ffn_w_gate",
        "ffn_w_up": "L0_ffn_w_up", "ffn_w_down": "L0_ffn_w_down",
    }
    for k, src in names.items():
        m[k] = np.ascontiguousarray(np.asarray(inp[src]))
    return m

def make_inputs(inp, b):
    m = {"x": np.ascontiguousarray(np.asarray(inp["L0_attn_hidden_states"])[b])}
    names = {
        "in_proj_w": "L0_attn_in_proj_w", "conv_w": "L0_attn_conv_w",
        "conv_b": "L0_attn_conv_b", "x_proj_w": "L0_attn_x_proj_w",
        "dt_proj_w": "L0_attn_dt_proj_w", "dt_proj_b": "L0_attn_dt_proj_b",
        "dt_ln_w": "L0_attn_dt_ln_w", "b_ln_w": "L0_attn_b_ln_w",
        "c_ln_w": "L0_attn_c_ln_w", "A_log": "L0_attn_A_log",
        "D_skip": "L0_attn_D_skip", "out_proj_w": "L0_attn_out_proj_w",
        "input_ln_w": "L0_attn_input_ln_w", "pre_moe_ln_w": "L0_attn_pre_moe_ln_w",
        "gate_w": "L0_attn_gate_w", "up_w": "L0_attn_up_w",
        "down_w": "L0_attn_down_w", "ffn_w_gate": "L0_ffn_w_gate",
        "ffn_w_up": "L0_ffn_w_up", "ffn_w_down": "L0_ffn_w_down",
    }
    for k, src in names.items():
        m[k] = np.ascontiguousarray(np.asarray(inp[src]))
    return m


_NC_CACHE = {}


def kernel(**inputs):
    from concourse.bass_utils import run_bass_kernel_spmd
    if "nc" not in _NC_CACHE:
        _NC_CACHE["nc"] = build_program(nchunks=8, dbg=())
    nc = _NC_CACHE["nc"]
    in_maps = [make_inputs(inputs, b) for b in range(8)]
    res = run_bass_kernel_spmd(nc, in_maps, core_ids=list(range(8)))
    return np.stack([np.asarray(r["out"]) for r in res.results], axis=0)


# version 16
# speedup vs baseline: 1.0821x; 1.0187x over previous
import numpy as np
import concourse.bass as bass
import concourse.mybir as mybir
from contextlib import ExitStack

F32 = mybir.dt.float32
AF = mybir.ActivationFunctionType
ALU = mybir.AluOpType
AX = mybir.AxisListType

T, DM, DI, DS, R, DFF, DQ, KC = 2048, 1024, 2048, 16, 64, 4096, 2816, 4
LC = 256
NTT = LC // 128
FDM, FDI, FFF, FQ = DM // 128, DI // 128, DFF // 128, DQ // 128
NBK_DM = DM // 16

CONSTS = {
    "th0": 0.25, "th1": 0.75, "th2": 1.25, "th3": 1.75, "th4": 2.5,
    "th5": 3.5, "th6": 5.0, "two": 2.0, "four": 4.0, "half": 0.5,
    "eps12": 1e-12, "inv1024": 1.0 / 1024.0, "six": 6.0,
}
C_IDX = {k: i for i, k in enumerate(CONSTS)}
C_T6 = len(CONSTS)

ENGS = ("tensor", "vector", "scalar", "sync", "gpsimd")


SELF_SEM = {"vector": "sV", "scalar": "sS", "gpsimd": "sP"}


class Plan:
    def __init__(self):
        self.ev = {e: [] for e in ENGS}
        self.cnt = {s: 0 for s in (["sV", "sS", "sW", "sWv", "sW0", "sW1", "sG", "sO", "sTB", "sVB", "sSB", "sX", "sP", "sW2", "sQ0", "sQ1"] + [f"sT{i}" for i in range(8)])}
        self.barn = 0

    def op(self, eng, fn, inc=None, n=1):
        if eng in SELF_SEM and (inc is None or inc == SELF_SEM[eng]):
            s = SELF_SEM[eng]
            self.cnt[s] += 1
            self.ev[eng].append(("auto", fn))
            return self.cnt[s]
        if inc is None:
            self.ev[eng].append(("op", fn))
            return None
        self.cnt[inc] += n
        self.ev[eng].append(("inc", fn, inc, n))
        return self.cnt[inc]

    def wait(self, eng, sem, val):
        if val:
            self.ev[eng].append(("wait", sem, val))

    def waitm(self, eng, marker):
        if marker is not None:
            self.wait(eng, marker[0], marker[1])


def build_program(nchunks=8, dbg=()):
    nc = bass.Bass("TRN2", target_bir_lowering=False)
    dbg = set(dbg)

    x_h = nc.dram_tensor("x", [T, DM], F32, kind="ExternalInput").ap()
    w_in = nc.dram_tensor("in_proj_w", [DM, 2 * DI], F32, kind="ExternalInput").ap()
    conv_w = nc.dram_tensor("conv_w", [DI, KC], F32, kind="ExternalInput").ap()
    conv_b = nc.dram_tensor("conv_b", [DI], F32, kind="ExternalInput").ap()
    xproj_w = nc.dram_tensor("x_proj_w", [DI, R + 2 * DS], F32, kind="ExternalInput").ap()
    dtproj_w = nc.dram_tensor("dt_proj_w", [R, DI], F32, kind="ExternalInput").ap()
    dtproj_b = nc.dram_tensor("dt_proj_b", [DI], F32, kind="ExternalInput").ap()
    dtln_w = nc.dram_tensor("dt_ln_w", [R], F32, kind="ExternalInput").ap()
    bln_w = nc.dram_tensor("b_ln_w", [DS], F32, kind="ExternalInput").ap()
    cln_w = nc.dram_tensor("c_ln_w", [DS], F32, kind="ExternalInput").ap()
    alog = nc.dram_tensor("A_log", [DI, DS], F32, kind="ExternalInput").ap()
    dskip = nc.dram_tensor("D_skip", [DI], F32, kind="ExternalInput").ap()
    wout = nc.dram_tensor("out_proj_w", [DI, DM], F32, kind="ExternalInput").ap()
    inln_w = nc.dram_tensor("input_ln_w", [DM], F32, kind="ExternalInput").ap()
    pmln_w = nc.dram_tensor("pre_moe_ln_w", [DM], F32, kind="ExternalInput").ap()
    gate_w = nc.dram_tensor("gate_w", [DM, DFF], F32, kind="ExternalInput").ap()
    up_w = nc.dram_tensor("up_w", [DM, DFF], F32, kind="ExternalInput").ap()
    down_w = nc.dram_tensor("down_w", [DFF, DM], F32, kind="ExternalInput").ap()
    fg_w = nc.dram_tensor("ffn_w_gate", [DQ, DM], F32, kind="ExternalInput").ap()
    fu_w = nc.dram_tensor("ffn_w_up", [DQ, DM], F32, kind="ExternalInput").ap()
    fd_w = nc.dram_tensor("ffn_w_down", [DM, DQ], F32, kind="ExternalInput").ap()
    out_h = nc.dram_tensor("out", [T, DM], F32, kind="ExternalOutput").ap()
    wqtg = nc.dram_tensor("wqtg", [DM, DQ], F32, kind="Internal").ap()
    wqtu = nc.dram_tensor("wqtu", [DM, DQ], F32, kind="Internal").ap()
    wqtd = nc.dram_tensor("wqtd", [DQ, DM], F32, kind="Internal").ap()

    dbg_h = {}
    def dbg_tensor(name, shape):
        if name in dbg:
            dbg_h[name] = nc.dram_tensor("d_" + name, shape, F32, kind="ExternalOutput").ap()
    dbg_tensor("xn", [T, DM]); dbg_tensor("xq", [T, DM])
    dbg_tensor("xssm", [DI, T]); dbg_tensor("sz", [DI, T])
    dbg_tensor("dt", [DI, T]); dbg_tensor("bm", [DS, T]); dbg_tensor("cm", [DS, T])
    dbg_tensor("y", [DI, T]); dbg_tensor("hid", [DM, T])
    dbg_tensor("s", [DM, T]); dbg_tensor("g", [DM, T]); dbg_tensor("hq", [T, DQ])

    ident_h = nc.inline_tensor(np.eye(128, dtype=np.float32), name="identc")
    rs = np.zeros((16, 16 * 128), dtype=np.float32)
    for n_ in range(16):
        rs[n_, n_ * 128:(n_ + 1) * 128] = 1.0
    rowsel_h = nc.inline_tensor(rs, name="rowselc")

    P = Plan()

    with ExitStack() as ctx:
        E = ctx.enter_context

        cpool = E(nc.sbuf_tensor("cpool", [128, 32], F32))
        colv = E(nc.sbuf_tensor("colv", [128, 64], F32))
        convw = E(nc.sbuf_tensor("convw", [128, 64], F32))
        aneg = E(nc.sbuf_tensor("aneg", [128, 256], F32))
        sident = E(nc.sbuf_tensor("sident", [128, 128], F32))
        rowsel = E(nc.sbuf_tensor("rowsel", [16, 16 * 128], F32))
        onesrow = E(nc.sbuf_tensor("onesrow", [1, 128], F32))
        onescol = E(nc.sbuf_tensor("onescol", [128, 1], F32))
        smallv = E(nc.sbuf_tensor("smallv", [64, 4], F32))
        vstage = E(nc.sbuf_tensor("vstage", [16, 128], F32))
        x_t = E(nc.sbuf_tensor("x_t", [128, NTT * DM], F32))
        xnT = E(nc.sbuf_tensor("xnT", [128, FDM * LC], F32))
        xnW = E(nc.sbuf_tensor("xnW", [128, FDM * LC], F32))
        xqT = E(nc.sbuf_tensor("xqT", [128, FDM * LC], F32))
        wbuf = E(nc.sbuf_tensor("wbuf", [128, 6144], F32))
        SP = E(nc.sbuf_tensor("SP", [128, 5120], F32))
        rows = E(nc.sbuf_tensor("rows", [1, 1024], F32))
        projS = E(nc.sbuf_tensor("projS", [64, LC], F32))
        projB = E(nc.sbuf_tensor("projB", [16, LC], F32))
        projC = E(nc.sbuf_tensor("projC", [16, LC], F32))
        dtrms = E(nc.sbuf_tensor("dtrms", [64, LC], F32))
        BmT = E(nc.sbuf_tensor("BmT", [16, LC], F32))
        CmT = E(nc.sbuf_tensor("CmT", [16, LC], F32))
        r2bcS = E(nc.sbuf_tensor("r2bcS", [128, LC], F32))
        r3bcS = E(nc.sbuf_tensor("r3bcS", [128, LC], F32))
        hiddenT = E(nc.sbuf_tensor("hiddenT", [128, FDM * LC], F32))
        sT_ = E(nc.sbuf_tensor("sT_", [128, FDM * LC], F32))
        szb = E(nc.sbuf_tensor("szb", [128, FDI * LC], F32))
        dtT = E(nc.sbuf_tensor("dtT", [128, FDI * LC], F32))
        hstate = E(nc.sbuf_tensor("hstate", [128, 256], F32))
        tails = E(nc.sbuf_tensor("tails", [128, 48], F32))
        xpw = E(nc.sbuf_tensor("xpw", [128, 16 * 96], F32))
        AR = E(nc.sbuf_tensor("AR", [128, 12352], F32))

        pb = [E(nc.psum_tensor(f"pb{i}", [128, 512], F32)) for i in range(8)]

        sems = {}
        for s in (["sV", "sS", "sW", "sWv", "sW0", "sW1", "sW2", "sG", "sO", "sTB", "sVB", "sSB", "sX", "sP", "sQ0", "sQ1"] + [f"sT{i}" for i in range(8)]):
            sems[s] = E(nc.semaphore(s))
        block = E(nc.Block())

        bank_free = [None] * 8
        whalf_free = [None, None, None]
        whalf_ctr = [0]

        def c(name):
            return cpool[:, C_IDX[name]:C_IDX[name] + 1]

        def arena(o, n):
            return AR[:, o:o + n]

        def sp(o, n):
            return SP[:, o:o + n]

        def bar(engs=("tensor", "vector", "scalar")):
            tvals = {f"sT{i}": P.cnt[f"sT{i}"] for i in range(8)}
            tb = P.op("tensor", lambda eng: eng.nop(), inc="sTB")
            vv = P.op("vector", lambda eng: eng.nop())
            ss = P.op("scalar", lambda eng: eng.nop())
            pp = P.op("gpsimd", lambda eng: eng.nop())
            P.wait("tensor", "sV", vv)
            P.wait("tensor", "sS", ss)
            P.wait("tensor", "sP", pp)
            for e in ("vector", "scalar"):
                for s_, val in tvals.items():
                    P.wait(e, s_, val)
                P.wait(e, "sTB", tb)
            P.wait("vector", "sS", ss)
            P.wait("vector", "sP", pp)
            P.wait("scalar", "sV", vv)
            P.wait("scalar", "sP", pp)
            P.wait("gpsimd", "sV", vv)
            P.wait("gpsimd", "sS", ss)
            for s_, val in tvals.items():
                P.wait("gpsimd", s_, val)
            P.wait("gpsimd", "sTB", tb)

        def te_bank(b):
            P.waitm("tensor", bank_free[b])
            bank_free[b] = None

        def dma(eng, out, in_, sem="sW"):
            return P.op(eng, lambda e, o=out, i=in_: e.dma_start(out=o, in_=i), inc=sem, n=16)

        def dbg_guard():
            v = P.cnt["sO"]
            for e in ("vector", "scalar", "tensor"):
                P.wait(e, "sO", v)
            dbg_post()

        def dbg_pre():
            P.wait("gpsimd", "sV", P.cnt["sV"])
            P.wait("gpsimd", "sS", P.cnt["sS"])
            P.wait("gpsimd", "sP", P.cnt["sP"])

        def dbg_post():
            vv = P.op("vector", lambda v: v.engine_nop())
            P.wait("gpsimd", "sV", vv)

        def emit_quant(xs, w_, nblk, wide, small):
            av, acc, ac2, c7s, sg = wide
            bm, scl, rcp = small
            x3 = xs.rearrange("p (b k) -> p b k", k=16)
            P.op("vector", lambda v: v.tensor_reduce(
                out=bm, in_=x3, axis=AX.X, op=ALU.max, apply_absolute_value=True))
            P.op("vector", lambda v: v.engine_nop())
            P.op("vector", lambda v: v.tensor_scalar(
                out=scl, in0=bm, scalar1=cpool[:, C_T6:C_T6 + 1], scalar2=c("eps12"),
                op0=ALU.mult, op1=ALU.max))
            P.op("vector", lambda v: v.engine_nop())
            P.op("vector", lambda v: v.reciprocal(out=rcp, in_=scl))
            P.op("vector", lambda v: v.engine_nop())
            rbc = rcp.unsqueeze(2).broadcast_to([128, nblk, 16])
            sbc = scl.unsqueeze(2).broadcast_to([128, nblk, 16])
            vq = P.op("vector", lambda v: v.tensor_tensor(
                out=av.rearrange("p (b k) -> p b k", k=16), in0=x3, in1=rbc,
                op=ALU.mult), inc="sV")
            P.wait("scalar", "sV", vq)
            P.op("scalar", lambda s: s.activation(out=sg, in_=av, func=AF.Sign))
            s2 = P.op("scalar", lambda s: s.activation(out=av, in_=av, func=AF.Abs), inc="sS")
            P.wait("vector", "sS", s2)
            P.op("vector", lambda v: v.tensor_single_scalar(
                out=acc, in_=av, scalar=c("th0"), op=ALU.is_gt))
            for th in ("th1", "th2", "th3"):
                P.op("vector", lambda v, t=th: v.scalar_tensor_tensor(
                    out=acc, in0=av, scalar=c(t), in1=acc, op0=ALU.is_gt, op1=ALU.add))
            P.op("vector", lambda v: v.tensor_single_scalar(
                out=ac2, in_=av, scalar=c("th4"), op=ALU.is_gt))
            P.op("vector", lambda v: v.scalar_tensor_tensor(
                out=ac2, in0=av, scalar=c("th5"), in1=ac2, op0=ALU.is_gt, op1=ALU.add))
            P.op("vector", lambda v: v.tensor_single_scalar(
                out=c7s, in_=av, scalar=c("th6"), op=ALU.is_gt))
            P.op("vector", lambda v: v.scalar_tensor_tensor(
                out=acc, in0=ac2, scalar=c("two"), in1=acc, op0=ALU.mult, op1=ALU.add))
            P.op("vector", lambda v: v.scalar_tensor_tensor(
                out=acc, in0=c7s, scalar=c("four"), in1=acc, op0=ALU.mult, op1=ALU.add))
            P.op("vector", lambda v: v.tensor_tensor(out=acc, in0=acc, in1=sg, op=ALU.mult))
            return P.op("vector", lambda v: v.scalar_tensor_tensor(
                out=xs.rearrange("p (b k) -> p b k", k=16),
                in0=acc.rearrange("p (b k) -> p b k", k=16),
                scalar=c("half"), in1=sbc, op0=ALU.mult, op1=ALU.mult), inc="sV")

        def emit_quant_gp(xs, w_, nblk, wide, small, pre=()):
            av, acc, ac2, c7s, sg = wide
            bm, scl, rcp = small
            x3 = xs.rearrange("p (b k) -> p b k", k=16)
            for (s_, v_) in pre:
                P.wait("gpsimd", s_, v_)
                if s_ != "sV":
                    P.wait("vector", s_, v_)
            P.op("vector", lambda v: v.tensor_reduce(
                out=bm, in_=x3, axis=AX.X, op=ALU.max, apply_absolute_value=True))
            P.op("vector", lambda v: v.engine_nop())
            P.op("vector", lambda v: v.tensor_scalar(
                out=scl, in0=bm, scalar1=cpool[:, C_T6:C_T6 + 1], scalar2=c("eps12"),
                op0=ALU.mult, op1=ALU.max))
            P.op("vector", lambda v: v.engine_nop())
            v1 = P.op("vector", lambda v: v.reciprocal(out=rcp, in_=scl))
            rbc = rcp.unsqueeze(2).broadcast_to([128, nblk, 16])
            sbc = scl.unsqueeze(2).broadcast_to([128, nblk, 16])
            P.wait("gpsimd", "sV", v1)
            gq = P.op("gpsimd", lambda g: g.tensor_tensor(
                out=av.rearrange("p (b k) -> p b k", k=16), in0=x3, in1=rbc,
                op=ALU.mult))
            P.wait("scalar", "sP", gq)
            P.op("scalar", lambda s: s.activation(out=sg, in_=av, func=AF.Sign))
            s2 = P.op("scalar", lambda s: s.activation(out=av, in_=av, func=AF.Abs), inc="sS")
            P.wait("gpsimd", "sS", s2)
            P.op("gpsimd", lambda g: g.tensor_single_scalar(
                out=acc, in_=av, scalar=c("th0"), op=ALU.is_gt))
            for th in ("th1", "th2", "th3"):
                P.op("gpsimd", lambda g, t=th: g.scalar_tensor_tensor(
                    out=acc, in0=av, scalar=c(t), in1=acc, op0=ALU.is_gt, op1=ALU.add))
            P.op("gpsimd", lambda g: g.tensor_single_scalar(
                out=ac2, in_=av, scalar=c("th4"), op=ALU.is_gt))
            P.op("gpsimd", lambda g: g.scalar_tensor_tensor(
                out=ac2, in0=av, scalar=c("th5"), in1=ac2, op0=ALU.is_gt, op1=ALU.add))
            P.op("gpsimd", lambda g: g.tensor_single_scalar(
                out=c7s, in_=av, scalar=c("th6"), op=ALU.is_gt))
            P.op("gpsimd", lambda g: g.scalar_tensor_tensor(
                out=acc, in0=ac2, scalar=c("two"), in1=acc, op0=ALU.mult, op1=ALU.add))
            P.op("gpsimd", lambda g: g.scalar_tensor_tensor(
                out=acc, in0=c7s, scalar=c("four"), in1=acc, op0=ALU.mult, op1=ALU.add))
            P.op("gpsimd", lambda g: g.tensor_tensor(out=acc, in0=acc, in1=sg, op=ALU.mult))
            gend = P.op("gpsimd", lambda g: g.scalar_tensor_tensor(
                out=xs.rearrange("p (b k) -> p b k", k=16),
                in0=acc.rearrange("p (b k) -> p b k", k=16),
                scalar=c("half"), in1=sbc, op0=ALU.mult, op1=ALU.mult))
            return ("sP", gend)

        def init_consts(vector):
            for k, v in CONSTS.items():
                vector.memset(cpool[:, C_IDX[k]:C_IDX[k] + 1], float(v))
            vector.memset(onescol[:], 1.0)
            vector.memset(onesrow[:], 1.0)
            vector.memset(hstate[:], 0.0)
            vector.memset(tails[:], 0.0)
        P.op("vector", init_consts)
        P.op("vector", lambda v: v.engine_nop())
        P.op("vector", lambda v: v.reciprocal(out=cpool[:, C_T6:C_T6 + 1], in_=c("six")))
        P.op("vector", lambda v: v.engine_nop(), inc="sV")

        dma("sync", sident[:], ident_h.ap()[:])
        dma("sync", rowsel[:], rowsel_h.ap()[:])
        for dt_i in range(FDI):
            dma("sync", convw[:, dt_i * 4:(dt_i + 1) * 4],
                conv_w[dt_i * 128:(dt_i + 1) * 128, :])
        for dt_i in range(FDI):
            dma("sync", arena(dt_i * 16, 16), alog[dt_i * 128:(dt_i + 1) * 128, :])
        w_res = dma("sync", xpw[:].rearrange("p (k j) -> p k j", j=96),
                    xproj_w[:].rearrange("(k p) j -> p k j", p=128))
        P.wait("tensor", "sW", w_res)
        P.wait("scalar", "sW", w_res)
        for dt_i in range(FDI):
            P.op("scalar", lambda s, i=dt_i: s.activation(
                out=aneg[:, i * 16:(i + 1) * 16], in_=arena(i * 16, 16), func=AF.Exp))
        for dt_i in range(FDI):
            P.op("scalar", lambda s, i=dt_i: s.mul(
                out=aneg[:, i * 16:(i + 1) * 16], in_=aneg[:, i * 16:(i + 1) * 16],
                mul=np.float32(-1.0)))

        for src, n, off in ((inln_w, DM, 0), (pmln_w, DM, 8), (conv_b, DI, 16),
                            (dtproj_b, DI, 32), (dskip, DI, 48)):
            k = n // 128
            w = dma("sync", vstage[0:k, :], src.rearrange("(k p) -> k p", p=128),
                    sem="sWv")
            P.wait("tensor", "sWv", w)
            t = P.op("tensor", lambda e, kk=k: e.transpose(
                pb[6][0:128, 0:kk], vstage[0:kk, 0:128], sident[0:kk, 0:kk]), inc="sT6")
            P.wait("scalar", "sT6", t)
            sc = P.op("scalar", lambda s, kk=k, o=off: s.copy(
                out=colv[:, o:o + kk], in_=pb[6][0:128, 0:kk]), inc="sS")
            P.wait("tensor", "sS", sc)
            P.wait("sync", "sS", sc)
        for src, n, col in ((dtln_w, R, 0), (bln_w, DS, 1), (cln_w, DS, 2)):
            w = dma("sync", vstage[0:1, 0:n], src.unsqueeze(0), sem="sWv")
            P.wait("tensor", "sWv", w)
            t = P.op("tensor", lambda e, nn=n: e.transpose(
                pb[6][0:nn, 0:1], vstage[0:1, 0:nn], sident[0:1, 0:1]), inc="sT6")
            P.wait("scalar", "sT6", t)
            sc = P.op("scalar", lambda s, nn=n, cc=col: s.copy(
                out=smallv[0:nn, cc:cc + 1], in_=pb[6][0:nn, 0:1]), inc="sS")
            P.wait("tensor", "sS", sc)
            P.wait("sync", "sS", sc)

        QW_X, QW_AV, QW_ACC, QW_AC2, QW_C7, QW_SG = 0, 1024, 2048, 3072, 4096, 4096
        W_STAGE = 6144
        CP_STAGE = 768
        P.wait("vector", "sS", P.cnt["sS"])
        jobs = []
        for (w_src, w_dst, nrows, kdim) in (
            (fg_w, wqtg, DQ, DM), (fu_w, wqtu, DQ, DM), (fd_w, wqtd, DM, DQ),
        ):
            for rt in range(nrows // 128):
                jobs.append((w_src, w_dst, rt, kdim))
        stage_mark = [None, None]
        st_mark = None
        for ji, (w_src, w_dst, rt, kdim) in enumerate(jobs):
            stg = ji % 2
            soff = W_STAGE + stg * 2816
            P.waitm("sync", stage_mark[stg])
            wld = dma("sync", arena(soff, kdim), w_src[rt * 128:(rt + 1) * 128, :],
                      sem=f"sQ{stg}")
            P.wait("vector", f"sQ{stg}", wld)
            slw = 1024 if kdim % 1024 == 0 else 704
            vq = None
            for s0 in range(0, kdim, slw):
                nb = slw // 16
                vq = emit_quant(
                    arena(soff + s0, slw), slw, nb,
                    (sp(QW_AV, slw), sp(QW_ACC, slw), sp(QW_AC2, slw),
                     sp(QW_X, slw), sp(QW_SG, slw)),
                    (arena(0, nb), arena(256, nb), arena(512, nb)))
            P.wait("tensor", "sV", vq)
            P.waitm("scalar", st_mark)
            last_t = None
            for fi in range(kdim // 128):
                bk = 6 + (fi % 2)
                te_bank(bk)
                t = P.op("tensor", lambda e, f=fi, bb=bk, so=soff: e.transpose(
                    pb[bb][0:128, 0:128], arena(so + f * 128, 128),
                    sident[:]), inc=f"sT{bk}")
                P.wait("scalar", f"sT{bk}", t)
                sc = P.op("scalar", lambda s, f=fi, bb=bk: s.copy(
                    out=arena(CP_STAGE + f * 128, 128), in_=pb[bb][0:128, 0:128]),
                    inc="sS")
                bank_free[bk] = ("sS", sc)
                last_t = t
            stage_mark[stg] = (f"sT{6 + (kdim // 128 - 1) % 2}", last_t)
            P.wait("gpsimd", "sS", P.cnt["sS"])
            gd = P.op("gpsimd", lambda g, r=rt, d=w_dst, kd=kdim: g.dma_start(
                out=d[0:kd, r * 128:(r + 1) * 128]
                .rearrange("(f p) j -> p f j", p=128),
                in_=arena(CP_STAGE, kd).rearrange("p (f j) -> p f j", j=128)),
                inc="sG", n=16)
            st_mark = ("sG", gd)
        P.wait("vector", "sG", P.cnt["sG"])
        P.wait("scalar", "sG", P.cnt["sG"])
        bar()
        P.wait("vector", "sG", P.cnt["sG"])

        def stream_mm(weight_h, mcol, k_tiles, rhs_fn, bank, ncols=LC):
            nseg = (k_tiles + 15) // 16
            tm = None
            for seg in range(nseg):
                k0 = seg * ((k_tiles + nseg - 1) // nseg)
                k1 = min(k_tiles, k0 + (k_tiles + nseg - 1) // nseg)
                half = whalf_ctr[0] % 3
                whalf_ctr[0] += 1
                P.waitm("sync", whalf_free[half]); whalf_free[half] = None
                wgx = dma("sync", wbuf[:, half * 2048:half * 2048 + (k1 - k0) * 128]
                          .rearrange("p (k j) -> p k j", j=128),
                          weight_h[k0 * 128:k1 * 128, mcol * 128:(mcol + 1) * 128]
                          .rearrange("(k p) j -> p k j", p=128),
                          sem=f"sW{half}")
                P.wait("tensor", f"sW{half}", wgx)
                for k in range(k0, k1):
                    tm = P.op("tensor", lambda e, kk=k, hh=half, bb=bank, kz=k0,
                              nc_=ncols: e.matmul(
                        pb[bb][0:128, 0:nc_],
                        wbuf[:, hh * 2048 + (kk - kz) * 128:
                             hh * 2048 + (kk - kz + 1) * 128],
                        rhs_fn(kk), start=(kk == 0), stop=(kk == k_tiles - 1)),
                        inc=(f"sT{bank}" if k == k1 - 1 else None))
                whalf_free[half] = (f"sT{bank}", tm)
            return (f"sT{bank}", tm)

        def load_pair(weight_h, m0, k_tiles=8):
            slot = whalf_ctr[0] % 3
            whalf_ctr[0] += 1
            P.waitm("sync", whalf_free[slot]); whalf_free[slot] = None
            off = slot * 2048
            wg = dma("sync", wbuf[:, off:off + k_tiles * 256]
                     .rearrange("p (k j) -> p k j", j=256),
                     weight_h[0:k_tiles * 128, m0 * 128:(m0 + 2) * 128]
                     .rearrange("(k p) j -> p k j", p=128),
                     sem=f"sW{slot}")
            return (slot, off, f"sW{slot}", wg)

        def pair_mm(ld, sub, bank, rhs_fn, k_tiles=8, ncols=LC):
            slot, off, sem_, wg = ld
            P.wait("tensor", sem_, wg)
            tm = None
            for k in range(k_tiles):
                tm = P.op("tensor", lambda e, kk=k, bb=bank, o=off, ss=sub,
                          nc_=ncols: e.matmul(
                    pb[bb][0:128, 0:nc_],
                    wbuf[:, o + kk * 256 + ss * 128: o + kk * 256 + ss * 128 + 128],
                    rhs_fn(kk), start=(kk == 0), stop=(kk == k_tiles - 1)),
                    inc=(f"sT{bank}" if k == k_tiles - 1 else None))
            if sub == 1:
                whalf_free[slot] = (f"sT{bank}", tm)
            return (f"sT{bank}", tm)

        A_XC, A_XN, A_XQ = 0, 2048, 4096
        A_XPAD, A_XSSM, A_Y = 0, 4160, 8256
        A_PROD, A_HNT = 0, 8192
        A_PRODQ, A_HQT, A_HQTT, A_PR = 0, 5632, 0, 8192
        A_GT = 5632

        def xpad(d):
            return arena(A_XPAD + d * 259, 259)

        def xssm(d):
            return arena(A_XSSM + d * 256, 256)

        def dtx(d):
            return arena(A_XPAD + d * 256, 256)

        def ya(d):
            return arena(A_Y + d * 256, 256)

        def szt(d):
            return szb[:, d * LC:(d + 1) * LC]

        def dtTt(d):
            return dtT[:, d * LC:(d + 1) * LC]

        RW_A, RW_B = 0, 256
        RW_C, RW_D = 512, 768
        RW_E, RW_F = 512, 768

        last_combine = None

        for ch in range(nchunks):
            tok0 = ch * LC

            P.waitm("gpsimd", last_combine)
            P.wait("scalar", "sO", P.cnt["sO"])
            wx = None
            for tt in range(NTT):
                wx = dma("gpsimd", x_t[:, tt * DM:(tt + 1) * DM],
                         x_h[tok0 + tt * 128: tok0 + (tt + 1) * 128, :], sem="sX")
            P.wait("vector", "sX", wx)
            stat = r2bcS
            for tt in range(NTT):
                P.op("vector", lambda v, t_=tt: v.reduce_sum(
                    out=stat[:, t_:t_ + 1], in_=x_t[:, t_ * DM:(t_ + 1) * DM], axis=AX.X))
            P.op("vector", lambda v: v.engine_nop())
            for tt in range(NTT):
                P.op("vector", lambda v, t_=tt: v.tensor_single_scalar(
                    out=stat[:, 2 + t_:3 + t_], in_=stat[:, t_:t_ + 1],
                    scalar=c("inv1024"), op=ALU.mult))
            P.op("vector", lambda v: v.engine_nop())
            vml = None
            for tt in range(NTT):
                vml = P.op("vector", lambda v, t_=tt: v.tensor_scalar(
                    out=arena(A_XC + t_ * DM, DM), in0=x_t[:, t_ * DM:(t_ + 1) * DM],
                    scalar1=stat[:, 2 + t_:3 + t_], scalar2=None, op0=ALU.subtract), inc="sV")
            P.wait("scalar", "sV", vml)
            ssl = None
            for tt in range(NTT):
                ssl = P.op("scalar", lambda s, t_=tt: s.activation(
                    out=arena(A_XN + t_ * DM, DM), in_=arena(A_XC + t_ * DM, DM),
                    func=AF.Square, accum_out=stat[:, 4 + t_:5 + t_]), inc="sS")
            ssl = P.op("scalar", lambda s: s.activation(
                out=stat[:, 6:6 + NTT], in_=stat[:, 4:4 + NTT], func=AF.Sqrt,
                scale=np.float32(1.0 / DM), bias=np.float32(1e-5)), inc="sS")
            P.wait("vector", "sS", ssl)
            vr = P.op("vector", lambda v: v.reciprocal(
                out=stat[:, 8:8 + NTT], in_=stat[:, 6:6 + NTT]), inc="sV")
            P.wait("scalar", "sV", vr)
            scn = None
            for tt in range(NTT):
                scn = P.op("scalar", lambda s, t_=tt: s.activation(
                    out=arena(A_XN + t_ * DM, DM), in_=arena(A_XC + t_ * DM, DM),
                    func=AF.Copy, scale=stat[:, 8 + t_:9 + t_]), inc="sS")
            if "xn" in dbg:
                dbg_pre()
                P.wait("gpsimd", "sS", scn)
                for tt in range(NTT):
                    dma("gpsimd", dbg_h["xn"][tok0 + tt * 128: tok0 + (tt + 1) * 128, :],
                        arena(A_XN + tt * DM, DM), sem="sO")
            P.wait("tensor", "sS", scn)
            for (src_off, kind) in ((A_XN, "xn"),):
                for tt in range(NTT):
                    for fi in range(FDM):
                        b = 6 + ((tt * FDM + fi) % 2)
                        te_bank(b)
                        t = P.op("tensor", lambda e, o=src_off, t_=tt, f=fi, bb=b: e.transpose(
                            pb[bb][0:128, 0:128], arena(o + t_ * DM + f * 128, 128),
                            sident[:]), inc=f"sT{b}")
                        P.wait("scalar", f"sT{b}", t)
                        if kind == "xn":
                            P.op("scalar", lambda s, f=fi, t_=tt, bb=b: s.copy(
                                out=xnT[:, f * LC + t_ * 128: f * LC + (t_ + 1) * 128],
                                in_=pb[bb][0:128, 0:128]))
                            sc = P.op("scalar", lambda s, f=fi, t_=tt, bb=b: s.activation(
                                out=xnW[:, f * LC + t_ * 128: f * LC + (t_ + 1) * 128],
                                in_=pb[bb][0:128, 0:128], func=AF.Copy,
                                scale=colv[:, f:f + 1]), inc="sS")
                        else:
                            sc = P.op("scalar", lambda s, f=fi, t_=tt, bb=b: s.copy(
                                out=xqT[:, f * LC + t_ * 128: f * LC + (t_ + 1) * 128],
                                in_=pb[bb][0:128, 0:128]), inc="sS")
                        bank_free[b] = ("sS", sc)
            P.wait("vector", "sS", scn)
            vdq = None
            for tt in range(NTT):
                P.op("vector", lambda v, t_=tt: v.tensor_copy(
                    out=arena(A_XQ + t_ * DM, DM), in_=arena(A_XN + t_ * DM, DM)))
                P.op("vector", lambda v: v.engine_nop())
                vdq = emit_quant(
                    arena(A_XQ + tt * DM, DM), DM, NBK_DM,
                    (sp(QW_AV, DM), sp(QW_ACC, DM), sp(QW_AC2, DM),
                     sp(QW_X, DM), sp(QW_SG, DM)),
                    (arena(12352 - 256, NBK_DM), arena(12352 - 512, NBK_DM),
                     arena(12352 - 768, NBK_DM)))
            if "xq" in dbg:
                dbg_pre()
                P.wait("gpsimd", "sV", vdq)
                for tt in range(NTT):
                    dma("gpsimd", dbg_h["xq"][tok0 + tt * 128: tok0 + (tt + 1) * 128, :],
                        arena(A_XQ + tt * DM, DM), sem="sO")
            P.wait("tensor", "sV", vdq)
            for (src_off, kind) in ((A_XQ, "xq"),):
                for tt in range(NTT):
                    for fi in range(FDM):
                        b = 6 + ((tt * FDM + fi) % 2)
                        te_bank(b)
                        t = P.op("tensor", lambda e, o=src_off, t_=tt, f=fi, bb=b: e.transpose(
                            pb[bb][0:128, 0:128], arena(o + t_ * DM + f * 128, 128),
                            sident[:]), inc=f"sT{b}")
                        P.wait("scalar", f"sT{b}", t)
                        if kind == "xn":
                            P.op("scalar", lambda s, f=fi, t_=tt, bb=b: s.copy(
                                out=xnT[:, f * LC + t_ * 128: f * LC + (t_ + 1) * 128],
                                in_=pb[bb][0:128, 0:128]))
                            sc = P.op("scalar", lambda s, f=fi, t_=tt, bb=b: s.activation(
                                out=xnW[:, f * LC + t_ * 128: f * LC + (t_ + 1) * 128],
                                in_=pb[bb][0:128, 0:128], func=AF.Copy,
                                scale=colv[:, f:f + 1]), inc="sS")
                        else:
                            sc = P.op("scalar", lambda s, f=fi, t_=tt, bb=b: s.copy(
                                out=xqT[:, f * LC + t_ * 128: f * LC + (t_ + 1) * 128],
                                in_=pb[bb][0:128, 0:128]), inc="sS")
                        bank_free[b] = ("sS", sc)
            bar()
            dbg_guard()

            te_bank(5)
            slot_mark = {}
            for fi in range(FDM):
                sl = fi % 2
                P.waitm("scalar", slot_mark.get(sl))
                ssq = P.op("scalar", lambda s, f=fi, o=sl: s.activation(
                    out=sp(o * 256, LC), in_=xnT[:, f * LC:(f + 1) * LC],
                    func=AF.Square), inc="sS")
                P.wait("tensor", "sS", ssq)
                tm = P.op("tensor", lambda e, o=sl, f=fi: e.matmul(
                    pb[5][0:1, 0:LC], onescol[:, 0:1], sp(o * 256, LC),
                    start=(f == 0), stop=(f == FDM - 1)), inc="sT5")
                slot_mark[sl] = ("sT5", tm)
            P.wait("scalar", "sT5", P.cnt["sT5"])
            srq = P.op("scalar", lambda s: s.activation(
                out=rows[0:1, RW_A:RW_A + LC], in_=pb[5][0:1, 0:LC], func=AF.Sqrt,
                scale=np.float32(1.0 / DM), bias=np.float32(1e-6)), inc="sS")
            P.wait("vector", "sS", srq)
            vr2 = P.op("vector", lambda v: v.reciprocal(
                out=rows[0:1, RW_B:RW_B + LC], in_=rows[0:1, RW_A:RW_A + LC]), inc="sV")
            P.wait("tensor", "sV", vr2)
            tbc = P.op("tensor", lambda e: e.matmul(
                pb[5][0:128, 0:LC], onesrow[0:1, 0:128], rows[0:1, RW_B:RW_B + LC],
                start=True, stop=True), inc="sT5")
            P.wait("scalar", "sT5", tbc)
            sbc2 = P.op("scalar", lambda s: s.copy(out=r2bcS[:], in_=pb[5][0:128, 0:LC]), inc="sS")
            P.wait("tensor", "sS", sbc2)
            P.wait("vector", "sS", sbc2)

            zmark = {}
            wg = None
            ip_half = {}
            for m in range(2 * FDI):
                if m % 2 == 0:
                    ip_half[m // 2] = whalf_ctr[0] % 3
                    whalf_ctr[0] += 1
                half = ip_half[m // 2]
                if m % 2 == 0:
                    P.waitm("sync", whalf_free[half]); whalf_free[half] = None
                    g = m // 2
                    wg = dma("sync", wbuf[:, half * 2048:half * 2048 + 2048]
                             .rearrange("p (k j) -> p k j", j=256),
                             w_in[:, g * 256:(g + 1) * 256]
                             .rearrange("(k p) j -> p k j", p=128),
                             sem=f"sW{half}")
                b = m % 4
                te_bank(b)
                P.wait("tensor", f"sW{half}", wg)
                tm = None
                for k in range(FDM):
                    tm = P.op("tensor", lambda e, mm=m, kk=k, hh=half, bb=b: e.matmul(
                        pb[bb][0:128, 0:LC],
                        wbuf[:, hh * 2048 + kk * 256 + (mm % 2) * 128:
                             hh * 2048 + kk * 256 + (mm % 2) * 128 + 128],
                        xnW[:, kk * LC:(kk + 1) * LC],
                        start=(kk == 0), stop=(kk == FDM - 1)),
                        inc=(f"sT{b}" if k == FDM - 1 else None))
                if m % 2 == 1:
                    whalf_free[half] = (f"sT{b}", tm)
                P.wait("vector", f"sT{b}", tm)
                if m < FDI:
                    vv = P.op("vector", lambda v, mm=m, bb=b: v.tensor_tensor(
                        out=xpad(mm)[:, 3:3 + LC], in0=pb[bb][0:128, 0:LC],
                        in1=r2bcS[:], op=ALU.mult), inc="sV")
                    bank_free[b] = ("sV", vv)
                else:
                    d = m - FDI
                    sl = 512 + (d % 2) * 256
                    P.waitm("vector", zmark.get(d % 2))
                    vv = P.op("vector", lambda v, bb=b, o=sl: v.tensor_tensor(
                        out=sp(o, LC), in0=pb[bb][0:128, 0:LC], in1=r2bcS[:],
                        op=ALU.mult), inc="sV")
                    bank_free[b] = ("sV", vv)
                    P.wait("scalar", "sV", vv)
                    scz = P.op("scalar", lambda s, dd=d, o=sl: s.activation(
                        out=szt(dd), in_=sp(o, LC), func=AF.Silu), inc="sS")
                    zmark[d % 2] = ("sS", scz)
            bar()

            for d in range(FDI):
                P.op("vector", lambda v, dd=d: v.tensor_copy(
                    out=xpad(dd)[:, 0:3], in_=tails[:, dd * 3:(dd + 1) * 3]))
            vtl = P.op("vector", lambda v: v.engine_nop(), inc="sV")
            P.wait("scalar", "sV", vtl)
            conv_marks = [None] * FDI
            for d in range(FDI):
                sl = (d % 2) * 256
                sm = P.op("scalar", lambda s, dd=d, o=sl: s.activation(
                    out=sp(o, LC), in_=xpad(dd)[:, 0:LC], func=AF.Copy,
                    scale=convw[:, dd * 4:dd * 4 + 1]), inc="sS")
                P.wait("vector", "sS", sm)
                for j in range(1, 4):
                    P.op("vector", lambda v, dd=d, jj=j, o=sl: v.scalar_tensor_tensor(
                        out=sp(o, LC), in0=xpad(dd)[:, jj:jj + LC],
                        scalar=convw[:, dd * 4 + jj:dd * 4 + jj + 1], in1=sp(o, LC),
                        op0=ALU.mult, op1=ALU.add))
                P.op("vector", lambda v, dd=d: v.tensor_copy(
                    out=tails[:, dd * 3:(dd + 1) * 3], in_=xpad(dd)[:, LC:LC + 3]))
                vcv = P.op("vector", lambda v: v.engine_nop(), inc="sV")
                P.wait("scalar", "sV", vcv)
                conv_marks[d] = P.op("scalar", lambda s, dd=d, o=sl: s.activation(
                    out=xssm(dd), in_=sp(o, LC), func=AF.Silu,
                    bias=colv[:, 16 + dd:17 + dd]), inc="sS")
            if dbg:
                bar()
            if "xssm" in dbg:
                dbg_pre()
                for d in range(FDI):
                    dma("gpsimd", dbg_h["xssm"][d * 128:(d + 1) * 128, tok0:tok0 + LC],
                        xssm(d), sem="sO")
            if "sz" in dbg:
                dbg_pre()
                for d in range(FDI):
                    dma("gpsimd", dbg_h["sz"][d * 128:(d + 1) * 128, tok0:tok0 + LC],
                        szt(d), sem="sO")
            dbg_guard()

            te_bank(4)
            te_bank(5)
            te_bank(6)
            tpj = None
            for k in range(FDI):
                P.wait("tensor", "sS", conv_marks[k])
                P.op("tensor", lambda e, kk=k: e.matmul(
                    pb[4][0:64, 0:LC], xpw[:, kk * 96:kk * 96 + 64], xssm(kk),
                    start=(kk == 0), stop=(kk == FDI - 1)))
                P.op("tensor", lambda e, kk=k: e.matmul(
                    pb[5][0:16, 0:LC], xpw[:, kk * 96 + 64:kk * 96 + 80], xssm(kk),
                    start=(kk == 0), stop=(kk == FDI - 1)))
                tpj = P.op("tensor", lambda e, kk=k: e.matmul(
                    pb[6][0:16, 0:LC], xpw[:, kk * 96 + 80:kk * 96 + 96], xssm(kk),
                    start=(kk == 0), stop=(kk == FDI - 1)),
                    inc=("sT4" if k == FDI - 1 else None))
            P.wait("vector", "sT4", tpj)
            P.op("vector", lambda v: v.tensor_copy(out=projS[:], in_=pb[4][0:64, 0:LC]))
            P.op("vector", lambda v: v.tensor_copy(out=projB[:], in_=pb[5][0:16, 0:LC]))
            vpj = P.op("vector", lambda v: v.tensor_copy(
                out=projC[:], in_=pb[6][0:16, 0:LC]), inc="sV")
            P.wait("scalar", "sV", vpj)
            s64 = P.op("scalar", lambda s: s.activation(
                out=sp(0, LC)[0:64, :], in_=projS[0:64, 0:LC], func=AF.Square), inc="sS")
            P.wait("tensor", "sS", s64)
            t64 = P.op("tensor", lambda e: e.matmul(
                pb[5][0:1, 0:LC], onescol[0:64, 0:1], sp(0, LC)[0:64, :],
                start=True, stop=True), inc="sT5")
            P.wait("scalar", "sT5", t64)
            s64b = P.op("scalar", lambda s: s.activation(
                out=rows[0:1, RW_A:RW_A + LC], in_=pb[5][0:1, 0:LC], func=AF.Sqrt,
                scale=np.float32(1.0 / R), bias=np.float32(1e-6)), inc="sS")
            P.wait("vector", "sS", s64b)
            v64 = P.op("vector", lambda v: v.reciprocal(
                out=rows[0:1, RW_B:RW_B + LC], in_=rows[0:1, RW_A:RW_A + LC]), inc="sV")
            P.wait("tensor", "sV", v64)
            t64b = P.op("tensor", lambda e: e.matmul(
                pb[5][0:64, 0:LC], onesrow[0:1, 0:64], rows[0:1, RW_B:RW_B + LC],
                start=True, stop=True), inc="sT5")
            P.wait("vector", "sT5", t64b)
            vdt = P.op("vector", lambda v: v.tensor_tensor(
                out=dtrms[:], in0=projS[0:64, 0:LC], in1=pb[5][0:64, 0:LC],
                op=ALU.mult), inc="sV")
            for (psrc, colid, dst, rw1, rw2) in ((projB, 1, BmT, RW_C, RW_D),
                                                 (projC, 2, CmT, RW_E, RW_F)):
                sB = P.op("scalar", lambda s, pp=psrc: s.activation(
                    out=sp(256, LC)[0:16, :], in_=pp[0:16, 0:LC], func=AF.Square), inc="sS")
                P.wait("tensor", "sS", sB)
                te_bank(6)
                tB = P.op("tensor", lambda e: e.matmul(
                    pb[6][0:1, 0:LC], onescol[0:16, 0:1], sp(256, LC)[0:16, :],
                    start=True, stop=True), inc="sT6")
                P.wait("scalar", "sT6", tB)
                sB2 = P.op("scalar", lambda s, w=rw1: s.activation(
                    out=rows[0:1, w:w + LC], in_=pb[6][0:1, 0:LC], func=AF.Sqrt,
                    scale=np.float32(1.0 / DS), bias=np.float32(1e-6)), inc="sS")
                P.wait("vector", "sS", sB2)
                vB = P.op("vector", lambda v, w1=rw1, w2=rw2: v.reciprocal(
                    out=rows[0:1, w2:w2 + LC], in_=rows[0:1, w1:w1 + LC]), inc="sV")
                P.wait("tensor", "sV", vB)
                tB2 = P.op("tensor", lambda e, w2=rw2: e.matmul(
                    pb[6][0:16, 0:LC], onesrow[0:1, 0:16], rows[0:1, w2:w2 + LC],
                    start=True, stop=True), inc="sT6")
                P.wait("vector", "sT6", tB2)
                vB2 = P.op("vector", lambda v, pp=psrc, cc=colid, dd=dst: v.scalar_tensor_tensor(
                    out=dd[:], in0=pp[0:16, 0:LC], scalar=smallv[0:16, cc:cc + 1],
                    in1=pb[6][0:16, 0:LC], op0=ALU.mult, op1=ALU.mult), inc="sV")
                bank_free[6] = ("sV", vB2)
            bar()
            if "bm" in dbg:
                dbg_pre()
                dma("gpsimd", dbg_h["bm"][0:16, tok0:tok0 + LC], BmT[:], sem="sO")
            if "cm" in dbg:
                dbg_pre()
                dma("gpsimd", dbg_h["cm"][0:16, tok0:tok0 + LC], CmT[:], sem="sO")
            dbg_guard()

            dslot = whalf_ctr[0] % 3
            whalf_ctr[0] += 1
            P.waitm("sync", whalf_free[dslot]); whalf_free[dslot] = None
            wdt = dma("sync", wbuf[0:64, dslot * 2048:(dslot + 1) * 2048]
                      .rearrange("p (k j) -> p k j", j=128),
                      dtproj_w[:].rearrange("p (k j) -> p k j", j=128),
                      sem=f"sW{dslot}")
            P.wait("scalar", f"sW{dslot}", wdt)
            sfold = P.op("scalar", lambda s, o=dslot * 2048: s.activation(
                out=wbuf[0:64, o:o + 2048], in_=wbuf[0:64, o:o + 2048],
                func=AF.Copy, scale=smallv[0:64, 0:1]), inc="sS")
            P.wait("tensor", "sS", sfold)
            vdt2_marks = {}
            for m in range(FDI):
                b = m % 4
                te_bank(b)
                tdm = P.op("tensor", lambda e, mm=m, bb=b, o=dslot * 2048: e.matmul(
                    pb[bb][0:128, 0:LC], wbuf[0:64, o + mm * 128:o + (mm + 1) * 128],
                    dtrms[0:64, 0:LC], start=True, stop=True), inc=f"sT{b}")
                P.wait("scalar", f"sT{b}", tdm)
                sl = (m % 2) * 1024
                P.waitm("scalar", vdt2_marks.get(m % 2))
                P.op("scalar", lambda s, bb=b, mm=m, o=sl: s.activation(
                    out=sp(o, LC), in_=pb[bb][0:128, 0:LC], func=AF.Relu,
                    bias=colv[:, 32 + mm:33 + mm]))
                sab = P.op("scalar", lambda s, bb=b, mm=m, o=sl: s.activation(
                    out=sp(o + 256, LC), in_=pb[bb][0:128, 0:LC], func=AF.Abs,
                    bias=colv[:, 32 + mm:33 + mm]), inc="sS")
                bank_free[b] = ("sS", sab)
                P.op("scalar", lambda s, o=sl: s.activation(
                    out=sp(o + 512, LC), in_=sp(o + 256, LC), func=AF.Exp,
                    scale=np.float32(-1.0)))
                sln = P.op("scalar", lambda s, o=sl: s.activation(
                    out=sp(o + 768, LC), in_=sp(o + 512, LC), func=AF.Ln,
                    bias=np.float32(1.0)), inc="sS")
                P.wait("vector", "sS", sln)
                vdt2 = P.op("vector", lambda v, mm=m, o=sl: v.tensor_tensor(
                    out=dtTt(mm), in0=sp(o, LC), in1=sp(o + 768, LC), op=ALU.add), inc="sV")
                vdt2_marks[m % 2] = ("sV", vdt2)
                if m == FDI - 1:
                    whalf_free[dslot] = (f"sT{b}", tdm)
            bar()
            if "dt" in dbg:
                dbg_pre()
                for d in range(FDI):
                    dma("gpsimd", dbg_h["dt"][d * 128:(d + 1) * 128, tok0:tok0 + LC],
                        dtTt(d), sem="sO")
            dbg_guard()
            for d in range(FDI):
                P.op("scalar", lambda s, dd=d: s.activation(
                    out=ya(dd), in_=xssm(dd), func=AF.Copy, scale=colv[:, 48 + dd:49 + dd]))
            for d in range(FDI):
                P.op("vector", lambda v, dd=d: v.tensor_tensor(
                    out=dtx(dd), in0=dtTt(dd), in1=xssm(dd), op=ALU.mult))
            bar()

            def pq0(i):
                return hiddenT[:, i * 128:(i + 1) * 128] if i < 16 else \
                    sT_[:, (i - 16) * 128:(i - 15) * 128]

            def pq1(i):
                return arena(4160 + i * 128, 128)

            def hq1t(i):
                return sT_[:, i * 128:(i + 1) * 128] if i < 16 else \
                    dtT[:, (i - 16) * 128:(i - 15) * 128]

            def hqt(i):
                return xnW[:, i * 128:(i + 1) * 128] if i < 16 else \
                    arena(6976 + (i - 16) * 128, 128)

            def hq_home(off, n_):
                if off < 2048:
                    return xnW[:, off:off + n_]
                return arena(6976 + (off - 2048), n_)

            HQ_SLICES = ((0, 1024), (1024, 1024), (2048, 768))

            F_DA, F_U, F_H, F_P, F_CBS = 0, 1028, 2056, 3084, 4108
            for buf in range(2):
                P.op("vector", lambda v, b_=buf: v.memset(
                    SP[:, F_DA + b_ * 514:F_DA + b_ * 514 + 514:257], 0.0))
            scan_marks = [None, None]
            ymark = [None, None]
            pmark = [None, None]
            cbs_mark = [None, None]
            hs_mark = [None, None]
            hs_pend = None
            P.wait("gpsimd", "sS", P.cnt["sS"])

            gsm_q = {}
            fp4_steps = []
            fp4_state = {}

            def fp4_make_steps(mp):
                def s_load():
                    fp4_state[(mp, "ldg")] = load_pair(wqtg, mp)
                    fp4_state[(mp, "ldu")] = load_pair(wqtu, mp)

                def s_gate(sub):
                    m = mp + sub
                    gs = r2bcS if m % 2 == 0 else r3bcS
                    b = m % 2
                    te_bank(b)
                    tm = pair_mm(fp4_state[(mp, "ldg")], sub, b,
                                 lambda k: xqT[:, k * LC:(k + 1) * LC])
                    P.waitm("scalar", tm)
                    P.waitm("scalar", gsm_q.get(m % 2))
                    sg_ = P.op("scalar", lambda s, bb=b, g_=gs: s.activation(
                        out=g_[:], in_=pb[bb][0:128, 0:LC], func=AF.Silu), inc="sS")
                    bank_free[b] = ("sS", sg_)
                    fp4_state[(m, "sg")] = sg_

                def s_up(sub):
                    m = mp + sub
                    b2 = 2 + (m % 2)
                    te_bank(b2)
                    fp4_state[(m, "tm2")] = pair_mm(
                        fp4_state[(mp, "ldu")], sub, b2,
                        lambda k: xqT[:, k * LC:(k + 1) * LC])

                def s_prod(sub):
                    m = mp + sub
                    gs = r2bcS if m % 2 == 0 else r3bcS
                    b2 = 2 + (m % 2)
                    P.waitm("vector", fp4_state[(m, "tm2")])
                    P.wait("vector", "sS", fp4_state[(m, "sg")])
                    P.op("vector", lambda v, bb=b2, g_=gs, mm=m: v.tensor_tensor(
                        out=pq0(mm), in0=g_[:, 0:128],
                        in1=pb[bb][0:128, 0:128], op=ALU.mult))
                    vp1 = P.op("vector", lambda v, bb=b2, g_=gs, mm=m: v.tensor_tensor(
                        out=pq1(mm), in0=g_[:, 128:256],
                        in1=pb[bb][0:128, 128:256], op=ALU.mult))
                    bank_free[b2] = ("sV", vp1)
                    gsm_q[m % 2] = ("sV", vp1)
                    fp4_state["vp_last"] = vp1

                fp4_steps.append(s_load)
                fp4_steps.append(lambda: s_gate(0))
                fp4_steps.append(lambda: (s_up(0), s_prod(0)))
                fp4_steps.append(lambda: s_gate(1))
                fp4_steps.append(lambda: (s_up(1), s_prod(1)))

            for mp_ in range(0, FQ, 2):
                fp4_make_steps(mp_)
            fp4_sched = {}
            tot_slots = 14 * 8
            for si in range(len(fp4_steps)):
                fp4_sched.setdefault(si * tot_slots // len(fp4_steps), []).append(si)

            tr0_last = None
            tr1_last = None
            for n in range(DS):
                bp = 4 + (n % 2)
                te_bank(bp)
                tbm = P.op("tensor", lambda e, nn=n, bb=bp: e.matmul(
                    pb[bb][0:128, 0:LC], rowsel[:, nn * 128:(nn + 1) * 128],
                    BmT[0:16, 0:LC], start=True, stop=True), inc=f"sT{bp}")
                te_bank(6)
                tc = P.op("tensor", lambda e, nn=n: e.matmul(
                    pb[6][0:128, 0:LC], rowsel[:, nn * 128:(nn + 1) * 128],
                    CmT[0:16, 0:LC], start=True, stop=True), inc="sT6")
                P.wait("vector", f"sT{bp}", tbm)
                P.wait("scalar", "sT6", tc)
                P.waitm("scalar", cbs_mark[n % 2])
                scb = P.op("scalar", lambda s, nn=n: s.copy(
                    out=sp(F_CBS + (nn % 2) * 256, LC), in_=pb[6][0:128, 0:LC]))
                bank_free[6] = ("sS", scb)
                ulast = None
                for j in range(FDI // 2):
                    pj = n * 8 + j
                    buf = pj % 2
                    dab = sp(F_DA + buf * 514, 514)
                    ub = sp(F_U + buf * 514, 514)
                    hb = sp(F_H + buf * 514, 514)
                    pbuf = sp(F_P + buf * 512, 512)
                    P.waitm("scalar", scan_marks[buf])
                    P.op("scalar", lambda s, a=ub, nn=n, jj=j: s.copy(
                        out=a[:, 0:514:257],
                        in_=hstate[:, nn * FDI + 2 * jj:nn * FDI + 2 * jj + 2]))
                    for hh in range(2):
                        d = 2 * j + hh
                        sda = P.op("scalar", lambda s, dd=d, nn=n, a=dab, o=1 + 257 * hh:
                                   s.activation(
                                       out=a[:, o:o + LC], in_=dtTt(dd), func=AF.Exp,
                                       scale=aneg[:, dd * 16 + nn:dd * 16 + nn + 1]), inc="sS")
                    if hs_pend is not None:
                        hs_pend()
                        hs_pend = None
                    ulast = P.op("vector", lambda v, a=ub, jj=j, bb=bp: v.tensor_tensor(
                        out=a.rearrange("p (two l) -> p two l", l=257)[:, :, 1:257],
                        in0=arena(A_XPAD + 2 * jj * 256, 512).rearrange(
                            "p (two l) -> p two l", l=256),
                        in1=pb[bb][0:128, 0:LC].unsqueeze(1).broadcast_to([128, 2, LC]),
                        op=ALU.mult))
                    P.wait("vector", "sS", sda)
                    P.waitm("vector", pmark[buf])
                    P.waitm("vector", hs_mark[buf])
                    sc_ = P.op("vector", lambda v, a=hb, d0=dab, d1=ub: v.tensor_tensor_scan(
                        out=a, data0=d0, data1=d1, initial=0.0,
                        op0=ALU.mult, op1=ALU.add))
                    scan_marks[buf] = ("sV", sc_)

                    def _mk_hsave(a=hb, nn=n, jj=j, sv=sc_, bf=buf):
                        def emit():
                            P.wait("scalar", "sV", sv)
                            hs_ = P.op("scalar", lambda s: s.copy(
                                out=hstate[:, nn * FDI + 2 * jj:
                                           nn * FDI + 2 * jj + 2],
                                in_=a[:, 256:514:257]))
                            hs_mark[bf] = ("sS", hs_)
                        return emit
                    hs_pend = _mk_hsave()
                    P.wait("gpsimd", "sV", sc_)
                    P.wait("gpsimd", "sS", scb)
                    P.waitm("gpsimd", ymark[buf])
                    pg = P.op("gpsimd", lambda g, a=pbuf, hh_=hb, nn=n: g.tensor_tensor(
                        out=a.rearrange("p (two l) -> p two l", l=256),
                        in0=hh_.rearrange("p (two l) -> p two l", l=257)[:, :, 1:257],
                        in1=sp(F_CBS + (nn % 2) * 256, LC).unsqueeze(1).broadcast_to(
                            [128, 2, LC]),
                        op=ALU.mult))
                    pmark[buf] = ("sP", pg)
                    if j % 2 == 0:
                        yv = P.op("gpsimd", lambda g, a=pbuf, jj=j: g.tensor_tensor(
                            out=arena(A_Y + 2 * jj * 256, 512),
                            in0=arena(A_Y + 2 * jj * 256, 512), in1=a, op=ALU.add))
                        ymark[buf] = ("sP", yv)
                    else:
                        P.wait("vector", "sP", pg)
                        yv = P.op("vector", lambda v, a=pbuf, jj=j: v.tensor_tensor(
                            out=arena(A_Y + 2 * jj * 256, 512),
                            in0=arena(A_Y + 2 * jj * 256, 512), in1=a, op=ALU.add))
                        ymark[buf] = ("sV", yv)
                    for si in fp4_sched.get(n * 8 + j, ()):
                        fp4_steps[si]()
                bank_free[bp] = ("sV", ulast)
                cbs_mark[n % 2] = pmark[(n * 8 + 7) % 2]
                if n == 15:
                    if hs_pend is not None:
                        hs_pend()
                        hs_pend = None
                    for i in range(FQ):
                        bt = (7, 0, 1, 2, 3)[i % 5]
                        te_bank(bt)
                        t = P.op("tensor", lambda e, ii=i, bb=bt: e.transpose(
                            pb[bb][0:128, 0:128], pq1(ii), sident[:]), inc=f"sT{bt}")
                        P.wait("scalar", f"sT{bt}", t)
                        sc = P.op("scalar", lambda s, ii=i, bb=bt: s.copy(
                            out=hq1t(ii), in_=pb[bb][0:128, 0:128]), inc="sS")
                        bank_free[bt] = ("sS", sc)
                        tr1_last = ("sS", sc)
                if n == 14:
                    P.wait("tensor", "sV", fp4_state["vp_last"])
                    for i in range(FQ):
                        bt = (7, 0, 1, 2, 3)[i % 5]
                        te_bank(bt)
                        t = P.op("tensor", lambda e, ii=i, bb=bt: e.transpose(
                            pb[bb][0:128, 0:128], pq0(ii), sident[:]), inc=f"sT{bt}")
                        P.wait("scalar", f"sT{bt}", t)
                        sc = P.op("scalar", lambda s, ii=i, bb=bt: s.copy(
                            out=hqt(ii), in_=pb[bb][0:128, 0:128]), inc="sS")
                        bank_free[bt] = ("sS", sc)
                        tr0_last = ("sS", sc)

            if hs_pend is not None:
                hs_pend()
                hs_pend = None

            def hqT0(i):
                return arena(768 + i * 128, 128)
            if "y" in dbg:
                dbg_pre()
                for d in range(FDI):
                    dma("gpsimd", dbg_h["y"][d * 128:(d + 1) * 128, tok0:tok0 + LC],
                        ya(d), sem="sO")
                dbg_guard()
            P.wait("vector", "sP", P.cnt["sP"])
            vg = None
            for d in range(FDI):
                vg = P.op("vector", lambda v, dd=d: v.tensor_tensor(
                    out=ya(dd), in0=ya(dd), in1=szt(dd), op=ALU.mult), inc="sV")
            P.wait("tensor", "sV", vg)
            def hq1_home(off, w_):
                return sT_[:, off:off + w_] if off < 2048 else \
                    dtT[:, off - 2048:off - 2048 + w_]
            gq1 = None
            P.wait("vector", "sS", tr1_last[1])
            for (off, w_) in HQ_SLICES:
                nb = w_ // 16
                gq1 = ("sV", emit_quant(
                    hq1_home(off, w_), w_, nb,
                    (sp(QW_AV, w_), sp(QW_ACC, w_), sp(QW_AC2, w_),
                     sp(QW_X, w_), sp(QW_SG, w_)),
                    (arena(0, nb), arena(256, nb), arena(512, nb))))
            for mo in range(FDM):
                half = whalf_ctr[0] % 3
                whalf_ctr[0] += 1
                P.waitm("sync", whalf_free[half]); whalf_free[half] = None
                wg = dma("sync", wbuf[:, half * 2048:half * 2048 + 2048]
                         .rearrange("p (k j) -> p k j", j=128),
                         wout[:, mo * 128:(mo + 1) * 128]
                         .rearrange("(k p) j -> p k j", p=128),
                         sem=f"sW{half}")
                b = mo % 7
                te_bank(b)
                P.wait("tensor", f"sW{half}", wg)
                tm = None
                for k in range(FDI):
                    tm = P.op("tensor", lambda e, kk=k, hh=half, bb=b: e.matmul(
                        pb[bb][0:128, 0:LC],
                        wbuf[:, hh * 2048 + kk * 128: hh * 2048 + (kk + 1) * 128],
                        ya(kk), start=(kk == 0), stop=(kk == FDI - 1)),
                        inc=(f"sT{b}" if k == FDI - 1 else None))
                whalf_free[half] = (f"sT{b}", tm)
                P.wait("vector", f"sT{b}", tm)
                vh = P.op("vector", lambda v, m=mo, bb=b: v.tensor_tensor(
                    out=hiddenT[:, m * LC:(m + 1) * LC], in0=pb[bb][0:128, 0:LC],
                    in1=xnT[:, m * LC:(m + 1) * LC], op=ALU.add), inc="sV")
                bank_free[b] = ("sV", vh)
            vq0 = None
            for (off, w_) in HQ_SLICES:
                nb = w_ // 16
                vq0 = emit_quant(
                    hq_home(off, w_), w_, nb,
                    (sp(QW_AV, w_), sp(QW_ACC, w_), sp(QW_AC2, w_),
                     sp(QW_X, w_), sp(QW_SG, w_)),
                    (arena(0, nb), arena(256, nb), arena(512, nb)))
            P.waitm("tensor", gq1)
            for i in range(FQ):
                bt = (7, 0, 1)[i % 3]
                te_bank(bt)
                t = P.op("tensor", lambda e, ii=i, bb=bt: e.transpose(
                    pb[bb][0:128, 0:128], hq1t(ii), sident[:]), inc=f"sT{bt}")
                P.wait("scalar", f"sT{bt}", t)
                sc = P.op("scalar", lambda s, ii=i, bb=bt: s.copy(
                    out=pq1(ii), in_=pb[bb][0:128, 0:128]), inc="sS")
                bank_free[bt] = ("sS", sc)
            P.wait("tensor", "sV", vq0)
            for i in range(FQ):
                bt = (7, 0, 1)[i % 3]
                te_bank(bt)
                t = P.op("tensor", lambda e, ii=i, bb=bt: e.transpose(
                    pb[bb][0:128, 0:128], hqt(ii), sident[:]), inc=f"sT{bt}")
                P.wait("scalar", f"sT{bt}", t)
                sc = P.op("scalar", lambda s, ii=i, bb=bt: s.copy(
                    out=hqT0(ii), in_=pb[bb][0:128, 0:128]), inc="sS")
                bank_free[bt] = ("sS", sc)
            P.wait("tensor", "sS", P.cnt["sS"])
            for mo in range(FDM):
                b = 4 + (mo % 3)
                te_bank(b)
                tmv = None
                for seg in range(2):
                    k0, k1 = seg * 11, seg * 11 + 11
                    slot = whalf_ctr[0] % 3
                    whalf_ctr[0] += 1
                    P.waitm("sync", whalf_free[slot]); whalf_free[slot] = None
                    so = slot * 2048
                    wgx = dma("sync", wbuf[:, so:so + 11 * 128]
                              .rearrange("p (k j) -> p k j", j=128),
                              wqtd[k0 * 128:k1 * 128, mo * 128:(mo + 1) * 128]
                              .rearrange("(k p) j -> p k j", p=128),
                              sem=f"sW{slot}")
                    P.wait("tensor", f"sW{slot}", wgx)
                    for k in range(k0, k1):
                        P.op("tensor", lambda e, kk=k, o=so, kz=k0, bb=b: e.matmul(
                            pb[bb][0:128, 0:128],
                            wbuf[:, o + (kk - kz) * 128: o + (kk - kz + 1) * 128],
                            hqT0(kk), start=(kk == 0), stop=False))
                        tmv = P.op("tensor", lambda e, kk=k, o=so, kz=k0, bb=b: e.matmul(
                            pb[bb][0:128, 128:256],
                            wbuf[:, o + (kk - kz) * 128: o + (kk - kz + 1) * 128],
                            pq1(kk), start=False, stop=(kk == FQ - 1)),
                            inc=(f"sT{b}" if k == k1 - 1 else None))
                    whalf_free[slot] = (f"sT{b}", tmv)
                P.waitm("scalar", (f"sT{b}", tmv))
                sg_ = P.op("scalar", lambda s, m=mo, bb=b: s.activation(
                    out=dtT[:, 1024 + m * 256:1024 + (m + 1) * 256],
                    in_=pb[bb][0:128, 0:256], func=AF.Silu), inc="sS")
                bank_free[b] = ("sS", sg_)
            bar()
            if "g" in dbg:
                dbg_pre()
                for mo in range(FDM):
                    for hh in range(2):
                        dma("gpsimd",
                            dbg_h["g"][mo * 128:(mo + 1) * 128,
                                       tok0 + hh * 128:tok0 + hh * 128 + 128],
                            dtT[:, 1024 + mo * 256 + hh * 128:
                                1024 + mo * 256 + hh * 128 + 128],
                            sem="sO")
            dbg_guard()
            if "hid" in dbg:
                dbg_pre()
                for mo in range(FDM):
                    dma("gpsimd", dbg_h["hid"][mo * 128:(mo + 1) * 128, tok0:tok0 + LC],
                        hiddenT[:, mo * LC:(mo + 1) * LC], sem="sO")
            dbg_guard()

            te_bank(5)
            slot_mark = {}
            for fi in range(FDM):
                sl = fi % 2
                P.waitm("scalar", slot_mark.get(sl))
                ssq = P.op("scalar", lambda s, f=fi, o=sl: s.activation(
                    out=sp(o * 256, LC), in_=hiddenT[:, f * LC:(f + 1) * LC],
                    func=AF.Square), inc="sS")
                P.wait("tensor", "sS", ssq)
                tm = P.op("tensor", lambda e, o=sl, f=fi: e.matmul(
                    pb[5][0:1, 0:LC], onescol[:, 0:1], sp(o * 256, LC),
                    start=(f == 0), stop=(f == FDM - 1)), inc="sT5")
                slot_mark[sl] = ("sT5", tm)
            P.wait("scalar", "sT5", P.cnt["sT5"])
            sm3 = P.op("scalar", lambda s: s.activation(
                out=rows[0:1, RW_A:RW_A + LC], in_=pb[5][0:1, 0:LC], func=AF.Sqrt,
                scale=np.float32(1.0 / DM), bias=np.float32(1e-6)), inc="sS")
            P.wait("vector", "sS", sm3)
            vr3 = P.op("vector", lambda v: v.reciprocal(
                out=rows[0:1, RW_B:RW_B + LC], in_=rows[0:1, RW_A:RW_A + LC]), inc="sV")
            P.wait("tensor", "sV", vr3)
            tbc3 = P.op("tensor", lambda e: e.matmul(
                pb[5][0:128, 0:LC], onesrow[0:1, 0:128], rows[0:1, RW_B:RW_B + LC],
                start=True, stop=True), inc="sT5")
            P.wait("scalar", "sT5", tbc3)
            sbc3 = P.op("scalar", lambda s: s.copy(out=r3bcS[:], in_=pb[5][0:128, 0:LC]), inc="sS")
            P.wait("vector", "sS", sbc3)
            vhn = None
            for m in range(FDM):
                vhn = P.op("vector", lambda v, mm=m: v.scalar_tensor_tensor(
                    out=arena(A_HNT + mm * LC, LC), in0=hiddenT[:, mm * LC:(mm + 1) * LC],
                    scalar=colv[:, 8 + mm:9 + mm], in1=r3bcS[:],
                    op0=ALU.mult, op1=ALU.mult), inc="sV")
            P.wait("tensor", "sV", vhn)


            gsm = {}
            for mp in range(0, FFF, 2):
                ldg = load_pair(gate_w, mp)
                ldu = load_pair(up_w, mp)
                for sub in range(2):
                    m = mp + sub
                    b = m % 2
                    te_bank(b)
                    tm = pair_mm(ldg, sub, b, lambda k: arena(A_HNT + k * LC, LC))
                    P.waitm("scalar", tm)
                    sl = 512 + (m % 2) * 256
                    P.waitm("scalar", gsm.get(m % 2))
                    sg_ = P.op("scalar", lambda s, bb=b, o=sl: s.activation(
                        out=sp(o, LC), in_=pb[bb][0:128, 0:LC], func=AF.Silu), inc="sS")
                    bank_free[b] = ("sS", sg_)
                    b2 = 2 + (m % 2)
                    te_bank(b2)
                    tm2 = pair_mm(ldu, sub, b2, lambda k: arena(A_HNT + k * LC, LC))
                    P.waitm("vector", tm2)
                    P.wait("vector", "sS", sg_)
                    vp = P.op("vector", lambda v, bb=b2, o=sl, mm=m: v.tensor_tensor(
                        out=arena(A_PROD + mm * LC, LC), in0=sp(o, LC),
                        in1=pb[bb][0:128, 0:LC], op=ALU.mult), inc="sV")
                    bank_free[b2] = ("sV", vp)
                    gsm[m % 2] = ("sV", vp)
            P.wait("tensor", "sV", P.cnt["sV"])
            for mo in range(FDM):
                b = mo % 4
                te_bank(b)
                tm = stream_mm(down_w, mo, FFF, lambda k: arena(A_PROD + k * LC, LC), b)
                P.waitm("vector", tm)
                vd_ = P.op("vector", lambda v, m=mo, bb=b: v.tensor_tensor(
                    out=sT_[:, m * LC:(m + 1) * LC], in0=pb[bb][0:128, 0:LC],
                    in1=hiddenT[:, m * LC:(m + 1) * LC], op=ALU.add), inc="sV")
                bank_free[b] = ("sV", vd_)
            bar()
            if "s" in dbg:
                dbg_pre()
                for mo in range(FDM):
                    dma("gpsimd", dbg_h["s"][mo * 128:(mo + 1) * 128, tok0:tok0 + LC],
                        sT_[:, mo * LC:(mo + 1) * LC], sem="sO")
            dbg_guard()

            vpr = None
            for mo in range(FDM):
                vpr = P.op("vector", lambda v, m=mo: v.tensor_tensor(
                    out=arena(A_PR + m * LC, LC), in0=sT_[:, m * LC:(m + 1) * LC],
                    in1=dtT[:, 1024 + m * 256:1024 + (m + 1) * 256],
                    op=ALU.mult), inc="sV")
            P.wait("tensor", "sV", vpr)
            vo = None
            for mo in range(FDM):
                for tt in range(NTT):
                    b = 6 + ((mo * NTT + tt) % 2)
                    te_bank(b)
                    t = P.op("tensor", lambda e, m=mo, t_=tt, bb=b: e.transpose(
                        pb[bb][0:128, 0:128], arena(A_PR + m * LC + t_ * 128, 128),
                        sident[:]), inc=f"sT{b}")
                    P.wait("vector", f"sT{b}", t)
                    vo = P.op("vector", lambda v, m=mo, t_=tt, bb=b: v.tensor_tensor(
                        out=xnW[:, t_ * DM + m * 128: t_ * DM + (m + 1) * 128],
                        in0=pb[bb][0:128, 0:128],
                        in1=x_t[:, t_ * DM + m * 128: t_ * DM + (m + 1) * 128],
                        op=ALU.add), inc="sV")
                    bank_free[b] = ("sV", vo)
            P.wait("gpsimd", "sV", vo)
            for tt in range(NTT):
                dma("gpsimd", out_h[tok0 + tt * 128: tok0 + (tt + 1) * 128, :],
                    xnW[:, tt * DM:(tt + 1) * DM], sem="sO")
            last_combine = ("sV", vo)
            bar()

        P.wait("gpsimd", "sO", P.cnt["sO"])
        P.wait("gpsimd", "sG", P.cnt["sG"])
        P.op("gpsimd", lambda g: g.nop())

        def runner(eng_name):
            self_sem = SELF_SEM.get(eng_name)

            def run(eng):
                idx = 0
                for e in P.ev[eng_name]:
                    if e[0] == "auto":
                        if idx > 0:
                            eng.wait_ge(sems[self_sem], idx)
                        r = e[1](eng)
                        if r is None:
                            r = eng.nop()
                        r.then_inc(sems[self_sem], 1)
                        idx += 1
                    elif e[0] == "op":
                        e[1](eng)
                    elif e[0] == "inc":
                        e[1](eng).then_inc(sems[e[2]], e[3])
                    else:
                        eng.wait_ge(sems[e[1]], e[2])
            return run

        block.tensor(runner("tensor"))
        block.vector(runner("vector"))
        block.scalar(runner("scalar"))
        block.sync(runner("sync"))
        block.gpsimd(runner("gpsimd"))

    return nc


def make_inputs(inp, b):
    m = {"x": np.ascontiguousarray(np.asarray(inp["L0_attn_hidden_states"])[b])}
    names = {
        "in_proj_w": "L0_attn_in_proj_w", "conv_w": "L0_attn_conv_w",
        "conv_b": "L0_attn_conv_b", "x_proj_w": "L0_attn_x_proj_w",
        "dt_proj_w": "L0_attn_dt_proj_w", "dt_proj_b": "L0_attn_dt_proj_b",
        "dt_ln_w": "L0_attn_dt_ln_w", "b_ln_w": "L0_attn_b_ln_w",
        "c_ln_w": "L0_attn_c_ln_w", "A_log": "L0_attn_A_log",
        "D_skip": "L0_attn_D_skip", "out_proj_w": "L0_attn_out_proj_w",
        "input_ln_w": "L0_attn_input_ln_w", "pre_moe_ln_w": "L0_attn_pre_moe_ln_w",
        "gate_w": "L0_attn_gate_w", "up_w": "L0_attn_up_w",
        "down_w": "L0_attn_down_w", "ffn_w_gate": "L0_ffn_w_gate",
        "ffn_w_up": "L0_ffn_w_up", "ffn_w_down": "L0_ffn_w_down",
    }
    for k, src in names.items():
        m[k] = np.ascontiguousarray(np.asarray(inp[src]))
    return m

def make_inputs(inp, b):
    m = {"x": np.ascontiguousarray(np.asarray(inp["L0_attn_hidden_states"])[b])}
    names = {
        "in_proj_w": "L0_attn_in_proj_w", "conv_w": "L0_attn_conv_w",
        "conv_b": "L0_attn_conv_b", "x_proj_w": "L0_attn_x_proj_w",
        "dt_proj_w": "L0_attn_dt_proj_w", "dt_proj_b": "L0_attn_dt_proj_b",
        "dt_ln_w": "L0_attn_dt_ln_w", "b_ln_w": "L0_attn_b_ln_w",
        "c_ln_w": "L0_attn_c_ln_w", "A_log": "L0_attn_A_log",
        "D_skip": "L0_attn_D_skip", "out_proj_w": "L0_attn_out_proj_w",
        "input_ln_w": "L0_attn_input_ln_w", "pre_moe_ln_w": "L0_attn_pre_moe_ln_w",
        "gate_w": "L0_attn_gate_w", "up_w": "L0_attn_up_w",
        "down_w": "L0_attn_down_w", "ffn_w_gate": "L0_ffn_w_gate",
        "ffn_w_up": "L0_ffn_w_up", "ffn_w_down": "L0_ffn_w_down",
    }
    for k, src in names.items():
        m[k] = np.ascontiguousarray(np.asarray(inp[src]))
    return m

def make_inputs(inp, b):
    m = {"x": np.ascontiguousarray(np.asarray(inp["L0_attn_hidden_states"])[b])}
    names = {
        "in_proj_w": "L0_attn_in_proj_w", "conv_w": "L0_attn_conv_w",
        "conv_b": "L0_attn_conv_b", "x_proj_w": "L0_attn_x_proj_w",
        "dt_proj_w": "L0_attn_dt_proj_w", "dt_proj_b": "L0_attn_dt_proj_b",
        "dt_ln_w": "L0_attn_dt_ln_w", "b_ln_w": "L0_attn_b_ln_w",
        "c_ln_w": "L0_attn_c_ln_w", "A_log": "L0_attn_A_log",
        "D_skip": "L0_attn_D_skip", "out_proj_w": "L0_attn_out_proj_w",
        "input_ln_w": "L0_attn_input_ln_w", "pre_moe_ln_w": "L0_attn_pre_moe_ln_w",
        "gate_w": "L0_attn_gate_w", "up_w": "L0_attn_up_w",
        "down_w": "L0_attn_down_w", "ffn_w_gate": "L0_ffn_w_gate",
        "ffn_w_up": "L0_ffn_w_up", "ffn_w_down": "L0_ffn_w_down",
    }
    for k, src in names.items():
        m[k] = np.ascontiguousarray(np.asarray(inp[src]))
    return m

def make_inputs(inp, b):
    m = {"x": np.ascontiguousarray(np.asarray(inp["L0_attn_hidden_states"])[b])}
    names = {
        "in_proj_w": "L0_attn_in_proj_w", "conv_w": "L0_attn_conv_w",
        "conv_b": "L0_attn_conv_b", "x_proj_w": "L0_attn_x_proj_w",
        "dt_proj_w": "L0_attn_dt_proj_w", "dt_proj_b": "L0_attn_dt_proj_b",
        "dt_ln_w": "L0_attn_dt_ln_w", "b_ln_w": "L0_attn_b_ln_w",
        "c_ln_w": "L0_attn_c_ln_w", "A_log": "L0_attn_A_log",
        "D_skip": "L0_attn_D_skip", "out_proj_w": "L0_attn_out_proj_w",
        "input_ln_w": "L0_attn_input_ln_w", "pre_moe_ln_w": "L0_attn_pre_moe_ln_w",
        "gate_w": "L0_attn_gate_w", "up_w": "L0_attn_up_w",
        "down_w": "L0_attn_down_w", "ffn_w_gate": "L0_ffn_w_gate",
        "ffn_w_up": "L0_ffn_w_up", "ffn_w_down": "L0_ffn_w_down",
    }
    for k, src in names.items():
        m[k] = np.ascontiguousarray(np.asarray(inp[src]))
    return m

def make_inputs(inp, b):
    m = {"x": np.ascontiguousarray(np.asarray(inp["L0_attn_hidden_states"])[b])}
    names = {
        "in_proj_w": "L0_attn_in_proj_w", "conv_w": "L0_attn_conv_w",
        "conv_b": "L0_attn_conv_b", "x_proj_w": "L0_attn_x_proj_w",
        "dt_proj_w": "L0_attn_dt_proj_w", "dt_proj_b": "L0_attn_dt_proj_b",
        "dt_ln_w": "L0_attn_dt_ln_w", "b_ln_w": "L0_attn_b_ln_w",
        "c_ln_w": "L0_attn_c_ln_w", "A_log": "L0_attn_A_log",
        "D_skip": "L0_attn_D_skip", "out_proj_w": "L0_attn_out_proj_w",
        "input_ln_w": "L0_attn_input_ln_w", "pre_moe_ln_w": "L0_attn_pre_moe_ln_w",
        "gate_w": "L0_attn_gate_w", "up_w": "L0_attn_up_w",
        "down_w": "L0_attn_down_w", "ffn_w_gate": "L0_ffn_w_gate",
        "ffn_w_up": "L0_ffn_w_up", "ffn_w_down": "L0_ffn_w_down",
    }
    for k, src in names.items():
        m[k] = np.ascontiguousarray(np.asarray(inp[src]))
    return m

def make_inputs(inp, b):
    m = {"x": np.ascontiguousarray(np.asarray(inp["L0_attn_hidden_states"])[b])}
    names = {
        "in_proj_w": "L0_attn_in_proj_w", "conv_w": "L0_attn_conv_w",
        "conv_b": "L0_attn_conv_b", "x_proj_w": "L0_attn_x_proj_w",
        "dt_proj_w": "L0_attn_dt_proj_w", "dt_proj_b": "L0_attn_dt_proj_b",
        "dt_ln_w": "L0_attn_dt_ln_w", "b_ln_w": "L0_attn_b_ln_w",
        "c_ln_w": "L0_attn_c_ln_w", "A_log": "L0_attn_A_log",
        "D_skip": "L0_attn_D_skip", "out_proj_w": "L0_attn_out_proj_w",
        "input_ln_w": "L0_attn_input_ln_w", "pre_moe_ln_w": "L0_attn_pre_moe_ln_w",
        "gate_w": "L0_attn_gate_w", "up_w": "L0_attn_up_w",
        "down_w": "L0_attn_down_w", "ffn_w_gate": "L0_ffn_w_gate",
        "ffn_w_up": "L0_ffn_w_up", "ffn_w_down": "L0_ffn_w_down",
    }
    for k, src in names.items():
        m[k] = np.ascontiguousarray(np.asarray(inp[src]))
    return m


_NC_CACHE = {}


def kernel(**inputs):
    from concourse.bass_utils import run_bass_kernel_spmd
    if "nc" not in _NC_CACHE:
        _NC_CACHE["nc"] = build_program(nchunks=8, dbg=())
    nc = _NC_CACHE["nc"]
    in_maps = [make_inputs(inputs, b) for b in range(8)]
    res = run_bass_kernel_spmd(nc, in_maps, core_ids=list(range(8)))
    return np.stack([np.asarray(r["out"]) for r in res.results], axis=0)


# version 17
# speedup vs baseline: 1.1415x; 1.0550x over previous
import numpy as np
import concourse.bass as bass
import concourse.mybir as mybir
from contextlib import ExitStack

F32 = mybir.dt.float32
AF = mybir.ActivationFunctionType
ALU = mybir.AluOpType
AX = mybir.AxisListType

T, DM, DI, DS, R, DFF, DQ, KC = 2048, 1024, 2048, 16, 64, 4096, 2816, 4
LC = 256
NTT = LC // 128
FDM, FDI, FFF, FQ = DM // 128, DI // 128, DFF // 128, DQ // 128
NBK_DM = DM // 16

CONSTS = {
    "th0": 0.25, "th1": 0.75, "th2": 1.25, "th3": 1.75, "th4": 2.5,
    "th5": 3.5, "th6": 5.0, "two": 2.0, "four": 4.0, "half": 0.5,
    "eps12": 1e-12, "inv1024": 1.0 / 1024.0, "six": 6.0,
}
C_IDX = {k: i for i, k in enumerate(CONSTS)}
C_T6 = len(CONSTS)

ENGS = ("tensor", "vector", "scalar", "sync", "gpsimd")


SELF_SEM = {"vector": "sV", "scalar": "sS", "gpsimd": "sP"}


class Plan:
    def __init__(self):
        self.ev = {e: [] for e in ENGS}
        self.cnt = {s: 0 for s in (["sV", "sS", "sW", "sWv", "sW0", "sW1", "sG", "sO", "sTB", "sVB", "sSB", "sX", "sP", "sW2", "sQ0", "sQ1"] + [f"sT{i}" for i in range(8)])}
        self.barn = 0

    def op(self, eng, fn, inc=None, n=1):
        if eng in SELF_SEM and (inc is None or inc == SELF_SEM[eng]):
            s = SELF_SEM[eng]
            self.cnt[s] += 1
            self.ev[eng].append(("auto", fn))
            return self.cnt[s]
        if inc is None:
            self.ev[eng].append(("op", fn))
            return None
        self.cnt[inc] += n
        self.ev[eng].append(("inc", fn, inc, n))
        return self.cnt[inc]

    def wait(self, eng, sem, val):
        if val:
            self.ev[eng].append(("wait", sem, val))

    def waitm(self, eng, marker):
        if marker is not None:
            self.wait(eng, marker[0], marker[1])


def build_program(nchunks=8, dbg=()):
    nc = bass.Bass("TRN2", target_bir_lowering=False)
    dbg = set(dbg)

    x_h = nc.dram_tensor("x", [T, DM], F32, kind="ExternalInput").ap()
    w_in = nc.dram_tensor("in_proj_w", [DM, 2 * DI], F32, kind="ExternalInput").ap()
    conv_w = nc.dram_tensor("conv_w", [DI, KC], F32, kind="ExternalInput").ap()
    conv_b = nc.dram_tensor("conv_b", [DI], F32, kind="ExternalInput").ap()
    xproj_w = nc.dram_tensor("x_proj_w", [DI, R + 2 * DS], F32, kind="ExternalInput").ap()
    dtproj_w = nc.dram_tensor("dt_proj_w", [R, DI], F32, kind="ExternalInput").ap()
    dtproj_b = nc.dram_tensor("dt_proj_b", [DI], F32, kind="ExternalInput").ap()
    dtln_w = nc.dram_tensor("dt_ln_w", [R], F32, kind="ExternalInput").ap()
    bln_w = nc.dram_tensor("b_ln_w", [DS], F32, kind="ExternalInput").ap()
    cln_w = nc.dram_tensor("c_ln_w", [DS], F32, kind="ExternalInput").ap()
    alog = nc.dram_tensor("A_log", [DI, DS], F32, kind="ExternalInput").ap()
    dskip = nc.dram_tensor("D_skip", [DI], F32, kind="ExternalInput").ap()
    wout = nc.dram_tensor("out_proj_w", [DI, DM], F32, kind="ExternalInput").ap()
    inln_w = nc.dram_tensor("input_ln_w", [DM], F32, kind="ExternalInput").ap()
    pmln_w = nc.dram_tensor("pre_moe_ln_w", [DM], F32, kind="ExternalInput").ap()
    gate_w = nc.dram_tensor("gate_w", [DM, DFF], F32, kind="ExternalInput").ap()
    up_w = nc.dram_tensor("up_w", [DM, DFF], F32, kind="ExternalInput").ap()
    down_w = nc.dram_tensor("down_w", [DFF, DM], F32, kind="ExternalInput").ap()
    fg_w = nc.dram_tensor("ffn_w_gate", [DQ, DM], F32, kind="ExternalInput").ap()
    fu_w = nc.dram_tensor("ffn_w_up", [DQ, DM], F32, kind="ExternalInput").ap()
    fd_w = nc.dram_tensor("ffn_w_down", [DM, DQ], F32, kind="ExternalInput").ap()
    out_h = nc.dram_tensor("out", [T, DM], F32, kind="ExternalOutput").ap()
    wqtg = nc.dram_tensor("wqtg", [DM, DQ], F32, kind="Internal").ap()
    wqtu = nc.dram_tensor("wqtu", [DM, DQ], F32, kind="Internal").ap()
    wqtd = nc.dram_tensor("wqtd", [DQ, DM], F32, kind="Internal").ap()

    dbg_h = {}
    def dbg_tensor(name, shape):
        if name in dbg:
            dbg_h[name] = nc.dram_tensor("d_" + name, shape, F32, kind="ExternalOutput").ap()
    dbg_tensor("xn", [T, DM]); dbg_tensor("xq", [T, DM])
    dbg_tensor("xssm", [DI, T]); dbg_tensor("sz", [DI, T])
    dbg_tensor("dt", [DI, T]); dbg_tensor("bm", [DS, T]); dbg_tensor("cm", [DS, T])
    dbg_tensor("y", [DI, T]); dbg_tensor("hid", [DM, T])
    dbg_tensor("s", [DM, T]); dbg_tensor("g", [DM, T]); dbg_tensor("hq", [T, DQ])

    ident_h = nc.inline_tensor(np.eye(128, dtype=np.float32), name="identc")
    rs = np.zeros((16, 16 * 128), dtype=np.float32)
    for n_ in range(16):
        rs[n_, n_ * 128:(n_ + 1) * 128] = 1.0
    rowsel_h = nc.inline_tensor(rs, name="rowselc")

    P = Plan()

    with ExitStack() as ctx:
        E = ctx.enter_context

        cpool = E(nc.sbuf_tensor("cpool", [128, 32], F32))
        colv = E(nc.sbuf_tensor("colv", [128, 64], F32))
        convw = E(nc.sbuf_tensor("convw", [128, 64], F32))
        aneg = E(nc.sbuf_tensor("aneg", [128, 256], F32))
        sident = E(nc.sbuf_tensor("sident", [128, 128], F32))
        rowsel = E(nc.sbuf_tensor("rowsel", [16, 16 * 128], F32))
        onesrow = E(nc.sbuf_tensor("onesrow", [1, 128], F32))
        onescol = E(nc.sbuf_tensor("onescol", [128, 1], F32))
        smallv = E(nc.sbuf_tensor("smallv", [64, 4], F32))
        vstage = E(nc.sbuf_tensor("vstage", [16, 128], F32))
        x_t = E(nc.sbuf_tensor("x_t", [128, NTT * DM], F32))
        xnT = E(nc.sbuf_tensor("xnT", [128, FDM * LC], F32))
        xnW = E(nc.sbuf_tensor("xnW", [128, FDM * LC], F32))
        xqT = E(nc.sbuf_tensor("xqT", [128, FDM * LC], F32))
        wbuf = E(nc.sbuf_tensor("wbuf", [128, 6144], F32))
        SP = E(nc.sbuf_tensor("SP", [128, 5120], F32))
        rows = E(nc.sbuf_tensor("rows", [1, 1024], F32))
        projS = E(nc.sbuf_tensor("projS", [64, LC], F32))
        projB = E(nc.sbuf_tensor("projB", [16, LC], F32))
        projC = E(nc.sbuf_tensor("projC", [16, LC], F32))
        dtrms = E(nc.sbuf_tensor("dtrms", [64, LC], F32))
        BmT = E(nc.sbuf_tensor("BmT", [16, LC], F32))
        CmT = E(nc.sbuf_tensor("CmT", [16, LC], F32))
        r2bcS = E(nc.sbuf_tensor("r2bcS", [128, LC], F32))
        r3bcS = E(nc.sbuf_tensor("r3bcS", [128, LC], F32))
        hiddenT = E(nc.sbuf_tensor("hiddenT", [128, FDM * LC], F32))
        sT_ = E(nc.sbuf_tensor("sT_", [128, FDM * LC], F32))
        szb = E(nc.sbuf_tensor("szb", [128, FDI * LC], F32))
        dtT = E(nc.sbuf_tensor("dtT", [128, FDI * LC], F32))
        hstate = E(nc.sbuf_tensor("hstate", [128, 256], F32))
        tails = E(nc.sbuf_tensor("tails", [128, 48], F32))
        xpw = E(nc.sbuf_tensor("xpw", [128, 16 * 96], F32))
        AR = E(nc.sbuf_tensor("AR", [128, 12352], F32))

        pb = [E(nc.psum_tensor(f"pb{i}", [128, 512], F32)) for i in range(8)]

        sems = {}
        for s in (["sV", "sS", "sW", "sWv", "sW0", "sW1", "sW2", "sG", "sO", "sTB", "sVB", "sSB", "sX", "sP", "sQ0", "sQ1"] + [f"sT{i}" for i in range(8)]):
            sems[s] = E(nc.semaphore(s))
        block = E(nc.Block())

        bank_free = [None] * 8
        whalf_free = [None, None, None]
        whalf_ctr = [0]

        def c(name):
            return cpool[:, C_IDX[name]:C_IDX[name] + 1]

        def arena(o, n):
            return AR[:, o:o + n]

        def sp(o, n):
            return SP[:, o:o + n]

        def bar(engs=("tensor", "vector", "scalar")):
            tvals = {f"sT{i}": P.cnt[f"sT{i}"] for i in range(8)}
            tb = P.op("tensor", lambda eng: eng.nop(), inc="sTB")
            vv = P.op("vector", lambda eng: eng.nop())
            ss = P.op("scalar", lambda eng: eng.nop())
            pp = P.op("gpsimd", lambda eng: eng.nop())
            P.wait("tensor", "sV", vv)
            P.wait("tensor", "sS", ss)
            P.wait("tensor", "sP", pp)
            for e in ("vector", "scalar"):
                for s_, val in tvals.items():
                    P.wait(e, s_, val)
                P.wait(e, "sTB", tb)
            P.wait("vector", "sS", ss)
            P.wait("vector", "sP", pp)
            P.wait("scalar", "sV", vv)
            P.wait("scalar", "sP", pp)
            P.wait("gpsimd", "sV", vv)
            P.wait("gpsimd", "sS", ss)
            for s_, val in tvals.items():
                P.wait("gpsimd", s_, val)
            P.wait("gpsimd", "sTB", tb)

        def te_bank(b):
            P.waitm("tensor", bank_free[b])
            bank_free[b] = None

        def dma(eng, out, in_, sem="sW"):
            return P.op(eng, lambda e, o=out, i=in_: e.dma_start(out=o, in_=i), inc=sem, n=16)

        def dbg_guard():
            v = P.cnt["sO"]
            for e in ("vector", "scalar", "tensor"):
                P.wait(e, "sO", v)
            dbg_post()

        def dbg_pre():
            P.wait("gpsimd", "sV", P.cnt["sV"])
            P.wait("gpsimd", "sS", P.cnt["sS"])
            P.wait("gpsimd", "sP", P.cnt["sP"])

        def dbg_post():
            vv = P.op("vector", lambda v: v.engine_nop())
            P.wait("gpsimd", "sV", vv)

        def emit_quant(xs, w_, nblk, wide, small):
            av, acc, ac2, c7s, sg = wide
            bm, scl, rcp = small
            x3 = xs.rearrange("p (b k) -> p b k", k=16)
            P.op("vector", lambda v: v.tensor_reduce(
                out=bm, in_=x3, axis=AX.X, op=ALU.max, apply_absolute_value=True))
            P.op("vector", lambda v: v.engine_nop())
            P.op("vector", lambda v: v.tensor_scalar(
                out=scl, in0=bm, scalar1=cpool[:, C_T6:C_T6 + 1], scalar2=c("eps12"),
                op0=ALU.mult, op1=ALU.max))
            P.op("vector", lambda v: v.engine_nop())
            P.op("vector", lambda v: v.reciprocal(out=rcp, in_=scl))
            P.op("vector", lambda v: v.engine_nop())
            rbc = rcp.unsqueeze(2).broadcast_to([128, nblk, 16])
            sbc = scl.unsqueeze(2).broadcast_to([128, nblk, 16])
            vq = P.op("vector", lambda v: v.tensor_tensor(
                out=av.rearrange("p (b k) -> p b k", k=16), in0=x3, in1=rbc,
                op=ALU.mult), inc="sV")
            P.wait("scalar", "sV", vq)
            P.op("scalar", lambda s: s.activation(out=sg, in_=av, func=AF.Sign))
            s2 = P.op("scalar", lambda s: s.activation(out=av, in_=av, func=AF.Abs), inc="sS")
            P.wait("vector", "sS", s2)
            P.op("vector", lambda v: v.tensor_single_scalar(
                out=acc, in_=av, scalar=c("th0"), op=ALU.is_gt))
            for th in ("th1", "th2", "th3"):
                P.op("vector", lambda v, t=th: v.scalar_tensor_tensor(
                    out=acc, in0=av, scalar=c(t), in1=acc, op0=ALU.is_gt, op1=ALU.add))
            P.op("vector", lambda v: v.tensor_single_scalar(
                out=ac2, in_=av, scalar=c("th4"), op=ALU.is_gt))
            P.op("vector", lambda v: v.scalar_tensor_tensor(
                out=ac2, in0=av, scalar=c("th5"), in1=ac2, op0=ALU.is_gt, op1=ALU.add))
            P.op("vector", lambda v: v.tensor_single_scalar(
                out=c7s, in_=av, scalar=c("th6"), op=ALU.is_gt))
            P.op("vector", lambda v: v.scalar_tensor_tensor(
                out=acc, in0=ac2, scalar=c("two"), in1=acc, op0=ALU.mult, op1=ALU.add))
            P.op("vector", lambda v: v.scalar_tensor_tensor(
                out=acc, in0=c7s, scalar=c("four"), in1=acc, op0=ALU.mult, op1=ALU.add))
            P.op("vector", lambda v: v.tensor_tensor(out=acc, in0=acc, in1=sg, op=ALU.mult))
            return P.op("vector", lambda v: v.scalar_tensor_tensor(
                out=xs.rearrange("p (b k) -> p b k", k=16),
                in0=acc.rearrange("p (b k) -> p b k", k=16),
                scalar=c("half"), in1=sbc, op0=ALU.mult, op1=ALU.mult), inc="sV")

        def emit_quant_gp(xs, w_, nblk, wide, small, pre=()):
            av, acc, ac2, c7s, sg = wide
            bm, scl, rcp = small
            x3 = xs.rearrange("p (b k) -> p b k", k=16)
            for (s_, v_) in pre:
                P.wait("gpsimd", s_, v_)
                if s_ != "sV":
                    P.wait("vector", s_, v_)
            P.op("vector", lambda v: v.tensor_reduce(
                out=bm, in_=x3, axis=AX.X, op=ALU.max, apply_absolute_value=True))
            P.op("vector", lambda v: v.engine_nop())
            P.op("vector", lambda v: v.tensor_scalar(
                out=scl, in0=bm, scalar1=cpool[:, C_T6:C_T6 + 1], scalar2=c("eps12"),
                op0=ALU.mult, op1=ALU.max))
            P.op("vector", lambda v: v.engine_nop())
            v1 = P.op("vector", lambda v: v.reciprocal(out=rcp, in_=scl))
            rbc = rcp.unsqueeze(2).broadcast_to([128, nblk, 16])
            sbc = scl.unsqueeze(2).broadcast_to([128, nblk, 16])
            P.wait("gpsimd", "sV", v1)
            gq = P.op("gpsimd", lambda g: g.tensor_tensor(
                out=av.rearrange("p (b k) -> p b k", k=16), in0=x3, in1=rbc,
                op=ALU.mult))
            P.wait("scalar", "sP", gq)
            P.op("scalar", lambda s: s.activation(out=sg, in_=av, func=AF.Sign))
            s2 = P.op("scalar", lambda s: s.activation(out=av, in_=av, func=AF.Abs), inc="sS")
            P.wait("gpsimd", "sS", s2)
            P.op("gpsimd", lambda g: g.tensor_single_scalar(
                out=acc, in_=av, scalar=c("th0"), op=ALU.is_gt))
            for th in ("th1", "th2", "th3"):
                P.op("gpsimd", lambda g, t=th: g.scalar_tensor_tensor(
                    out=acc, in0=av, scalar=c(t), in1=acc, op0=ALU.is_gt, op1=ALU.add))
            P.op("gpsimd", lambda g: g.tensor_single_scalar(
                out=ac2, in_=av, scalar=c("th4"), op=ALU.is_gt))
            P.op("gpsimd", lambda g: g.scalar_tensor_tensor(
                out=ac2, in0=av, scalar=c("th5"), in1=ac2, op0=ALU.is_gt, op1=ALU.add))
            P.op("gpsimd", lambda g: g.tensor_single_scalar(
                out=c7s, in_=av, scalar=c("th6"), op=ALU.is_gt))
            P.op("gpsimd", lambda g: g.scalar_tensor_tensor(
                out=acc, in0=ac2, scalar=c("two"), in1=acc, op0=ALU.mult, op1=ALU.add))
            P.op("gpsimd", lambda g: g.scalar_tensor_tensor(
                out=acc, in0=c7s, scalar=c("four"), in1=acc, op0=ALU.mult, op1=ALU.add))
            P.op("gpsimd", lambda g: g.tensor_tensor(out=acc, in0=acc, in1=sg, op=ALU.mult))
            gend = P.op("gpsimd", lambda g: g.scalar_tensor_tensor(
                out=xs.rearrange("p (b k) -> p b k", k=16),
                in0=acc.rearrange("p (b k) -> p b k", k=16),
                scalar=c("half"), in1=sbc, op0=ALU.mult, op1=ALU.mult))
            return ("sP", gend)

        def init_consts(vector):
            for k, v in CONSTS.items():
                vector.memset(cpool[:, C_IDX[k]:C_IDX[k] + 1], float(v))
            vector.memset(onescol[:], 1.0)
            vector.memset(onesrow[:], 1.0)
            vector.memset(hstate[:], 0.0)
            vector.memset(tails[:], 0.0)
        P.op("vector", init_consts)
        P.op("vector", lambda v: v.engine_nop())
        P.op("vector", lambda v: v.reciprocal(out=cpool[:, C_T6:C_T6 + 1], in_=c("six")))
        P.op("vector", lambda v: v.engine_nop(), inc="sV")

        dma("sync", sident[:], ident_h.ap()[:])
        dma("sync", rowsel[:], rowsel_h.ap()[:])
        for dt_i in range(FDI):
            dma("sync", convw[:, dt_i * 4:(dt_i + 1) * 4],
                conv_w[dt_i * 128:(dt_i + 1) * 128, :])
        for dt_i in range(FDI):
            dma("sync", arena(dt_i * 16, 16), alog[dt_i * 128:(dt_i + 1) * 128, :])
        w_res = dma("sync", xpw[:].rearrange("p (k j) -> p k j", j=96),
                    xproj_w[:].rearrange("(k p) j -> p k j", p=128))
        P.wait("tensor", "sW", w_res)
        P.wait("scalar", "sW", w_res)
        for dt_i in range(FDI):
            P.op("scalar", lambda s, i=dt_i: s.activation(
                out=aneg[:, i * 16:(i + 1) * 16], in_=arena(i * 16, 16), func=AF.Exp))
        for dt_i in range(FDI):
            P.op("scalar", lambda s, i=dt_i: s.mul(
                out=aneg[:, i * 16:(i + 1) * 16], in_=aneg[:, i * 16:(i + 1) * 16],
                mul=np.float32(-1.0)))

        for src, n, off in ((inln_w, DM, 0), (pmln_w, DM, 8), (conv_b, DI, 16),
                            (dtproj_b, DI, 32), (dskip, DI, 48)):
            k = n // 128
            w = dma("sync", vstage[0:k, :], src.rearrange("(k p) -> k p", p=128),
                    sem="sWv")
            P.wait("tensor", "sWv", w)
            t = P.op("tensor", lambda e, kk=k: e.transpose(
                pb[6][0:128, 0:kk], vstage[0:kk, 0:128], sident[0:kk, 0:kk]), inc="sT6")
            P.wait("scalar", "sT6", t)
            sc = P.op("scalar", lambda s, kk=k, o=off: s.copy(
                out=colv[:, o:o + kk], in_=pb[6][0:128, 0:kk]), inc="sS")
            P.wait("tensor", "sS", sc)
            P.wait("sync", "sS", sc)
        for src, n, col in ((dtln_w, R, 0), (bln_w, DS, 1), (cln_w, DS, 2)):
            w = dma("sync", vstage[0:1, 0:n], src.unsqueeze(0), sem="sWv")
            P.wait("tensor", "sWv", w)
            t = P.op("tensor", lambda e, nn=n: e.transpose(
                pb[6][0:nn, 0:1], vstage[0:1, 0:nn], sident[0:1, 0:1]), inc="sT6")
            P.wait("scalar", "sT6", t)
            sc = P.op("scalar", lambda s, nn=n, cc=col: s.copy(
                out=smallv[0:nn, cc:cc + 1], in_=pb[6][0:nn, 0:1]), inc="sS")
            P.wait("tensor", "sS", sc)
            P.wait("sync", "sS", sc)

        QW_X, QW_AV, QW_ACC, QW_AC2, QW_C7, QW_SG = 0, 1024, 2048, 3072, 4096, 4096
        W_STAGE = 6144
        CP_STAGE = 768
        P.wait("vector", "sS", P.cnt["sS"])
        jobs = []
        for (w_src, w_dst, nrows, kdim) in (
            (fg_w, wqtg, DQ, DM), (fu_w, wqtu, DQ, DM), (fd_w, wqtd, DM, DQ),
        ):
            for rt in range(nrows // 128):
                jobs.append((w_src, w_dst, rt, kdim))
        stage_mark = [None, None]
        st_mark = None
        for ji, (w_src, w_dst, rt, kdim) in enumerate(jobs):
            stg = ji % 2
            soff = W_STAGE + stg * 2816
            P.waitm("sync", stage_mark[stg])
            wld = dma("sync", arena(soff, kdim), w_src[rt * 128:(rt + 1) * 128, :],
                      sem=f"sQ{stg}")
            P.wait("vector", f"sQ{stg}", wld)
            slw = 1024 if kdim % 1024 == 0 else 704
            vq = None
            for s0 in range(0, kdim, slw):
                nb = slw // 16
                vq = emit_quant(
                    arena(soff + s0, slw), slw, nb,
                    (sp(QW_AV, slw), sp(QW_ACC, slw), sp(QW_AC2, slw),
                     sp(QW_X, slw), sp(QW_SG, slw)),
                    (arena(0, nb), arena(256, nb), arena(512, nb)))
            P.wait("tensor", "sV", vq)
            P.waitm("scalar", st_mark)
            last_t = None
            for fi in range(kdim // 128):
                bk = 6 + (fi % 2)
                te_bank(bk)
                t = P.op("tensor", lambda e, f=fi, bb=bk, so=soff: e.transpose(
                    pb[bb][0:128, 0:128], arena(so + f * 128, 128),
                    sident[:]), inc=f"sT{bk}")
                P.wait("scalar", f"sT{bk}", t)
                sc = P.op("scalar", lambda s, f=fi, bb=bk: s.copy(
                    out=arena(CP_STAGE + f * 128, 128), in_=pb[bb][0:128, 0:128]),
                    inc="sS")
                bank_free[bk] = ("sS", sc)
                last_t = t
            stage_mark[stg] = (f"sT{6 + (kdim // 128 - 1) % 2}", last_t)
            P.wait("gpsimd", "sS", P.cnt["sS"])
            gd = P.op("gpsimd", lambda g, r=rt, d=w_dst, kd=kdim: g.dma_start(
                out=d[0:kd, r * 128:(r + 1) * 128]
                .rearrange("(f p) j -> p f j", p=128),
                in_=arena(CP_STAGE, kd).rearrange("p (f j) -> p f j", j=128)),
                inc="sG", n=16)
            st_mark = ("sG", gd)
        P.wait("vector", "sG", P.cnt["sG"])
        P.wait("scalar", "sG", P.cnt["sG"])
        bar()
        P.wait("vector", "sG", P.cnt["sG"])

        def stream_mm(weight_h, mcol, k_tiles, rhs_fn, bank, ncols=LC):
            nseg = (k_tiles + 15) // 16
            tm = None
            for seg in range(nseg):
                k0 = seg * ((k_tiles + nseg - 1) // nseg)
                k1 = min(k_tiles, k0 + (k_tiles + nseg - 1) // nseg)
                half = whalf_ctr[0] % 3
                whalf_ctr[0] += 1
                P.waitm("sync", whalf_free[half]); whalf_free[half] = None
                wgx = dma("sync", wbuf[:, half * 2048:half * 2048 + (k1 - k0) * 128]
                          .rearrange("p (k j) -> p k j", j=128),
                          weight_h[k0 * 128:k1 * 128, mcol * 128:(mcol + 1) * 128]
                          .rearrange("(k p) j -> p k j", p=128),
                          sem=f"sW{half}")
                P.wait("tensor", f"sW{half}", wgx)
                for k in range(k0, k1):
                    tm = P.op("tensor", lambda e, kk=k, hh=half, bb=bank, kz=k0,
                              nc_=ncols: e.matmul(
                        pb[bb][0:128, 0:nc_],
                        wbuf[:, hh * 2048 + (kk - kz) * 128:
                             hh * 2048 + (kk - kz + 1) * 128],
                        rhs_fn(kk), start=(kk == 0), stop=(kk == k_tiles - 1)),
                        inc=(f"sT{bank}" if k == k1 - 1 else None))
                whalf_free[half] = (f"sT{bank}", tm)
            return (f"sT{bank}", tm)

        def load_pair(weight_h, m0, k_tiles=8):
            slot = whalf_ctr[0] % 3
            whalf_ctr[0] += 1
            P.waitm("sync", whalf_free[slot]); whalf_free[slot] = None
            off = slot * 2048
            wg = dma("sync", wbuf[:, off:off + k_tiles * 256]
                     .rearrange("p (k j) -> p k j", j=256),
                     weight_h[0:k_tiles * 128, m0 * 128:(m0 + 2) * 128]
                     .rearrange("(k p) j -> p k j", p=128),
                     sem=f"sW{slot}")
            return (slot, off, f"sW{slot}", wg)

        def pair_mm(ld, sub, bank, rhs_fn, k_tiles=8, ncols=LC):
            slot, off, sem_, wg = ld
            P.wait("tensor", sem_, wg)
            tm = None
            for k in range(k_tiles):
                tm = P.op("tensor", lambda e, kk=k, bb=bank, o=off, ss=sub,
                          nc_=ncols: e.matmul(
                    pb[bb][0:128, 0:nc_],
                    wbuf[:, o + kk * 256 + ss * 128: o + kk * 256 + ss * 128 + 128],
                    rhs_fn(kk), start=(kk == 0), stop=(kk == k_tiles - 1)),
                    inc=(f"sT{bank}" if k == k_tiles - 1 else None))
            if sub == 1:
                whalf_free[slot] = (f"sT{bank}", tm)
            return (f"sT{bank}", tm)

        A_XC, A_XN, A_XQ = 0, 2048, 4096
        A_XPAD, A_XSSM, A_Y = 0, 4160, 8256
        A_PROD, A_HNT = 0, 8192
        A_PRODQ, A_HQT, A_HQTT, A_PR = 0, 5632, 0, 8192
        A_GT = 5632

        def xpad(d):
            return arena(A_XPAD + d * 259, 259)

        def xssm(d):
            return arena(A_XSSM + d * 256, 256)

        def dtx(d):
            return arena(A_XPAD + d * 256, 256)

        def ya(d):
            return arena(A_Y + d * 256, 256)

        def szt(d):
            return szb[:, d * LC:(d + 1) * LC]

        def dtTt(d):
            return dtT[:, d * LC:(d + 1) * LC]

        RW_A, RW_B = 0, 256
        RW_C, RW_D = 512, 768
        RW_E, RW_F = 512, 768

        last_combine = None

        for ch in range(nchunks):
            tok0 = ch * LC

            P.waitm("gpsimd", last_combine)
            P.wait("scalar", "sO", P.cnt["sO"])
            wx = None
            for tt in range(NTT):
                wx = dma("gpsimd", x_t[:, tt * DM:(tt + 1) * DM],
                         x_h[tok0 + tt * 128: tok0 + (tt + 1) * 128, :], sem="sX")
            P.wait("vector", "sX", wx)
            stat = r2bcS
            for tt in range(NTT):
                P.op("vector", lambda v, t_=tt: v.reduce_sum(
                    out=stat[:, t_:t_ + 1], in_=x_t[:, t_ * DM:(t_ + 1) * DM], axis=AX.X))
            P.op("vector", lambda v: v.engine_nop())
            for tt in range(NTT):
                P.op("vector", lambda v, t_=tt: v.tensor_single_scalar(
                    out=stat[:, 2 + t_:3 + t_], in_=stat[:, t_:t_ + 1],
                    scalar=c("inv1024"), op=ALU.mult))
            P.op("vector", lambda v: v.engine_nop())
            vml = None
            for tt in range(NTT):
                vml = P.op("vector", lambda v, t_=tt: v.tensor_scalar(
                    out=arena(A_XC + t_ * DM, DM), in0=x_t[:, t_ * DM:(t_ + 1) * DM],
                    scalar1=stat[:, 2 + t_:3 + t_], scalar2=None, op0=ALU.subtract), inc="sV")
            P.wait("scalar", "sV", vml)
            ssl = None
            for tt in range(NTT):
                ssl = P.op("scalar", lambda s, t_=tt: s.activation(
                    out=arena(A_XN + t_ * DM, DM), in_=arena(A_XC + t_ * DM, DM),
                    func=AF.Square, accum_out=stat[:, 4 + t_:5 + t_]), inc="sS")
            ssl = P.op("scalar", lambda s: s.activation(
                out=stat[:, 6:6 + NTT], in_=stat[:, 4:4 + NTT], func=AF.Sqrt,
                scale=np.float32(1.0 / DM), bias=np.float32(1e-5)), inc="sS")
            P.wait("vector", "sS", ssl)
            vr = P.op("vector", lambda v: v.reciprocal(
                out=stat[:, 8:8 + NTT], in_=stat[:, 6:6 + NTT]), inc="sV")
            P.wait("scalar", "sV", vr)
            scn = None
            for tt in range(NTT):
                scn = P.op("scalar", lambda s, t_=tt: s.activation(
                    out=arena(A_XN + t_ * DM, DM), in_=arena(A_XC + t_ * DM, DM),
                    func=AF.Copy, scale=stat[:, 8 + t_:9 + t_]), inc="sS")
            if "xn" in dbg:
                dbg_pre()
                P.wait("gpsimd", "sS", scn)
                for tt in range(NTT):
                    dma("gpsimd", dbg_h["xn"][tok0 + tt * 128: tok0 + (tt + 1) * 128, :],
                        arena(A_XN + tt * DM, DM), sem="sO")
            P.wait("tensor", "sS", scn)
            for (src_off, kind) in ((A_XN, "xn"),):
                for tt in range(NTT):
                    for fi in range(FDM):
                        b = 6 + ((tt * FDM + fi) % 2)
                        te_bank(b)
                        t = P.op("tensor", lambda e, o=src_off, t_=tt, f=fi, bb=b: e.transpose(
                            pb[bb][0:128, 0:128], arena(o + t_ * DM + f * 128, 128),
                            sident[:]), inc=f"sT{b}")
                        P.wait("scalar", f"sT{b}", t)
                        if kind == "xn":
                            P.op("scalar", lambda s, f=fi, t_=tt, bb=b: s.copy(
                                out=xnT[:, f * LC + t_ * 128: f * LC + (t_ + 1) * 128],
                                in_=pb[bb][0:128, 0:128]))
                            sc = P.op("scalar", lambda s, f=fi, t_=tt, bb=b: s.activation(
                                out=xnW[:, f * LC + t_ * 128: f * LC + (t_ + 1) * 128],
                                in_=pb[bb][0:128, 0:128], func=AF.Copy,
                                scale=colv[:, f:f + 1]), inc="sS")
                        else:
                            sc = P.op("scalar", lambda s, f=fi, t_=tt, bb=b: s.copy(
                                out=xqT[:, f * LC + t_ * 128: f * LC + (t_ + 1) * 128],
                                in_=pb[bb][0:128, 0:128]), inc="sS")
                        bank_free[b] = ("sS", sc)
            P.wait("vector", "sS", scn)
            vdq = None
            for tt in range(NTT):
                P.op("vector", lambda v, t_=tt: v.tensor_copy(
                    out=arena(A_XQ + t_ * DM, DM), in_=arena(A_XN + t_ * DM, DM)))
                P.op("vector", lambda v: v.engine_nop())
                vdq = emit_quant(
                    arena(A_XQ + tt * DM, DM), DM, NBK_DM,
                    (sp(QW_AV, DM), sp(QW_ACC, DM), sp(QW_AC2, DM),
                     sp(QW_X, DM), sp(QW_SG, DM)),
                    (arena(12352 - 256, NBK_DM), arena(12352 - 512, NBK_DM),
                     arena(12352 - 768, NBK_DM)))
            if "xq" in dbg:
                dbg_pre()
                P.wait("gpsimd", "sV", vdq)
                for tt in range(NTT):
                    dma("gpsimd", dbg_h["xq"][tok0 + tt * 128: tok0 + (tt + 1) * 128, :],
                        arena(A_XQ + tt * DM, DM), sem="sO")
            P.wait("tensor", "sV", vdq)
            for (src_off, kind) in ((A_XQ, "xq"),):
                for tt in range(NTT):
                    for fi in range(FDM):
                        b = 6 + ((tt * FDM + fi) % 2)
                        te_bank(b)
                        t = P.op("tensor", lambda e, o=src_off, t_=tt, f=fi, bb=b: e.transpose(
                            pb[bb][0:128, 0:128], arena(o + t_ * DM + f * 128, 128),
                            sident[:]), inc=f"sT{b}")
                        P.wait("scalar", f"sT{b}", t)
                        if kind == "xn":
                            P.op("scalar", lambda s, f=fi, t_=tt, bb=b: s.copy(
                                out=xnT[:, f * LC + t_ * 128: f * LC + (t_ + 1) * 128],
                                in_=pb[bb][0:128, 0:128]))
                            sc = P.op("scalar", lambda s, f=fi, t_=tt, bb=b: s.activation(
                                out=xnW[:, f * LC + t_ * 128: f * LC + (t_ + 1) * 128],
                                in_=pb[bb][0:128, 0:128], func=AF.Copy,
                                scale=colv[:, f:f + 1]), inc="sS")
                        else:
                            sc = P.op("scalar", lambda s, f=fi, t_=tt, bb=b: s.copy(
                                out=xqT[:, f * LC + t_ * 128: f * LC + (t_ + 1) * 128],
                                in_=pb[bb][0:128, 0:128]), inc="sS")
                        bank_free[b] = ("sS", sc)
            bar()
            dbg_guard()

            te_bank(5)
            slot_mark = {}
            for fi in range(FDM):
                sl = fi % 2
                P.waitm("scalar", slot_mark.get(sl))
                ssq = P.op("scalar", lambda s, f=fi, o=sl: s.activation(
                    out=sp(o * 256, LC), in_=xnT[:, f * LC:(f + 1) * LC],
                    func=AF.Square), inc="sS")
                P.wait("tensor", "sS", ssq)
                tm = P.op("tensor", lambda e, o=sl, f=fi: e.matmul(
                    pb[5][0:1, 0:LC], onescol[:, 0:1], sp(o * 256, LC),
                    start=(f == 0), stop=(f == FDM - 1)), inc="sT5")
                slot_mark[sl] = ("sT5", tm)
            P.wait("scalar", "sT5", P.cnt["sT5"])
            srq = P.op("scalar", lambda s: s.activation(
                out=rows[0:1, RW_A:RW_A + LC], in_=pb[5][0:1, 0:LC], func=AF.Sqrt,
                scale=np.float32(1.0 / DM), bias=np.float32(1e-6)), inc="sS")
            P.wait("vector", "sS", srq)
            vr2 = P.op("vector", lambda v: v.reciprocal(
                out=rows[0:1, RW_B:RW_B + LC], in_=rows[0:1, RW_A:RW_A + LC]), inc="sV")
            P.wait("tensor", "sV", vr2)
            tbc = P.op("tensor", lambda e: e.matmul(
                pb[5][0:128, 0:LC], onesrow[0:1, 0:128], rows[0:1, RW_B:RW_B + LC],
                start=True, stop=True), inc="sT5")
            P.wait("scalar", "sT5", tbc)
            sbc2 = P.op("scalar", lambda s: s.copy(out=r2bcS[:], in_=pb[5][0:128, 0:LC]), inc="sS")
            P.wait("tensor", "sS", sbc2)
            P.wait("vector", "sS", sbc2)

            zmark = {}
            wg = None
            ip_half = {}
            for m in range(2 * FDI):
                if m % 2 == 0:
                    ip_half[m // 2] = whalf_ctr[0] % 3
                    whalf_ctr[0] += 1
                half = ip_half[m // 2]
                if m % 2 == 0:
                    P.waitm("sync", whalf_free[half]); whalf_free[half] = None
                    g = m // 2
                    wg = dma("sync", wbuf[:, half * 2048:half * 2048 + 2048]
                             .rearrange("p (k j) -> p k j", j=256),
                             w_in[:, g * 256:(g + 1) * 256]
                             .rearrange("(k p) j -> p k j", p=128),
                             sem=f"sW{half}")
                b = m % 4
                te_bank(b)
                P.wait("tensor", f"sW{half}", wg)
                tm = None
                for k in range(FDM):
                    tm = P.op("tensor", lambda e, mm=m, kk=k, hh=half, bb=b: e.matmul(
                        pb[bb][0:128, 0:LC],
                        wbuf[:, hh * 2048 + kk * 256 + (mm % 2) * 128:
                             hh * 2048 + kk * 256 + (mm % 2) * 128 + 128],
                        xnW[:, kk * LC:(kk + 1) * LC],
                        start=(kk == 0), stop=(kk == FDM - 1)),
                        inc=(f"sT{b}" if k == FDM - 1 else None))
                if m % 2 == 1:
                    whalf_free[half] = (f"sT{b}", tm)
                P.wait("vector", f"sT{b}", tm)
                if m < FDI:
                    vv = P.op("vector", lambda v, mm=m, bb=b: v.tensor_tensor(
                        out=xpad(mm)[:, 3:3 + LC], in0=pb[bb][0:128, 0:LC],
                        in1=r2bcS[:], op=ALU.mult), inc="sV")
                    bank_free[b] = ("sV", vv)
                else:
                    d = m - FDI
                    sl = 512 + (d % 2) * 256
                    P.waitm("vector", zmark.get(d % 2))
                    vv = P.op("vector", lambda v, bb=b, o=sl: v.tensor_tensor(
                        out=sp(o, LC), in0=pb[bb][0:128, 0:LC], in1=r2bcS[:],
                        op=ALU.mult), inc="sV")
                    bank_free[b] = ("sV", vv)
                    P.wait("scalar", "sV", vv)
                    scz = P.op("scalar", lambda s, dd=d, o=sl: s.activation(
                        out=szt(dd), in_=sp(o, LC), func=AF.Silu), inc="sS")
                    zmark[d % 2] = ("sS", scz)
            bar()

            for d in range(FDI):
                P.op("vector", lambda v, dd=d: v.tensor_copy(
                    out=xpad(dd)[:, 0:3], in_=tails[:, dd * 3:(dd + 1) * 3]))
            vtl = P.op("vector", lambda v: v.engine_nop(), inc="sV")
            P.wait("scalar", "sV", vtl)
            conv_marks = [None] * FDI
            for d in range(FDI):
                sl = (d % 2) * 256
                sm = P.op("scalar", lambda s, dd=d, o=sl: s.activation(
                    out=sp(o, LC), in_=xpad(dd)[:, 0:LC], func=AF.Copy,
                    scale=convw[:, dd * 4:dd * 4 + 1]), inc="sS")
                P.wait("vector", "sS", sm)
                for j in range(1, 4):
                    P.op("vector", lambda v, dd=d, jj=j, o=sl: v.scalar_tensor_tensor(
                        out=sp(o, LC), in0=xpad(dd)[:, jj:jj + LC],
                        scalar=convw[:, dd * 4 + jj:dd * 4 + jj + 1], in1=sp(o, LC),
                        op0=ALU.mult, op1=ALU.add))
                P.op("vector", lambda v, dd=d: v.tensor_copy(
                    out=tails[:, dd * 3:(dd + 1) * 3], in_=xpad(dd)[:, LC:LC + 3]))
                vcv = P.op("vector", lambda v: v.engine_nop(), inc="sV")
                P.wait("scalar", "sV", vcv)
                conv_marks[d] = P.op("scalar", lambda s, dd=d, o=sl: s.activation(
                    out=xssm(dd), in_=sp(o, LC), func=AF.Silu,
                    bias=colv[:, 16 + dd:17 + dd]), inc="sS")
            if dbg:
                bar()
            if "xssm" in dbg:
                dbg_pre()
                for d in range(FDI):
                    dma("gpsimd", dbg_h["xssm"][d * 128:(d + 1) * 128, tok0:tok0 + LC],
                        xssm(d), sem="sO")
            if "sz" in dbg:
                dbg_pre()
                for d in range(FDI):
                    dma("gpsimd", dbg_h["sz"][d * 128:(d + 1) * 128, tok0:tok0 + LC],
                        szt(d), sem="sO")
            dbg_guard()

            te_bank(4)
            te_bank(5)
            te_bank(6)
            tpj = None
            for k in range(FDI):
                P.wait("tensor", "sS", conv_marks[k])
                P.op("tensor", lambda e, kk=k: e.matmul(
                    pb[4][0:64, 0:LC], xpw[:, kk * 96:kk * 96 + 64], xssm(kk),
                    start=(kk == 0), stop=(kk == FDI - 1)))
                P.op("tensor", lambda e, kk=k: e.matmul(
                    pb[5][0:16, 0:LC], xpw[:, kk * 96 + 64:kk * 96 + 80], xssm(kk),
                    start=(kk == 0), stop=(kk == FDI - 1)))
                tpj = P.op("tensor", lambda e, kk=k: e.matmul(
                    pb[6][0:16, 0:LC], xpw[:, kk * 96 + 80:kk * 96 + 96], xssm(kk),
                    start=(kk == 0), stop=(kk == FDI - 1)),
                    inc=("sT4" if k == FDI - 1 else None))
            P.wait("vector", "sT4", tpj)
            P.op("vector", lambda v: v.tensor_copy(out=projS[:], in_=pb[4][0:64, 0:LC]))
            P.op("vector", lambda v: v.tensor_copy(out=projB[:], in_=pb[5][0:16, 0:LC]))
            vpj = P.op("vector", lambda v: v.tensor_copy(
                out=projC[:], in_=pb[6][0:16, 0:LC]), inc="sV")
            P.wait("scalar", "sV", vpj)
            s64 = P.op("scalar", lambda s: s.activation(
                out=sp(0, LC)[0:64, :], in_=projS[0:64, 0:LC], func=AF.Square), inc="sS")
            P.wait("tensor", "sS", s64)
            t64 = P.op("tensor", lambda e: e.matmul(
                pb[5][0:1, 0:LC], onescol[0:64, 0:1], sp(0, LC)[0:64, :],
                start=True, stop=True), inc="sT5")
            P.wait("scalar", "sT5", t64)
            s64b = P.op("scalar", lambda s: s.activation(
                out=rows[0:1, RW_A:RW_A + LC], in_=pb[5][0:1, 0:LC], func=AF.Sqrt,
                scale=np.float32(1.0 / R), bias=np.float32(1e-6)), inc="sS")
            P.wait("vector", "sS", s64b)
            v64 = P.op("vector", lambda v: v.reciprocal(
                out=rows[0:1, RW_B:RW_B + LC], in_=rows[0:1, RW_A:RW_A + LC]), inc="sV")
            P.wait("tensor", "sV", v64)
            t64b = P.op("tensor", lambda e: e.matmul(
                pb[5][0:64, 0:LC], onesrow[0:1, 0:64], rows[0:1, RW_B:RW_B + LC],
                start=True, stop=True), inc="sT5")
            P.wait("vector", "sT5", t64b)
            vdt = P.op("vector", lambda v: v.tensor_tensor(
                out=dtrms[:], in0=projS[0:64, 0:LC], in1=pb[5][0:64, 0:LC],
                op=ALU.mult), inc="sV")
            for (psrc, colid, dst, rw1, rw2) in ((projB, 1, BmT, RW_C, RW_D),
                                                 (projC, 2, CmT, RW_E, RW_F)):
                sB = P.op("scalar", lambda s, pp=psrc: s.activation(
                    out=sp(256, LC)[0:16, :], in_=pp[0:16, 0:LC], func=AF.Square), inc="sS")
                P.wait("tensor", "sS", sB)
                te_bank(6)
                tB = P.op("tensor", lambda e: e.matmul(
                    pb[6][0:1, 0:LC], onescol[0:16, 0:1], sp(256, LC)[0:16, :],
                    start=True, stop=True), inc="sT6")
                P.wait("scalar", "sT6", tB)
                sB2 = P.op("scalar", lambda s, w=rw1: s.activation(
                    out=rows[0:1, w:w + LC], in_=pb[6][0:1, 0:LC], func=AF.Sqrt,
                    scale=np.float32(1.0 / DS), bias=np.float32(1e-6)), inc="sS")
                P.wait("vector", "sS", sB2)
                vB = P.op("vector", lambda v, w1=rw1, w2=rw2: v.reciprocal(
                    out=rows[0:1, w2:w2 + LC], in_=rows[0:1, w1:w1 + LC]), inc="sV")
                P.wait("tensor", "sV", vB)
                tB2 = P.op("tensor", lambda e, w2=rw2: e.matmul(
                    pb[6][0:16, 0:LC], onesrow[0:1, 0:16], rows[0:1, w2:w2 + LC],
                    start=True, stop=True), inc="sT6")
                P.wait("vector", "sT6", tB2)
                vB2 = P.op("vector", lambda v, pp=psrc, cc=colid, dd=dst: v.scalar_tensor_tensor(
                    out=dd[:], in0=pp[0:16, 0:LC], scalar=smallv[0:16, cc:cc + 1],
                    in1=pb[6][0:16, 0:LC], op0=ALU.mult, op1=ALU.mult), inc="sV")
                bank_free[6] = ("sV", vB2)
            bar()
            if "bm" in dbg:
                dbg_pre()
                dma("gpsimd", dbg_h["bm"][0:16, tok0:tok0 + LC], BmT[:], sem="sO")
            if "cm" in dbg:
                dbg_pre()
                dma("gpsimd", dbg_h["cm"][0:16, tok0:tok0 + LC], CmT[:], sem="sO")
            dbg_guard()

            dslot = whalf_ctr[0] % 3
            whalf_ctr[0] += 1
            P.waitm("sync", whalf_free[dslot]); whalf_free[dslot] = None
            wdt = dma("sync", wbuf[0:64, dslot * 2048:(dslot + 1) * 2048]
                      .rearrange("p (k j) -> p k j", j=128),
                      dtproj_w[:].rearrange("p (k j) -> p k j", j=128),
                      sem=f"sW{dslot}")
            P.wait("scalar", f"sW{dslot}", wdt)
            sfold = P.op("scalar", lambda s, o=dslot * 2048: s.activation(
                out=wbuf[0:64, o:o + 2048], in_=wbuf[0:64, o:o + 2048],
                func=AF.Copy, scale=smallv[0:64, 0:1]), inc="sS")
            P.wait("tensor", "sS", sfold)
            vdt2_marks = {}
            for m in range(FDI):
                b = m % 4
                te_bank(b)
                tdm = P.op("tensor", lambda e, mm=m, bb=b, o=dslot * 2048: e.matmul(
                    pb[bb][0:128, 0:LC], wbuf[0:64, o + mm * 128:o + (mm + 1) * 128],
                    dtrms[0:64, 0:LC], start=True, stop=True), inc=f"sT{b}")
                P.wait("scalar", f"sT{b}", tdm)
                sl = (m % 2) * 1024
                P.waitm("scalar", vdt2_marks.get(m % 2))
                P.op("scalar", lambda s, bb=b, mm=m, o=sl: s.activation(
                    out=sp(o, LC), in_=pb[bb][0:128, 0:LC], func=AF.Relu,
                    bias=colv[:, 32 + mm:33 + mm]))
                sab = P.op("scalar", lambda s, bb=b, mm=m, o=sl: s.activation(
                    out=sp(o + 256, LC), in_=pb[bb][0:128, 0:LC], func=AF.Abs,
                    bias=colv[:, 32 + mm:33 + mm]), inc="sS")
                bank_free[b] = ("sS", sab)
                P.op("scalar", lambda s, o=sl: s.activation(
                    out=sp(o + 512, LC), in_=sp(o + 256, LC), func=AF.Exp,
                    scale=np.float32(-1.0)))
                sln = P.op("scalar", lambda s, o=sl: s.activation(
                    out=sp(o + 768, LC), in_=sp(o + 512, LC), func=AF.Ln,
                    bias=np.float32(1.0)), inc="sS")
                P.wait("vector", "sS", sln)
                vdt2 = P.op("vector", lambda v, mm=m, o=sl: v.tensor_tensor(
                    out=dtTt(mm), in0=sp(o, LC), in1=sp(o + 768, LC), op=ALU.add), inc="sV")
                vdt2_marks[m % 2] = ("sV", vdt2)
                if m == FDI - 1:
                    whalf_free[dslot] = (f"sT{b}", tdm)
            bar()
            if "dt" in dbg:
                dbg_pre()
                for d in range(FDI):
                    dma("gpsimd", dbg_h["dt"][d * 128:(d + 1) * 128, tok0:tok0 + LC],
                        dtTt(d), sem="sO")
            dbg_guard()
            for d in range(FDI):
                P.op("scalar", lambda s, dd=d: s.activation(
                    out=ya(dd), in_=xssm(dd), func=AF.Copy, scale=colv[:, 48 + dd:49 + dd]))
            for d in range(FDI):
                P.op("vector", lambda v, dd=d: v.tensor_tensor(
                    out=dtx(dd), in0=dtTt(dd), in1=xssm(dd), op=ALU.mult))
            bar()

            def pq0(i):
                return hiddenT[:, i * 128:(i + 1) * 128] if i < 16 else \
                    sT_[:, (i - 16) * 128:(i - 15) * 128]

            def pq1(i):
                return arena(4160 + i * 128, 128)

            def hq1t(i):
                return sT_[:, i * 128:(i + 1) * 128] if i < 16 else \
                    dtT[:, (i - 16) * 128:(i - 15) * 128]

            def hqt(i):
                return xnW[:, i * 128:(i + 1) * 128] if i < 16 else \
                    arena(6976 + (i - 16) * 128, 128)

            def hq_home(off, n_):
                if off < 2048:
                    return xnW[:, off:off + n_]
                return arena(6976 + (off - 2048), n_)

            HQ_SLICES = ((0, 1024), (1024, 1024), (2048, 768))

            F_DA, F_U, F_H, F_P, F_CBS = 0, 1028, 2056, 3084, 4108
            for buf in range(2):
                P.op("vector", lambda v, b_=buf: v.memset(
                    SP[:, F_DA + b_ * 514:F_DA + b_ * 514 + 514:257], 0.0))
            scan_marks = [None, None]
            ymark = [None, None]
            pmark = [None, None]
            cbs_mark = [None, None]
            hs_mark = [None, None]
            hs_pend = None
            P.wait("gpsimd", "sS", P.cnt["sS"])

            gsm_q = {}
            fp4_steps = []
            fp4_state = {}

            def fp4_make_steps(mp):
                def s_load():
                    fp4_state[(mp, "ldg")] = load_pair(wqtg, mp)
                    fp4_state[(mp, "ldu")] = load_pair(wqtu, mp)

                def s_gate(sub):
                    m = mp + sub
                    gs = r2bcS if m % 2 == 0 else r3bcS
                    b = m % 2
                    te_bank(b)
                    tm = pair_mm(fp4_state[(mp, "ldg")], sub, b,
                                 lambda k: xqT[:, k * LC:(k + 1) * LC])
                    P.waitm("scalar", tm)
                    P.waitm("scalar", gsm_q.get(m % 2))
                    sg_ = P.op("scalar", lambda s, bb=b, g_=gs: s.activation(
                        out=g_[:], in_=pb[bb][0:128, 0:LC], func=AF.Silu), inc="sS")
                    bank_free[b] = ("sS", sg_)
                    fp4_state[(m, "sg")] = sg_

                def s_up(sub):
                    m = mp + sub
                    b2 = 2 + (m % 2)
                    te_bank(b2)
                    fp4_state[(m, "tm2")] = pair_mm(
                        fp4_state[(mp, "ldu")], sub, b2,
                        lambda k: xqT[:, k * LC:(k + 1) * LC])

                def s_prod(sub):
                    m = mp + sub
                    gs = r2bcS if m % 2 == 0 else r3bcS
                    b2 = 2 + (m % 2)
                    P.waitm("vector", fp4_state[(m, "tm2")])
                    P.wait("vector", "sS", fp4_state[(m, "sg")])
                    P.op("vector", lambda v, bb=b2, g_=gs, mm=m: v.tensor_tensor(
                        out=pq0(mm), in0=g_[:, 0:128],
                        in1=pb[bb][0:128, 0:128], op=ALU.mult))
                    vp1 = P.op("vector", lambda v, bb=b2, g_=gs, mm=m: v.tensor_tensor(
                        out=pq1(mm), in0=g_[:, 128:256],
                        in1=pb[bb][0:128, 128:256], op=ALU.mult))
                    bank_free[b2] = ("sV", vp1)
                    gsm_q[m % 2] = ("sV", vp1)
                    fp4_state["vp_last"] = vp1

                fp4_steps.append(s_load)
                fp4_steps.append(lambda: s_gate(0))
                fp4_steps.append(lambda: (s_up(0), s_prod(0)))
                fp4_steps.append(lambda: s_gate(1))
                fp4_steps.append(lambda: (s_up(1), s_prod(1)))

            for mp_ in range(0, FQ, 2):
                fp4_make_steps(mp_)
            fp4_sched = {}
            tot_slots = 14 * 8
            for si in range(len(fp4_steps)):
                fp4_sched.setdefault(si * tot_slots // len(fp4_steps), []).append(si)

            tr0_last = None
            tr1_last = None
            for n in range(DS):
                bp = 4 + (n % 2)
                te_bank(bp)
                tbm = P.op("tensor", lambda e, nn=n, bb=bp: e.matmul(
                    pb[bb][0:128, 0:LC], rowsel[:, nn * 128:(nn + 1) * 128],
                    BmT[0:16, 0:LC], start=True, stop=True), inc=f"sT{bp}")
                te_bank(6)
                tc = P.op("tensor", lambda e, nn=n: e.matmul(
                    pb[6][0:128, 0:LC], rowsel[:, nn * 128:(nn + 1) * 128],
                    CmT[0:16, 0:LC], start=True, stop=True), inc="sT6")
                P.wait("vector", f"sT{bp}", tbm)
                P.wait("scalar", "sT6", tc)
                P.waitm("scalar", cbs_mark[n % 2])
                scb = P.op("scalar", lambda s, nn=n: s.copy(
                    out=sp(F_CBS + (nn % 2) * 256, LC), in_=pb[6][0:128, 0:LC]))
                bank_free[6] = ("sS", scb)
                ulast = None
                for j in range(FDI // 2):
                    pj = n * 8 + j
                    buf = pj % 2
                    dab = sp(F_DA + buf * 514, 514)
                    ub = sp(F_U + buf * 514, 514)
                    hb = sp(F_H + buf * 514, 514)
                    pbuf = sp(F_P + buf * 512, 512)
                    P.waitm("scalar", scan_marks[buf])
                    P.op("scalar", lambda s, a=ub, nn=n, jj=j: s.copy(
                        out=a[:, 0:514:257],
                        in_=hstate[:, nn * FDI + 2 * jj:nn * FDI + 2 * jj + 2]))
                    for hh in range(2):
                        d = 2 * j + hh
                        sda = P.op("scalar", lambda s, dd=d, nn=n, a=dab, o=1 + 257 * hh:
                                   s.activation(
                                       out=a[:, o:o + LC], in_=dtTt(dd), func=AF.Exp,
                                       scale=aneg[:, dd * 16 + nn:dd * 16 + nn + 1]), inc="sS")
                    if hs_pend is not None:
                        hs_pend()
                        hs_pend = None
                    ulast = P.op("vector", lambda v, a=ub, jj=j, bb=bp: v.tensor_tensor(
                        out=a.rearrange("p (two l) -> p two l", l=257)[:, :, 1:257],
                        in0=arena(A_XPAD + 2 * jj * 256, 512).rearrange(
                            "p (two l) -> p two l", l=256),
                        in1=pb[bb][0:128, 0:LC].unsqueeze(1).broadcast_to([128, 2, LC]),
                        op=ALU.mult))
                    P.wait("vector", "sS", sda)
                    P.waitm("vector", pmark[buf])
                    P.waitm("vector", hs_mark[buf])
                    sc_ = P.op("vector", lambda v, a=hb, d0=dab, d1=ub: v.tensor_tensor_scan(
                        out=a, data0=d0, data1=d1, initial=0.0,
                        op0=ALU.mult, op1=ALU.add))
                    scan_marks[buf] = ("sV", sc_)

                    def _mk_hsave(a=hb, nn=n, jj=j, sv=sc_, bf=buf):
                        def emit():
                            P.wait("scalar", "sV", sv)
                            hs_ = P.op("scalar", lambda s: s.copy(
                                out=hstate[:, nn * FDI + 2 * jj:
                                           nn * FDI + 2 * jj + 2],
                                in_=a[:, 256:514:257]))
                            hs_mark[bf] = ("sS", hs_)
                        return emit
                    hs_pend = _mk_hsave()
                    P.wait("gpsimd", "sV", sc_)
                    P.wait("gpsimd", "sS", scb)
                    P.waitm("gpsimd", ymark[buf])
                    pg = P.op("gpsimd", lambda g, a=pbuf, hh_=hb, nn=n: g.tensor_tensor(
                        out=a.rearrange("p (two l) -> p two l", l=256),
                        in0=hh_.rearrange("p (two l) -> p two l", l=257)[:, :, 1:257],
                        in1=sp(F_CBS + (nn % 2) * 256, LC).unsqueeze(1).broadcast_to(
                            [128, 2, LC]),
                        op=ALU.mult))
                    pmark[buf] = ("sP", pg)
                    yv = P.op("gpsimd", lambda g, a=pbuf, jj=j: g.tensor_tensor(
                        out=arena(A_Y + 2 * jj * 256, 512),
                        in0=arena(A_Y + 2 * jj * 256, 512), in1=a, op=ALU.add))
                    ymark[buf] = ("sP", yv)
                    for si in fp4_sched.get(n * 8 + j, ()):
                        fp4_steps[si]()
                bank_free[bp] = ("sV", ulast)
                cbs_mark[n % 2] = pmark[(n * 8 + 7) % 2]
                if n == 15:
                    if hs_pend is not None:
                        hs_pend()
                        hs_pend = None
                    for i in range(FQ):
                        bt = (7, 0, 1, 2, 3)[i % 5]
                        te_bank(bt)
                        t = P.op("tensor", lambda e, ii=i, bb=bt: e.transpose(
                            pb[bb][0:128, 0:128], pq1(ii), sident[:]), inc=f"sT{bt}")
                        P.wait("scalar", f"sT{bt}", t)
                        sc = P.op("scalar", lambda s, ii=i, bb=bt: s.copy(
                            out=hq1t(ii), in_=pb[bb][0:128, 0:128]), inc="sS")
                        bank_free[bt] = ("sS", sc)
                        tr1_last = ("sS", sc)
                if n == 14:
                    P.wait("tensor", "sV", fp4_state["vp_last"])
                    for i in range(FQ):
                        bt = (7, 0, 1, 2, 3)[i % 5]
                        te_bank(bt)
                        t = P.op("tensor", lambda e, ii=i, bb=bt: e.transpose(
                            pb[bb][0:128, 0:128], pq0(ii), sident[:]), inc=f"sT{bt}")
                        P.wait("scalar", f"sT{bt}", t)
                        sc = P.op("scalar", lambda s, ii=i, bb=bt: s.copy(
                            out=hqt(ii), in_=pb[bb][0:128, 0:128]), inc="sS")
                        bank_free[bt] = ("sS", sc)
                        tr0_last = ("sS", sc)

            if hs_pend is not None:
                hs_pend()
                hs_pend = None

            def hqT0(i):
                return arena(768 + i * 128, 128)
            if "y" in dbg:
                dbg_pre()
                for d in range(FDI):
                    dma("gpsimd", dbg_h["y"][d * 128:(d + 1) * 128, tok0:tok0 + LC],
                        ya(d), sem="sO")
                dbg_guard()
            P.wait("vector", "sP", P.cnt["sP"])
            vg = None
            for d in range(FDI):
                vg = P.op("vector", lambda v, dd=d: v.tensor_tensor(
                    out=ya(dd), in0=ya(dd), in1=szt(dd), op=ALU.mult), inc="sV")
            P.wait("tensor", "sV", vg)
            def hq1_home(off, w_):
                return sT_[:, off:off + w_] if off < 2048 else \
                    dtT[:, off - 2048:off - 2048 + w_]
            gq1 = None
            P.wait("vector", "sS", tr1_last[1])
            for (off, w_) in HQ_SLICES:
                nb = w_ // 16
                gq1 = ("sV", emit_quant(
                    hq1_home(off, w_), w_, nb,
                    (sp(QW_AV, w_), sp(QW_ACC, w_), sp(QW_AC2, w_),
                     sp(QW_X, w_), sp(QW_SG, w_)),
                    (arena(0, nb), arena(256, nb), arena(512, nb))))
            for mo in range(FDM):
                half = whalf_ctr[0] % 3
                whalf_ctr[0] += 1
                P.waitm("sync", whalf_free[half]); whalf_free[half] = None
                wg = dma("sync", wbuf[:, half * 2048:half * 2048 + 2048]
                         .rearrange("p (k j) -> p k j", j=128),
                         wout[:, mo * 128:(mo + 1) * 128]
                         .rearrange("(k p) j -> p k j", p=128),
                         sem=f"sW{half}")
                b = mo % 7
                te_bank(b)
                P.wait("tensor", f"sW{half}", wg)
                tm = None
                for k in range(FDI):
                    tm = P.op("tensor", lambda e, kk=k, hh=half, bb=b: e.matmul(
                        pb[bb][0:128, 0:LC],
                        wbuf[:, hh * 2048 + kk * 128: hh * 2048 + (kk + 1) * 128],
                        ya(kk), start=(kk == 0), stop=(kk == FDI - 1)),
                        inc=(f"sT{b}" if k == FDI - 1 else None))
                whalf_free[half] = (f"sT{b}", tm)
                P.wait("vector", f"sT{b}", tm)
                vh = P.op("vector", lambda v, m=mo, bb=b: v.tensor_tensor(
                    out=hiddenT[:, m * LC:(m + 1) * LC], in0=pb[bb][0:128, 0:LC],
                    in1=xnT[:, m * LC:(m + 1) * LC], op=ALU.add), inc="sV")
                bank_free[b] = ("sV", vh)
            vq0 = None
            for (off, w_) in HQ_SLICES:
                nb = w_ // 16
                vq0 = emit_quant(
                    hq_home(off, w_), w_, nb,
                    (sp(QW_AV, w_), sp(QW_ACC, w_), sp(QW_AC2, w_),
                     sp(QW_X, w_), sp(QW_SG, w_)),
                    (arena(0, nb), arena(256, nb), arena(512, nb)))
            P.waitm("tensor", gq1)
            for i in range(FQ):
                bt = (7, 0, 1)[i % 3]
                te_bank(bt)
                t = P.op("tensor", lambda e, ii=i, bb=bt: e.transpose(
                    pb[bb][0:128, 0:128], hq1t(ii), sident[:]), inc=f"sT{bt}")
                P.wait("scalar", f"sT{bt}", t)
                sc = P.op("scalar", lambda s, ii=i, bb=bt: s.copy(
                    out=pq1(ii), in_=pb[bb][0:128, 0:128]), inc="sS")
                bank_free[bt] = ("sS", sc)
            P.wait("tensor", "sV", vq0)
            for i in range(FQ):
                bt = (7, 0, 1)[i % 3]
                te_bank(bt)
                t = P.op("tensor", lambda e, ii=i, bb=bt: e.transpose(
                    pb[bb][0:128, 0:128], hqt(ii), sident[:]), inc=f"sT{bt}")
                P.wait("scalar", f"sT{bt}", t)
                sc = P.op("scalar", lambda s, ii=i, bb=bt: s.copy(
                    out=hqT0(ii), in_=pb[bb][0:128, 0:128]), inc="sS")
                bank_free[bt] = ("sS", sc)
            P.wait("tensor", "sS", P.cnt["sS"])
            for mo in range(FDM):
                b = 4 + (mo % 3)
                te_bank(b)
                tmv = None
                for seg in range(2):
                    k0, k1 = seg * 11, seg * 11 + 11
                    slot = whalf_ctr[0] % 3
                    whalf_ctr[0] += 1
                    P.waitm("sync", whalf_free[slot]); whalf_free[slot] = None
                    so = slot * 2048
                    wgx = dma("sync", wbuf[:, so:so + 11 * 128]
                              .rearrange("p (k j) -> p k j", j=128),
                              wqtd[k0 * 128:k1 * 128, mo * 128:(mo + 1) * 128]
                              .rearrange("(k p) j -> p k j", p=128),
                              sem=f"sW{slot}")
                    P.wait("tensor", f"sW{slot}", wgx)
                    for k in range(k0, k1):
                        P.op("tensor", lambda e, kk=k, o=so, kz=k0, bb=b: e.matmul(
                            pb[bb][0:128, 0:128],
                            wbuf[:, o + (kk - kz) * 128: o + (kk - kz + 1) * 128],
                            hqT0(kk), start=(kk == 0), stop=False))
                        tmv = P.op("tensor", lambda e, kk=k, o=so, kz=k0, bb=b: e.matmul(
                            pb[bb][0:128, 128:256],
                            wbuf[:, o + (kk - kz) * 128: o + (kk - kz + 1) * 128],
                            pq1(kk), start=False, stop=(kk == FQ - 1)),
                            inc=(f"sT{b}" if k == k1 - 1 else None))
                    whalf_free[slot] = (f"sT{b}", tmv)
                P.waitm("scalar", (f"sT{b}", tmv))
                sg_ = P.op("scalar", lambda s, m=mo, bb=b: s.activation(
                    out=dtT[:, 1024 + m * 256:1024 + (m + 1) * 256],
                    in_=pb[bb][0:128, 0:256], func=AF.Silu), inc="sS")
                bank_free[b] = ("sS", sg_)
            bar()
            if "g" in dbg:
                dbg_pre()
                for mo in range(FDM):
                    for hh in range(2):
                        dma("gpsimd",
                            dbg_h["g"][mo * 128:(mo + 1) * 128,
                                       tok0 + hh * 128:tok0 + hh * 128 + 128],
                            dtT[:, 1024 + mo * 256 + hh * 128:
                                1024 + mo * 256 + hh * 128 + 128],
                            sem="sO")
            dbg_guard()
            if "hid" in dbg:
                dbg_pre()
                for mo in range(FDM):
                    dma("gpsimd", dbg_h["hid"][mo * 128:(mo + 1) * 128, tok0:tok0 + LC],
                        hiddenT[:, mo * LC:(mo + 1) * LC], sem="sO")
            dbg_guard()

            te_bank(5)
            slot_mark = {}
            for fi in range(FDM):
                sl = fi % 2
                P.waitm("scalar", slot_mark.get(sl))
                ssq = P.op("scalar", lambda s, f=fi, o=sl: s.activation(
                    out=sp(o * 256, LC), in_=hiddenT[:, f * LC:(f + 1) * LC],
                    func=AF.Square), inc="sS")
                P.wait("tensor", "sS", ssq)
                tm = P.op("tensor", lambda e, o=sl, f=fi: e.matmul(
                    pb[5][0:1, 0:LC], onescol[:, 0:1], sp(o * 256, LC),
                    start=(f == 0), stop=(f == FDM - 1)), inc="sT5")
                slot_mark[sl] = ("sT5", tm)
            P.wait("scalar", "sT5", P.cnt["sT5"])
            sm3 = P.op("scalar", lambda s: s.activation(
                out=rows[0:1, RW_A:RW_A + LC], in_=pb[5][0:1, 0:LC], func=AF.Sqrt,
                scale=np.float32(1.0 / DM), bias=np.float32(1e-6)), inc="sS")
            P.wait("vector", "sS", sm3)
            vr3 = P.op("vector", lambda v: v.reciprocal(
                out=rows[0:1, RW_B:RW_B + LC], in_=rows[0:1, RW_A:RW_A + LC]), inc="sV")
            P.wait("tensor", "sV", vr3)
            tbc3 = P.op("tensor", lambda e: e.matmul(
                pb[5][0:128, 0:LC], onesrow[0:1, 0:128], rows[0:1, RW_B:RW_B + LC],
                start=True, stop=True), inc="sT5")
            P.wait("scalar", "sT5", tbc3)
            sbc3 = P.op("scalar", lambda s: s.copy(out=r3bcS[:], in_=pb[5][0:128, 0:LC]), inc="sS")
            P.wait("vector", "sS", sbc3)
            vhn = None
            for m in range(FDM):
                vhn = P.op("vector", lambda v, mm=m: v.scalar_tensor_tensor(
                    out=arena(A_HNT + mm * LC, LC), in0=hiddenT[:, mm * LC:(mm + 1) * LC],
                    scalar=colv[:, 8 + mm:9 + mm], in1=r3bcS[:],
                    op0=ALU.mult, op1=ALU.mult), inc="sV")
            P.wait("tensor", "sV", vhn)


            gsm = {}
            for mp in range(0, FFF, 2):
                ldg = load_pair(gate_w, mp)
                ldu = load_pair(up_w, mp)
                for sub in range(2):
                    m = mp + sub
                    b = m % 2
                    te_bank(b)
                    tm = pair_mm(ldg, sub, b, lambda k: arena(A_HNT + k * LC, LC))
                    P.waitm("scalar", tm)
                    sl = 512 + (m % 2) * 256
                    P.waitm("scalar", gsm.get(m % 2))
                    sg_ = P.op("scalar", lambda s, bb=b, o=sl: s.activation(
                        out=sp(o, LC), in_=pb[bb][0:128, 0:LC], func=AF.Silu), inc="sS")
                    bank_free[b] = ("sS", sg_)
                    b2 = 2 + (m % 2)
                    te_bank(b2)
                    tm2 = pair_mm(ldu, sub, b2, lambda k: arena(A_HNT + k * LC, LC))
                    P.waitm("vector", tm2)
                    P.wait("vector", "sS", sg_)
                    vp = P.op("vector", lambda v, bb=b2, o=sl, mm=m: v.tensor_tensor(
                        out=arena(A_PROD + mm * LC, LC), in0=sp(o, LC),
                        in1=pb[bb][0:128, 0:LC], op=ALU.mult), inc="sV")
                    bank_free[b2] = ("sV", vp)
                    gsm[m % 2] = ("sV", vp)
            P.wait("tensor", "sV", P.cnt["sV"])
            for mo in range(FDM):
                b = mo % 4
                te_bank(b)
                tm = stream_mm(down_w, mo, FFF, lambda k: arena(A_PROD + k * LC, LC), b)
                P.waitm("vector", tm)
                vd_ = P.op("vector", lambda v, m=mo, bb=b: v.tensor_tensor(
                    out=sT_[:, m * LC:(m + 1) * LC], in0=pb[bb][0:128, 0:LC],
                    in1=hiddenT[:, m * LC:(m + 1) * LC], op=ALU.add), inc="sV")
                bank_free[b] = ("sV", vd_)
            bar()
            if "s" in dbg:
                dbg_pre()
                for mo in range(FDM):
                    dma("gpsimd", dbg_h["s"][mo * 128:(mo + 1) * 128, tok0:tok0 + LC],
                        sT_[:, mo * LC:(mo + 1) * LC], sem="sO")
            dbg_guard()

            vpr = None
            for mo in range(FDM):
                vpr = P.op("vector", lambda v, m=mo: v.tensor_tensor(
                    out=arena(A_PR + m * LC, LC), in0=sT_[:, m * LC:(m + 1) * LC],
                    in1=dtT[:, 1024 + m * 256:1024 + (m + 1) * 256],
                    op=ALU.mult), inc="sV")
            P.wait("tensor", "sV", vpr)
            vo = None
            for mo in range(FDM):
                for tt in range(NTT):
                    b = 6 + ((mo * NTT + tt) % 2)
                    te_bank(b)
                    t = P.op("tensor", lambda e, m=mo, t_=tt, bb=b: e.transpose(
                        pb[bb][0:128, 0:128], arena(A_PR + m * LC + t_ * 128, 128),
                        sident[:]), inc=f"sT{b}")
                    P.wait("vector", f"sT{b}", t)
                    vo = P.op("vector", lambda v, m=mo, t_=tt, bb=b: v.tensor_tensor(
                        out=xnW[:, t_ * DM + m * 128: t_ * DM + (m + 1) * 128],
                        in0=pb[bb][0:128, 0:128],
                        in1=x_t[:, t_ * DM + m * 128: t_ * DM + (m + 1) * 128],
                        op=ALU.add), inc="sV")
                    bank_free[b] = ("sV", vo)
            P.wait("gpsimd", "sV", vo)
            for tt in range(NTT):
                dma("gpsimd", out_h[tok0 + tt * 128: tok0 + (tt + 1) * 128, :],
                    xnW[:, tt * DM:(tt + 1) * DM], sem="sO")
            last_combine = ("sV", vo)
            bar()

        P.wait("gpsimd", "sO", P.cnt["sO"])
        P.wait("gpsimd", "sG", P.cnt["sG"])
        P.op("gpsimd", lambda g: g.nop())

        def runner(eng_name):
            self_sem = SELF_SEM.get(eng_name)

            def run(eng):
                idx = 0
                for e in P.ev[eng_name]:
                    if e[0] == "auto":
                        if idx > 0:
                            eng.wait_ge(sems[self_sem], idx)
                        r = e[1](eng)
                        if r is None:
                            r = eng.nop()
                        r.then_inc(sems[self_sem], 1)
                        idx += 1
                    elif e[0] == "op":
                        e[1](eng)
                    elif e[0] == "inc":
                        e[1](eng).then_inc(sems[e[2]], e[3])
                    else:
                        eng.wait_ge(sems[e[1]], e[2])
            return run

        block.tensor(runner("tensor"))
        block.vector(runner("vector"))
        block.scalar(runner("scalar"))
        block.sync(runner("sync"))
        block.gpsimd(runner("gpsimd"))

    return nc


def make_inputs(inp, b):
    m = {"x": np.ascontiguousarray(np.asarray(inp["L0_attn_hidden_states"])[b])}
    names = {
        "in_proj_w": "L0_attn_in_proj_w", "conv_w": "L0_attn_conv_w",
        "conv_b": "L0_attn_conv_b", "x_proj_w": "L0_attn_x_proj_w",
        "dt_proj_w": "L0_attn_dt_proj_w", "dt_proj_b": "L0_attn_dt_proj_b",
        "dt_ln_w": "L0_attn_dt_ln_w", "b_ln_w": "L0_attn_b_ln_w",
        "c_ln_w": "L0_attn_c_ln_w", "A_log": "L0_attn_A_log",
        "D_skip": "L0_attn_D_skip", "out_proj_w": "L0_attn_out_proj_w",
        "input_ln_w": "L0_attn_input_ln_w", "pre_moe_ln_w": "L0_attn_pre_moe_ln_w",
        "gate_w": "L0_attn_gate_w", "up_w": "L0_attn_up_w",
        "down_w": "L0_attn_down_w", "ffn_w_gate": "L0_ffn_w_gate",
        "ffn_w_up": "L0_ffn_w_up", "ffn_w_down": "L0_ffn_w_down",
    }
    for k, src in names.items():
        m[k] = np.ascontiguousarray(np.asarray(inp[src]))
    return m

def make_inputs(inp, b):
    m = {"x": np.ascontiguousarray(np.asarray(inp["L0_attn_hidden_states"])[b])}
    names = {
        "in_proj_w": "L0_attn_in_proj_w", "conv_w": "L0_attn_conv_w",
        "conv_b": "L0_attn_conv_b", "x_proj_w": "L0_attn_x_proj_w",
        "dt_proj_w": "L0_attn_dt_proj_w", "dt_proj_b": "L0_attn_dt_proj_b",
        "dt_ln_w": "L0_attn_dt_ln_w", "b_ln_w": "L0_attn_b_ln_w",
        "c_ln_w": "L0_attn_c_ln_w", "A_log": "L0_attn_A_log",
        "D_skip": "L0_attn_D_skip", "out_proj_w": "L0_attn_out_proj_w",
        "input_ln_w": "L0_attn_input_ln_w", "pre_moe_ln_w": "L0_attn_pre_moe_ln_w",
        "gate_w": "L0_attn_gate_w", "up_w": "L0_attn_up_w",
        "down_w": "L0_attn_down_w", "ffn_w_gate": "L0_ffn_w_gate",
        "ffn_w_up": "L0_ffn_w_up", "ffn_w_down": "L0_ffn_w_down",
    }
    for k, src in names.items():
        m[k] = np.ascontiguousarray(np.asarray(inp[src]))
    return m

def make_inputs(inp, b):
    m = {"x": np.ascontiguousarray(np.asarray(inp["L0_attn_hidden_states"])[b])}
    names = {
        "in_proj_w": "L0_attn_in_proj_w", "conv_w": "L0_attn_conv_w",
        "conv_b": "L0_attn_conv_b", "x_proj_w": "L0_attn_x_proj_w",
        "dt_proj_w": "L0_attn_dt_proj_w", "dt_proj_b": "L0_attn_dt_proj_b",
        "dt_ln_w": "L0_attn_dt_ln_w", "b_ln_w": "L0_attn_b_ln_w",
        "c_ln_w": "L0_attn_c_ln_w", "A_log": "L0_attn_A_log",
        "D_skip": "L0_attn_D_skip", "out_proj_w": "L0_attn_out_proj_w",
        "input_ln_w": "L0_attn_input_ln_w", "pre_moe_ln_w": "L0_attn_pre_moe_ln_w",
        "gate_w": "L0_attn_gate_w", "up_w": "L0_attn_up_w",
        "down_w": "L0_attn_down_w", "ffn_w_gate": "L0_ffn_w_gate",
        "ffn_w_up": "L0_ffn_w_up", "ffn_w_down": "L0_ffn_w_down",
    }
    for k, src in names.items():
        m[k] = np.ascontiguousarray(np.asarray(inp[src]))
    return m

def make_inputs(inp, b):
    m = {"x": np.ascontiguousarray(np.asarray(inp["L0_attn_hidden_states"])[b])}
    names = {
        "in_proj_w": "L0_attn_in_proj_w", "conv_w": "L0_attn_conv_w",
        "conv_b": "L0_attn_conv_b", "x_proj_w": "L0_attn_x_proj_w",
        "dt_proj_w": "L0_attn_dt_proj_w", "dt_proj_b": "L0_attn_dt_proj_b",
        "dt_ln_w": "L0_attn_dt_ln_w", "b_ln_w": "L0_attn_b_ln_w",
        "c_ln_w": "L0_attn_c_ln_w", "A_log": "L0_attn_A_log",
        "D_skip": "L0_attn_D_skip", "out_proj_w": "L0_attn_out_proj_w",
        "input_ln_w": "L0_attn_input_ln_w", "pre_moe_ln_w": "L0_attn_pre_moe_ln_w",
        "gate_w": "L0_attn_gate_w", "up_w": "L0_attn_up_w",
        "down_w": "L0_attn_down_w", "ffn_w_gate": "L0_ffn_w_gate",
        "ffn_w_up": "L0_ffn_w_up", "ffn_w_down": "L0_ffn_w_down",
    }
    for k, src in names.items():
        m[k] = np.ascontiguousarray(np.asarray(inp[src]))
    return m

def make_inputs(inp, b):
    m = {"x": np.ascontiguousarray(np.asarray(inp["L0_attn_hidden_states"])[b])}
    names = {
        "in_proj_w": "L0_attn_in_proj_w", "conv_w": "L0_attn_conv_w",
        "conv_b": "L0_attn_conv_b", "x_proj_w": "L0_attn_x_proj_w",
        "dt_proj_w": "L0_attn_dt_proj_w", "dt_proj_b": "L0_attn_dt_proj_b",
        "dt_ln_w": "L0_attn_dt_ln_w", "b_ln_w": "L0_attn_b_ln_w",
        "c_ln_w": "L0_attn_c_ln_w", "A_log": "L0_attn_A_log",
        "D_skip": "L0_attn_D_skip", "out_proj_w": "L0_attn_out_proj_w",
        "input_ln_w": "L0_attn_input_ln_w", "pre_moe_ln_w": "L0_attn_pre_moe_ln_w",
        "gate_w": "L0_attn_gate_w", "up_w": "L0_attn_up_w",
        "down_w": "L0_attn_down_w", "ffn_w_gate": "L0_ffn_w_gate",
        "ffn_w_up": "L0_ffn_w_up", "ffn_w_down": "L0_ffn_w_down",
    }
    for k, src in names.items():
        m[k] = np.ascontiguousarray(np.asarray(inp[src]))
    return m

def make_inputs(inp, b):
    m = {"x": np.ascontiguousarray(np.asarray(inp["L0_attn_hidden_states"])[b])}
    names = {
        "in_proj_w": "L0_attn_in_proj_w", "conv_w": "L0_attn_conv_w",
        "conv_b": "L0_attn_conv_b", "x_proj_w": "L0_attn_x_proj_w",
        "dt_proj_w": "L0_attn_dt_proj_w", "dt_proj_b": "L0_attn_dt_proj_b",
        "dt_ln_w": "L0_attn_dt_ln_w", "b_ln_w": "L0_attn_b_ln_w",
        "c_ln_w": "L0_attn_c_ln_w", "A_log": "L0_attn_A_log",
        "D_skip": "L0_attn_D_skip", "out_proj_w": "L0_attn_out_proj_w",
        "input_ln_w": "L0_attn_input_ln_w", "pre_moe_ln_w": "L0_attn_pre_moe_ln_w",
        "gate_w": "L0_attn_gate_w", "up_w": "L0_attn_up_w",
        "down_w": "L0_attn_down_w", "ffn_w_gate": "L0_ffn_w_gate",
        "ffn_w_up": "L0_ffn_w_up", "ffn_w_down": "L0_ffn_w_down",
    }
    for k, src in names.items():
        m[k] = np.ascontiguousarray(np.asarray(inp[src]))
    return m

def make_inputs(inp, b):
    m = {"x": np.ascontiguousarray(np.asarray(inp["L0_attn_hidden_states"])[b])}
    names = {
        "in_proj_w": "L0_attn_in_proj_w", "conv_w": "L0_attn_conv_w",
        "conv_b": "L0_attn_conv_b", "x_proj_w": "L0_attn_x_proj_w",
        "dt_proj_w": "L0_attn_dt_proj_w", "dt_proj_b": "L0_attn_dt_proj_b",
        "dt_ln_w": "L0_attn_dt_ln_w", "b_ln_w": "L0_attn_b_ln_w",
        "c_ln_w": "L0_attn_c_ln_w", "A_log": "L0_attn_A_log",
        "D_skip": "L0_attn_D_skip", "out_proj_w": "L0_attn_out_proj_w",
        "input_ln_w": "L0_attn_input_ln_w", "pre_moe_ln_w": "L0_attn_pre_moe_ln_w",
        "gate_w": "L0_attn_gate_w", "up_w": "L0_attn_up_w",
        "down_w": "L0_attn_down_w", "ffn_w_gate": "L0_ffn_w_gate",
        "ffn_w_up": "L0_ffn_w_up", "ffn_w_down": "L0_ffn_w_down",
    }
    for k, src in names.items():
        m[k] = np.ascontiguousarray(np.asarray(inp[src]))
    return m


_NC_CACHE = {}


def kernel(**inputs):
    from concourse.bass_utils import run_bass_kernel_spmd
    if "nc" not in _NC_CACHE:
        _NC_CACHE["nc"] = build_program(nchunks=8, dbg=())
    nc = _NC_CACHE["nc"]
    in_maps = [make_inputs(inputs, b) for b in range(8)]
    res = run_bass_kernel_spmd(nc, in_maps, core_ids=list(range(8)))
    return np.stack([np.asarray(r["out"]) for r in res.results], axis=0)
